# version 02
# baseline (speedup 1.0000x reference)
import contextlib
import math

import numpy as np
import ml_dtypes
import concourse.bass as bass
import concourse.mybir as mybir
from concourse.bass_utils import run_bass_kernel_spmd
from concourse.aot_env import allow_ici_all_to_all

F32 = mybir.dt.float32
BF16 = mybir.dt.bfloat16
AF = mybir.ActivationFunctionType
ALU = mybir.AluOpType
AX = mybir.AxisListType

NCORE = 8
S = 8192
DM = 2048
TPC = 1024
NT = 8
DI = 8192
EM = 8192
DIN = 4096
DST = 16
DTR = 128
HEADS = 16
GRP = 4
HD = 128
RND = 12582912.0
EPS = 1e-5
LAM11 = 1.0 / math.sqrt(2048.0)
LAM12 = 1.0 / math.sqrt(8192.0)

PHASES = ("A", "B", "C", "D")
FUSED = False
DEBUG_C = False



class Ctx:
    def __init__(self, nc, block, sem):
        self.nc = nc
        self.block = block
        self.sem = sem
        self.cnt = {k: 0 for k in sem}
        self.tag = {}

    def sec(self, eng):
        return getattr(self.block, eng)

    def inc(self, instr, key, n=1, tag=None):
        instr.then_inc(self.sem[key], n)
        self.cnt[key] += n
        if tag is not None:
            self.tag[tag] = (key, self.cnt[key])
        return self.cnt[key]

    def bump(self, key, n, tag=None):
        self.cnt[key] += n
        if tag is not None:
            self.tag[tag] = (key, self.cnt[key])
        return self.cnt[key]

    def wait(self, eng, tag):
        if tag in self.tag:
            key, v = self.tag[tag]
            eng.wait_ge(self.sem[key], v)

    def wait_cnt(self, eng, key):
        eng.wait_ge(self.sem[key], self.cnt[key])



def _ln_sections(ctx, g, src_ap, nm, rs, t0, sqscr, n_feat, tag, rms_only=False):
    S_ = ctx.sec

    @S_("vector")
    def _(vector):
        if not rms_only:
            ctx.inc(vector.tensor_reduce(nm[:], src_ap, AX.X, ALU.add),
                    "VE", tag=tag + "_sum")

    @S_("scalar")
    def _(scalar):
        ctx.inc(scalar.activation(sqscr[:, 0:n_feat], src_ap, AF.Square,
                                  accum_out=t0[:]), "AC", tag=tag + "_sq")

    @S_("vector")
    def _(vector):
        ctx.wait(vector, tag + "_sq")
        if rms_only:
            ctx.inc(vector.tensor_scalar(t0[:], t0[:], 1.0 / n_feat, None, ALU.mult),
                    "VE", tag=tag + "_var")
        else:
            ctx.wait(vector, tag + "_sum")
            ctx.inc(vector.tensor_scalar(nm[:], nm[:], -1.0 / n_feat, None, ALU.mult), "VE")
            ctx.wait_cnt(vector, "VE")
            ctx.inc(vector.tensor_tensor(rs[:], nm[:], nm[:], ALU.mult), "VE")
            ctx.wait_cnt(vector, "VE")
            ctx.inc(vector.scalar_tensor_tensor(t0[:], t0[:], 1.0 / n_feat, rs[:],
                                                ALU.mult, ALU.subtract),
                    "VE", tag=tag + "_var")

    @S_("scalar")
    def _(scalar):
        ctx.wait(scalar, tag + "_var")
        ctx.inc(scalar.activation(t0[:], t0[:], AF.Ln, bias=g["cEPS"]), "AC")
        ctx.wait_cnt(scalar, "AC")
        ctx.inc(scalar.activation(rs[:], t0[:], AF.Exp, scale=-0.5),
                "AC", tag=tag + "_rs")



def emit_phase_B(ctx, g, xin_ap, xout_ap):
    S_ = ctx.sec
    xsp = g["xsp"]
    WguT, Wd0T = g["WguT"], g["Wd0T"]
    ident = g["ident_sb"]
    src = xin_ap if xin_ap is not None else xsp

    P = g["phase_stack"].enter_context
    nc = ctx.nc
    xt = P(nc.sbuf_tensor("b_xt", [128, DM], F32))
    xnbf = P(nc.sbuf_tensor("b_xnbf", [128, DM], BF16))
    sqscr = P(nc.sbuf_tensor("b_sqscr", [128, DM], F32))
    nm = P(nc.sbuf_tensor("b_nm", [128, 1], F32))
    rs = P(nc.sbuf_tensor("b_rs", [128, 1], F32))
    t0 = P(nc.sbuf_tensor("b_t0", [128, 1], F32))
    xn2T = P(nc.sbuf_tensor("b_xn2T", [128, 16 * TPC], BF16))
    wg = [P(nc.sbuf_tensor(f"b_wg{j}", [128, 16 * 128], BF16)) for j in range(2)]
    wd = [P(nc.sbuf_tensor(f"b_wd{j}", [128, 64 * 128], BF16)) for j in range(2)]
    h_sb = P(nc.sbuf_tensor("b_h", [128, 64 * 512], BF16))
    silu_sb = P(nc.sbuf_tensor("b_silu", [128, 512], BF16))
    dn_fm = P(nc.sbuf_tensor("b_dnfm", [128, 16 * 512], BF16))
    dn_tm = P(nc.sbuf_tensor("b_dntm", [128, DM], BF16))
    ps_t = P(nc.psum_tensor("b_pst", [128, 2048], BF16))
    ps_gu = P(nc.psum_tensor("b_psgu", [128, 1024], F32))
    ps_d = P(nc.psum_tensor("b_psd", [128, 512], F32))

    for i in range(NT):
        @S_("sync")
        def _(sync, i=i):
            ctx.wait(sync, f"b_xn_{i-1}")
            ctx.wait(sync, f"b_sp_{i-1}")
            sync.dma_start(out=xt[:], in_=src[i * 128:(i + 1) * 128, :]).then_inc(ctx.sem["LD"], 16)
            ctx.bump("LD", 16, tag=f"b_ld_{i}")
            if xin_ap is not None:
                sync.wait_ge(ctx.sem["LD"], ctx.cnt["LD"])
                sync.dma_start(out=xsp[i * 128:(i + 1) * 128, :], in_=xt[:]).then_inc(ctx.sem["ST"], 16)
                ctx.bump("ST", 16, tag=f"b_sp_{i}")

        @S_("vector")
        def _(vector, i=i):
            ctx.wait(vector, f"b_ld_{i}")

        @S_("scalar")
        def _(scalar, i=i):
            ctx.wait(scalar, f"b_ld_{i}")

        _ln_sections(ctx, g, xt[:], nm, rs, t0, sqscr, DM, f"b_ln_{i}")

        @S_("vector")
        def _(vector, i=i):
            ctx.wait(vector, f"b_ln_{i}_rs")
            ctx.wait(vector, f"b_tr_{i-1}")
            ctx.inc(vector.tensor_scalar(xnbf[:], xt[:], nm[:], rs[:], ALU.add, ALU.mult),
                    "VE", tag=f"b_xn_{i}")

        @S_("tensor")
        def _(tensor, i=i):
            ctx.wait(tensor, f"b_xn_{i}")
            ctx.wait(tensor, f"b_trev_{i-1}")
            for fh in range(2):
                for f8 in range(8):
                    f = fh * 8 + f8
                    mm = tensor.transpose(ps_t[:, fh * 1024 + f8 * 128:fh * 1024 + (f8 + 1) * 128],
                                          xnbf[:, f * 128:(f + 1) * 128], ident[:])
                ctx.inc(mm, "PE", tag=f"b_trp_{i}_{fh}")
            ctx.tag[f"b_tr_{i}"] = ctx.tag[f"b_trp_{i}_1"]

        @S_("vector")
        def _(vector, i=i):
            for fh in range(2):
                ctx.wait(vector, f"b_trp_{i}_{fh}")
                dst = xn2T[:].rearrange("p (f t) -> p f t", t=TPC)[:, fh * 8:(fh + 1) * 8,
                                                                  i * 128:(i + 1) * 128]
                srcv = ps_t[:, fh * 1024:(fh + 1) * 1024].rearrange("p (f t) -> p f t", t=128)
                ctx.inc(vector.tensor_copy(dst, srcv), "VE", tag=f"b_trev_{i}")

    for hf in range(2):
        tok0 = hf * 512

        for oc in range(64):
            for gu in range(2):
                wbuf = wg[(oc * 2 + gu) % 2]
                col0 = gu * DI + oc * 128

                par = "WL" + str((oc * 2 + gu) % 2)

                @S_("gpsimd")
                def _(gp, wbuf=wbuf, col0=col0, oc=oc, gu=gu, hf=hf, par=par):
                    ctx.wait(gp, f"b_parcons_{par}")
                    srcw = WguT[:, col0:col0 + 128].rearrange("(c p) o -> p c o", p=128)
                    gp.dma_start(out=wbuf[:].rearrange("p (c o) -> p c o", o=128),
                                 in_=srcw).then_inc(ctx.sem[par], 16)
                    ctx.bump(par, 16, tag=f"b_wld_{hf}_{oc}_{gu}")

                @S_("tensor")
                def _(tensor, wbuf=wbuf, oc=oc, gu=gu, hf=hf, tok0=tok0, par=par):
                    if hf == 0 and oc == 0 and gu == 0:
                        ctx.wait(tensor, "b_trev_7")
                    ctx.wait(tensor, f"b_wld_{hf}_{oc}_{gu}")
                    ctx.wait(tensor, "b_silu_used" if gu == 0 else "b_h_written")
                    for fc in range(16):
                        mm = tensor.matmul(ps_gu[:, gu * 512:gu * 512 + 512],
                                           wbuf[:, fc * 128:(fc + 1) * 128],
                                           xn2T[:, fc * TPC + tok0:fc * TPC + tok0 + 512],
                                           start=(fc == 0), stop=(fc == 15))
                    ctx.inc(mm, "PE", tag=f"b_wmm_{id(wbuf)}")
                    ctx.tag[f"b_gu_{gu}"] = ctx.tag[f"b_wmm_{id(wbuf)}"]
                    ctx.tag[f"b_parcons_{par}"] = ctx.tag[f"b_wmm_{id(wbuf)}"]

                if gu == 0:
                    @S_("scalar")
                    def _(scalar):
                        ctx.wait(scalar, "b_gu_0")
                        ctx.wait(scalar, "b_silu_read")
                        ctx.inc(scalar.activation(silu_sb[:], ps_gu[:, 0:512], AF.Silu),
                                "AC", tag="b_silu_used")
                else:
                    @S_("vector")
                    def _(vector, oc=oc, hf=hf):
                        ctx.wait(vector, "b_gu_1")
                        ctx.wait(vector, "b_silu_used")
                        ctx.wait(vector, f"b_hWAR_{hf}")
                        n = ctx.inc(vector.tensor_tensor(h_sb[:, oc * 512:(oc + 1) * 512],
                                                         ps_gu[:, 512:1024], silu_sb[:],
                                                         ALU.mult),
                                    "VE", tag="b_h_written")
                        ctx.tag["b_silu_read"] = ("VE", n)

        ctx.tag[f"b_hdone_{hf}"] = ctx.tag["b_h_written"]

        for o2 in range(16):
            wbuf = wd[o2 % 2]

            par = "WL" + str(o2 % 2)

            @S_("gpsimd")
            def _(gp, wbuf=wbuf, o2=o2, hf=hf, par=par):
                ctx.wait(gp, f"b_parcons_{par}")
                srcw = Wd0T[:, o2 * 128:(o2 + 1) * 128].rearrange("(c p) o -> p c o", p=128)
                gp.dma_start(out=wbuf[:].rearrange("p (c o) -> p c o", o=128),
                             in_=srcw).then_inc(ctx.sem[par], 16)
                ctx.bump(par, 16, tag=f"b_wdld_{hf}_{o2}")

            @S_("tensor")
            def _(tensor, wbuf=wbuf, o2=o2, hf=hf, par=par):
                ctx.wait(tensor, f"b_wdld_{hf}_{o2}")
                ctx.wait(tensor, f"b_hdone_{hf}")
                ctx.wait(tensor, "b_dnev")
                for ic in range(64):
                    mm = tensor.matmul(ps_d[:], wbuf[:, ic * 128:(ic + 1) * 128],
                                       h_sb[:, ic * 512:(ic + 1) * 512],
                                       start=(ic == 0), stop=(ic == 63))
                ctx.inc(mm, "PE", tag=f"b_wmm_{id(wbuf)}")
                ctx.tag["b_dnmm"] = ctx.tag[f"b_wmm_{id(wbuf)}"]
                ctx.tag[f"b_parcons_{par}"] = ctx.tag[f"b_wmm_{id(wbuf)}"]

            @S_("vector")
            def _(vector, o2=o2, hf=hf):
                ctx.wait(vector, "b_dnmm")
                ctx.wait(vector, "b_dnfm_free")
                ctx.inc(vector.tensor_copy(dn_fm[:, o2 * 512:(o2 + 1) * 512], ps_d[:]),
                        "VE", tag="b_dnev")
        ctx.tag[f"b_hWAR_{1 - hf}"] = ctx.tag["b_dnmm"]

        for tt in range(4):
            i = hf * 4 + tt

            @S_("tensor")
            def _(tensor, tt=tt, hf=hf):
                ctx.wait(tensor, "b_dnev")
                ctx.wait(tensor, f"b_dtev")
                for fh in range(2):
                    for f8 in range(8):
                        f = fh * 8 + f8
                        mm = tensor.transpose(ps_t[:, fh * 1024 + f8 * 128:fh * 1024 + (f8 + 1) * 128],
                                              dn_fm[:, f * 512 + tt * 128:f * 512 + (tt + 1) * 128],
                                              ident[:])
                    ctx.inc(mm, "PE", tag=f"b_dtp_{fh}")
                ctx.tag["b_dnfm_free"] = ctx.tag["b_dtp_1"]

            @S_("vector")
            def _(vector, tt=tt):
                for fh in range(2):
                    ctx.wait(vector, f"b_dtp_{fh}")
                    ctx.inc(vector.tensor_copy(dn_tm[:, fh * 1024:(fh + 1) * 1024],
                                               ps_t[:, fh * 1024:(fh + 1) * 1024]), "VE", tag="b_dtev")

            emit_post_ln_add(ctx, g, dn_tm, xt, nm, rs, t0, sqscr, xnbf, i,
                             xout_ap, tag=f"bp_{i}", dn_ready_tag="b_dtev")


def emit_post_ln_add(ctx, g, dn_tm, xt, nm, rs, t0, sqscr, obf, i, xout_ap, tag,
                     dn_ready_tag):
    S_ = ctx.sec
    xsp = g["xsp"]

    @S_("sync")
    def _(sync, i=i):
        ctx.wait_cnt(sync, "ST")
        ctx.wait(sync, dn_ready_tag)
        sync.dma_start(out=xt[:], in_=xsp[i * 128:(i + 1) * 128, :]).then_inc(ctx.sem["LD"], 16)
        ctx.bump("LD", 16, tag=f"{tag}_ld")

    @S_("vector")
    def _(vector):
        ctx.wait(vector, dn_ready_tag)

    @S_("scalar")
    def _(scalar):
        ctx.wait(scalar, dn_ready_tag)

    _ln_sections(ctx, g, dn_tm[:], nm, rs, t0, sqscr, DM, tag + "_ln")

    @S_("vector")
    def _(vector):
        ctx.wait(vector, tag + "_ln_rs")
        ctx.wait(vector, f"{tag}_ld")
        ctx.inc(vector.tensor_scalar(obf[:], dn_tm[:], nm[:], rs[:], ALU.add, ALU.mult), "VE")
        ctx.wait_cnt(vector, "VE")
        ctx.inc(vector.tensor_tensor(xt[:], xt[:], obf[:], ALU.add),
                "VE", tag=f"{tag}_new")

    @S_("sync")
    def _(sync, i=i):
        ctx.wait(sync, f"{tag}_new")
        sync.dma_start(out=xsp[i * 128:(i + 1) * 128, :], in_=xt[:]).then_inc(ctx.sem["ST"], 16)
        n = ctx.bump("ST", 16)
        if xout_ap is not None:
            sync.dma_start(out=xout_ap[i * 128:(i + 1) * 128, :], in_=xt[:]).then_inc(ctx.sem["ST"], 16)
            n = ctx.bump("ST", 16)
        ctx.tag[f"{tag}_st"] = ("ST", n)
        ctx.tag[_next_xused(tag)] = ("ST", n)


def _next_xused(tag):
    head, idx = tag.rsplit("_", 1)
    return f"{head}_{int(idx) + 1}_xused_prev"



def emit_phase_A(ctx, g, xin_ap, xout_ap):
    S_ = ctx.sec
    nc = ctx.nc
    xsp = g["xsp"]
    ident = g["ident_sb"]
    src = xin_ap if xin_ap is not None else xsp
    WqT, WkT, WvT, WoT, SD = g["WqT"], g["WkT"], g["WvT"], g["WoT"], g["SD"]
    kag_in, kag_out = g["kag_in"], g["kag_out"]
    vag_in, vag_out = g["vag_in"], g["vag_out"]

    P = g["phase_stack"].enter_context
    xt = P(nc.sbuf_tensor("a_xt", [128, DM], F32))
    xnbf = P(nc.sbuf_tensor("a_xnbf", [128, DM], BF16))
    sqscr = P(nc.sbuf_tensor("a_sqscr", [128, DM], F32))
    nm = P(nc.sbuf_tensor("a_nm", [128, 1], F32))
    rs = P(nc.sbuf_tensor("a_rs", [128, 1], F32))
    t0 = P(nc.sbuf_tensor("a_t0", [128, 1], F32))
    abig = P(nc.sbuf_tensor("a_big", [128, 4 * 8192], BF16))
    xn1T = abig[:, 0:16384]
    wkv = abig[:, 16384:32768]
    scores = abig[:]
    qsb = P(nc.sbuf_tensor("a_q", [128, 16 * TPC], BF16))
    wqb = [P(nc.sbuf_tensor(f"a_wq{j}", [128, 16 * 128], BF16)) for j in range(2)]
    wob = [P(nc.sbuf_tensor(f"a_wo{j}", [128, 2048], BF16)) for j in range(2)]
    kvown = P(nc.sbuf_tensor("a_kvown", [128, 1024], BF16))
    ksb = [P(nc.sbuf_tensor(f"a_k{j}", [128, 512], BF16)) for j in range(2)]
    vsb = [P(nc.sbuf_tensor(f"a_v{j}", [128, 512], BF16)) for j in range(2)]
    sdb = [P(nc.sbuf_tensor(f"a_sd{j}", [128, 512], BF16)) for j in range(2)]
    pt_sb = [P(nc.sbuf_tensor(f"a_pt{j}", [128, 512], BF16)) for j in range(2)]
    gmx = P(nc.sbuf_tensor("a_gmx", [128, 8], F32))
    sumc = P(nc.sbuf_tensor("a_sumc", [128, 8], F32))
    bcol = P(nc.sbuf_tensor("a_bcol", [128, 8], F32))
    rcolbf = P(nc.sbuf_tensor("a_rcolbf", [128, 16], BF16))
    rowbf = P(nc.sbuf_tensor("a_rowbf", [1, 2048], BF16))
    nbc = P(nc.sbuf_tensor("a_nbc", [128, 2048], BF16))
    o_sb = P(nc.sbuf_tensor("a_o", [128, 2048], BF16))
    dn_tm = P(nc.sbuf_tensor("a_dntm", [128, DM], BF16))
    onesb_a = P(nc.sbuf_tensor("a_onesb", [1, 128], BF16))
    g["onesb_a"] = onesb_a
    ps_kv = P(nc.psum_tensor("a_pskv", [128, 1024], F32))
    ps_pt = P(nc.psum_tensor("a_pspt", [128, 512], BF16))
    ps_o = P(nc.psum_tensor("a_pso", [128, 2048], F32))
    ps_s = ps_kv
    ps_t = ps_o[:].bitcast(BF16)

    def slope(h):
        return 2.0 ** (-0.5 * (h + 1))

    @S_("vector")
    def _(vector):
        ctx.inc(vector.memset(onesb_a[:], 1.0), "VE", tag="a_ones")

    def sd_idx(i, kc):
        return sum(2 * (j + 1) for j in range(i)) + kc

    def owner(a):
        return a % 8, a // 8

    for i in range(NT):
        @S_("sync")
        def _(sync, i=i):
            ctx.wait(sync, f"a_xn_{i-1}")
            ctx.wait(sync, f"a_sp_{i-1}")
            sync.dma_start(out=xt[:], in_=src[i * 128:(i + 1) * 128, :]).then_inc(ctx.sem["LD"], 16)
            ctx.bump("LD", 16, tag=f"a_ld_{i}")
            if xin_ap is not None:
                sync.wait_ge(ctx.sem["LD"], ctx.cnt["LD"])
                sync.dma_start(out=xsp[i * 128:(i + 1) * 128, :], in_=xt[:]).then_inc(ctx.sem["ST"], 16)
                ctx.bump("ST", 16, tag=f"a_sp_{i}")

        @S_("vector")
        def _(vector, i=i):
            ctx.wait(vector, f"a_ld_{i}")

        @S_("scalar")
        def _(scalar, i=i):
            ctx.wait(scalar, f"a_ld_{i}")

        _ln_sections(ctx, g, xt[:], nm, rs, t0, sqscr, DM, f"a_ln_{i}")

        @S_("vector")
        def _(vector, i=i):
            ctx.wait(vector, f"a_ln_{i}_rs")
            ctx.wait(vector, f"a_tr_{i-1}")
            ctx.inc(vector.tensor_scalar(xnbf[:], xt[:], nm[:], rs[:], ALU.add, ALU.mult),
                    "VE", tag=f"a_xn_{i}")

        @S_("tensor")
        def _(tensor, i=i):
            ctx.wait(tensor, f"a_xn_{i}")
            ctx.wait(tensor, f"a_trev_{i-1}")
            for fh in range(2):
                for f8 in range(8):
                    f = fh * 8 + f8
                    mm = tensor.transpose(
                        ps_t[:, fh * 1024 + f8 * 128:fh * 1024 + (f8 + 1) * 128],
                        xnbf[:, f * 128:(f + 1) * 128], ident[:])
                ctx.inc(mm, "PE", tag=f"a_trp_{i}_{fh}")
            ctx.tag[f"a_tr_{i}"] = ctx.tag[f"a_trp_{i}_1"]

        @S_("vector")
        def _(vector, i=i):
            for fh in range(2):
                ctx.wait(vector, f"a_trp_{i}_{fh}")
                dst = xn1T.rearrange("p (f t) -> p f t", t=TPC)[:, fh * 8:(fh + 1) * 8,
                                                                  i * 128:(i + 1) * 128]
                srcv = ps_t[:, fh * 1024:(fh + 1) * 1024].rearrange("p (f t) -> p f t", t=128)
                ctx.inc(vector.tensor_copy(dst, srcv), "VE", tag=f"a_trev_{i}")
    ctx.tag["a_xnT_ready"] = ctx.tag["a_trev_7"]

    @S_("gpsimd")
    def _(gp):
        sk = WkT.rearrange("(hc p) k -> p hc k", p=128)
        gp.dma_start(out=wkv.rearrange("p (hc e) -> p hc e", e=1024)[:, :, 0:512],
                     in_=sk).then_inc(ctx.sem["LDG"], 16)
        ctx.bump("LDG", 16)
        sv = WvT.rearrange("(hc p) k -> p hc k", p=128)
        gp.dma_start(out=wkv.rearrange("p (hc e) -> p hc e", e=1024)[:, :, 512:1024],
                     in_=sv).then_inc(ctx.sem["LDG"], 16)
        ctx.bump("LDG", 16, tag="a_wkv_ld")

    for gg in range(GRP):
        @S_("tensor")
        def _(tensor, gg=gg):
            if gg == 0:
                ctx.wait(tensor, "a_xnT_ready")
                ctx.wait(tensor, "a_wkv_ld")
            ctx.wait(tensor, "a_kv_ev")
            for hc in range(16):
                for tb in range(2):
                    mm = tensor.matmul(ps_kv[:, tb * 512:(tb + 1) * 512],
                                       wkv[:, hc * 1024 + gg * 128:hc * 1024 + gg * 128 + 128],
                                       xn1T[:, hc * TPC + tb * 512:hc * TPC + (tb + 1) * 512],
                                       start=(hc == 0), stop=(hc == 15))
            ctx.inc(mm, "PE", tag="a_kv_mm")

        @S_("vector")
        def _(vector, gg=gg):
            ctx.wait(vector, "a_kv_mm")
            ctx.wait(vector, "a_kv_st")
            ctx.inc(vector.tensor_copy(kvown[:], ps_kv[:]), "VE", tag="a_kv_ev")

        @S_("sync")
        def _(sync, gg=gg):
            ctx.wait(sync, "a_kv_ev")
            sync.dma_start(out=kag_in.ap()[gg * 128:(gg + 1) * 128, :], in_=kvown[:]).then_inc(ctx.sem["ST"], 16)
            ctx.bump("ST", 16, tag="a_kv_st")

    for tt in range(NT):
        @S_("tensor")
        def _(tensor, tt=tt):
            ctx.wait(tensor, "a_kv_ev")
            for hc in range(16):
                mm = tensor.matmul(ps_kv[:, 0:512],
                                   xn1T[:, hc * TPC + tt * 128:hc * TPC + (tt + 1) * 128],
                                   wkv.rearrange("p (hc e) -> p hc e", e=1024)[:, hc, 512:1024],
                                   start=(hc == 0), stop=(hc == 15))
            ctx.inc(mm, "PE", tag="a_kv_mm")

        @S_("vector")
        def _(vector, tt=tt):
            ctx.wait(vector, "a_kv_mm")
            ctx.wait(vector, "a_kv_st")
            ctx.inc(vector.tensor_copy(kvown[:, 0:512], ps_kv[:, 0:512]), "VE", tag="a_kv_ev")

        @S_("sync")
        def _(sync, tt=tt):
            ctx.wait(sync, "a_kv_ev")
            sync.dma_start(out=vag_in.ap()[tt * 128:(tt + 1) * 128, :], in_=kvown[:, 0:512]).then_inc(ctx.sem["ST"], 16)
            ctx.bump("ST", 16, tag="a_kv_st")

    @S_("gpsimd")
    def _(gp):
        ctx.wait(gp, "a_kv_st")
        gp.collective_compute("AllGather", ALU.bypass, replica_groups=[list(range(8))],
                              ins=[kag_in.ap().opt()], outs=[kag_out.ap().opt()]).then_inc(ctx.sem["CC"])
        ctx.bump("CC", 1)
        gp.wait_ge(ctx.sem["CC"], ctx.cnt["CC"])
        gp.collective_compute("AllGather", ALU.bypass, replica_groups=[list(range(8))],
                              ins=[vag_in.ap().opt()], outs=[vag_out.ap().opt()]).then_inc(ctx.sem["CC"])
        ctx.bump("CC", 1, tag="a_gather")
        gp.wait_ge(ctx.sem["CC"], ctx.cnt["CC"])

    for h in range(HEADS):
        par = "WL" + str(h % 2)
        wbuf = wqb[h % 2]

        @S_("gpsimd")
        def _(gp, h=h, par=par, wbuf=wbuf):
            ctx.wait(gp, f"a_qcons_{par}")
            srcw = WqT[:, h * 128:(h + 1) * 128].rearrange("(c p) o -> p c o", p=128)
            gp.dma_start(out=wbuf[:].rearrange("p (c o) -> p c o", o=128),
                         in_=srcw).then_inc(ctx.sem[par], 16)
            ctx.bump(par, 16, tag=f"a_qld_{h}")

        @S_("tensor")
        def _(tensor, h=h, par=par, wbuf=wbuf):
            ctx.wait(tensor, f"a_qld_{h}")
            ctx.wait(tensor, "a_q_ev")
            if h == 0:
                ctx.wait(tensor, "a_kv_ev")
            for hc in range(16):
                for tb in range(2):
                    mm = tensor.matmul(ps_kv[:, tb * 512:(tb + 1) * 512],
                                       wbuf[:, hc * 128:(hc + 1) * 128],
                                       xn1T[:, hc * TPC + tb * 512:hc * TPC + (tb + 1) * 512],
                                       start=(hc == 0), stop=(hc == 15))
            n = ctx.inc(mm, "PE", tag="a_q_mm")
            ctx.tag[f"a_qcons_{par}"] = ("PE", n)

        @S_("vector")
        def _(vector, h=h):
            ctx.wait(vector, "a_q_mm")
            ctx.inc(vector.tensor_scalar(qsb[:, h * TPC:(h + 1) * TPC], ps_kv[:],
                                         1.0 / (slope(h) * math.sqrt(HD)), None,
                                         ALU.mult), "VE", tag="a_q_ev")
    ctx.tag["a_q_ready"] = ctx.tag["a_q_ev"]

    for i in range(NT):
        nkc = 2 * (i + 1)
        for gg in range(GRP):
            for kc in range(nkc):
                kp = kc % 2

                @S_("sync")
                def _(sync, i=i, gg=gg, kc=kc, kp=kp):
                    if i == 0 and gg == 0 and kc == 0:
                        ctx.wait(sync, "a_gather")
                    ctx.wait(sync, f"a_kuse_{kp}")
                    for sub in range(4):
                        a = kc * 4 + sub
                        rnk, slot = owner(a)
                        sync.dma_start(
                            out=ksb[kp][:, sub * 128:(sub + 1) * 128],
                            in_=kag_out.ap()[rnk * 512 + gg * 128:rnk * 512 + (gg + 1) * 128,
                                             slot * 128:(slot + 1) * 128]).then_inc(ctx.sem["KL" + str(kp)], 16)
                        ctx.bump("KL" + str(kp), 16)
                    ctx.tag[f"a_kld_{kp}"] = ("KL" + str(kp), ctx.cnt["KL" + str(kp)])

                @S_("scalar")
                def _(scalar, i=i, kc=kc, kp=kp):
                    ctx.wait(scalar, f"a_sduse_{kp}")
                    scalar.dma_start(out=sdb[kp][:],
                                     in_=SD[sd_idx(i, kc) * 128:(sd_idx(i, kc) + 1) * 128, :]).then_inc(ctx.sem["SD" + str(kp)], 16)
                    ctx.bump("SD" + str(kp), 16, tag=f"a_sdld_{kp}")

                for r in range(4):
                    h = gg * 4 + r
                    sp = (kc * 4 + r) % 2

                    @S_("tensor")
                    def _(tensor, i=i, h=h, kc=kc, kp=kp, sp=sp):
                        if kc == 0 and r == 0:
                            ctx.wait(tensor, "a_q_ready")
                        ctx.wait(tensor, f"a_kld_{kp}")
                        ctx.wait(tensor, f"a_sev_{sp}")
                        ctx.inc(tensor.matmul(ps_s[:, sp * 512:(sp + 1) * 512],
                                              qsb[:, h * TPC + i * 128:h * TPC + (i + 1) * 128],
                                              ksb[kp][:], start=True, stop=True),
                                "PE", tag=f"a_smm_{sp}")
                        if r == 3:
                            ctx.tag[f"a_kuse_{kp}"] = ("PE", ctx.cnt["PE"])

                    @S_("vector")
                    def _(vector, r=r, kc=kc, kp=kp, sp=sp):
                        ctx.wait(vector, f"a_smm_{sp}")
                        ctx.wait(vector, f"a_sdld_{kp}")
                        ctx.wait(vector, "a_sc_free")
                        n = ctx.inc(vector.tensor_tensor(
                            scores[:, r * 8192 + kc * 512:r * 8192 + (kc + 1) * 512],
                            ps_s[:, sp * 512:(sp + 1) * 512], sdb[kp][:], ALU.subtract),
                            "VE", tag=f"a_sev_{sp}")
                        if r == 3:
                            ctx.tag[f"a_sduse_{kp}"] = ("VE", n)
                ctx.tag[f"a_scores_{gg}_{kc}"] = ctx.tag[f"a_sev_{(kc * 4 + 3) % 2}"]
            ctx.tag["a_passA"] = ("VE", ctx.cnt["VE"])

            @S_("vector")
            def _(vector, i=i, gg=gg, nkc=nkc):
                ctx.wait(vector, "a_passA")
                for r in range(4):
                    ctx.wait_cnt(vector, "VE")
                    ctx.inc(vector.reduce_max(gmx[:, r:r + 1],
                                              scores[:, r * 8192:r * 8192 + nkc * 512],
                                              AX.X), "VE", tag=f"a_mx_{r}")

            @S_("vector")
            def _(vector, gg=gg):
                for r in range(4):
                    ctx.wait(vector, f"a_mx_{r}")
                    ctx.wait_cnt(vector, "VE")
                    ctx.inc(vector.tensor_scalar(bcol[:, r:r + 1], gmx[:, r:r + 1],
                                                 -slope(gg * 4 + r), None, ALU.mult),
                            "VE", tag=f"a_bc_{r}")

            for r in range(4):
                @S_("scalar")
                def _(scalar, r=r, gg=gg, nkc=nkc):
                    ctx.wait(scalar, f"a_bc_{r}")
                    ctx.wait(scalar, "a_rc_done")
                    ctx.inc(scalar.activation(scores[:, r * 8192:r * 8192 + nkc * 512],
                                              scores[:, r * 8192:r * 8192 + nkc * 512],
                                              AF.Exp, bias=bcol[:, r:r + 1],
                                              scale=slope(gg * 4 + r),
                                              accum_out=sumc[:, r:r + 1]), "AC",
                            tag=f"a_exp_{r}")

            for kc in range(nkc):
                kp = kc % 2

                @S_("sync")
                def _(sync, gg=gg, kc=kc, kp=kp):
                    ctx.wait(sync, f"a_vuse_{kp}")
                    for sub in range(4):
                        a = kc * 4 + sub
                        rnk, slot = owner(a)
                        sync.dma_start(
                            out=vsb[kp][:, sub * 128:(sub + 1) * 128],
                            in_=vag_out.ap()[rnk * 1024 + slot * 128:rnk * 1024 + (slot + 1) * 128,
                                             gg * 128:(gg + 1) * 128]).then_inc(ctx.sem["VL" + str(kp)], 16)
                        ctx.bump("VL" + str(kp), 16)
                    ctx.tag[f"a_vld_{kp}"] = ("VL" + str(kp), ctx.cnt["VL" + str(kp)])

                for sub in range(4):
                    pp = (kc * 4 + sub) % 2

                    @S_("tensor")
                    def _(tensor, gg=gg, kc=kc, sub=sub, pp=pp, i=i):
                        for r in range(4):
                            if r == 0:
                                ctx.wait(tensor, "a_exp_3")
                                ctx.wait(tensor, f"a_ptev_{pp}")
                            mm = tensor.transpose(
                                ps_pt[:, r * 128:(r + 1) * 128],
                                scores[:, r * 8192 + kc * 512 + sub * 128:r * 8192 + kc * 512 + (sub + 1) * 128],
                                ident[:])
                        ctx.inc(mm, "PE", tag=f"a_pt_{pp}")

                    @S_("vector")
                    def _(vector, pp=pp):
                        ctx.wait(vector, f"a_pt_{pp}")
                        ctx.wait(vector, f"a_ptuse_{pp}")
                        ctx.inc(vector.tensor_copy(pt_sb[pp][:], ps_pt[:]), "VE",
                                tag=f"a_ptev_{pp}")

                    @S_("tensor")
                    def _(tensor, gg=gg, kc=kc, sub=sub, pp=pp, kp=kp):
                        ctx.wait(tensor, f"a_ptev_{pp}")
                        ctx.wait(tensor, f"a_vld_{kp}")
                        if kc == 0 and sub == 0:
                            ctx.wait(tensor, "a_opev")
                        n = ctx.inc(tensor.matmul(ps_o[:, gg * 512:(gg + 1) * 512],
                                                  vsb[kp][:, sub * 128:(sub + 1) * 128],
                                                  pt_sb[pp][:],
                                                  start=(kc == 0 and sub == 0),
                                                  stop=(kc == nkc - 1 and sub == 3)),
                                    "PE", tag=f"a_pv_{pp}")
                        ctx.tag[f"a_ptuse_{pp}"] = ("PE", n)
                        if sub == 3:
                            ctx.tag[f"a_vuse_{kp}"] = ("PE", n)
            ctx.tag["a_sc_free"] = ("PE", ctx.cnt["PE"])

            @S_("vector")
            def _(vector, gg=gg):
                ctx.wait(vector, "a_exp_3")
                for r in range(4):
                    ctx.wait_cnt(vector, "VE")
                    ctx.inc(vector.reciprocal(bcol[:, 4 + r:5 + r], sumc[:, r:r + 1]), "VE")
                    ctx.wait_cnt(vector, "VE")
                    ctx.inc(vector.tensor_copy(rcolbf[:, gg * 4 + r:gg * 4 + r + 1],
                                               bcol[:, 4 + r:5 + r]),
                            "VE", tag="a_rc_done")

        for rnd in range(2):
            @S_("tensor")
            def _(tensor, rnd=rnd):
                ctx.wait(tensor, "a_rc_done")
                ctx.wait(tensor, "a_rowev")
                for h8 in range(8):
                    h = rnd * 8 + h8
                    mm = tensor.matmul(ps_s[0:1, h8 * 128:(h8 + 1) * 128],
                                       rcolbf[:, h:h + 1], ident[:],
                                       start=True, stop=True)
                ctx.inc(mm, "PE", tag="a_nrow")

            @S_("vector")
            def _(vector, rnd=rnd):
                ctx.wait(vector, "a_nrow")
                ctx.inc(vector.tensor_copy(rowbf[0:1, rnd * 1024:(rnd + 1) * 1024],
                                           ps_s[0:1, 0:1024]), "VE", tag="a_rowev")

        for cch in range(4):
            @S_("tensor")
            def _(tensor, cch=cch):
                ctx.wait(tensor, "a_rowev")
                ctx.wait(tensor, f"a_nbcev_{(cch - 2) % 2}")
                ctx.inc(tensor.matmul(ps_s[:, (cch % 2) * 512:(cch % 2 + 1) * 512],
                                      g["onesb_a"][:], rowbf[0:1, cch * 512:(cch + 1) * 512],
                                      start=True, stop=True), "PE", tag=f"a_nbc_{cch % 2}")

            @S_("vector")
            def _(vector, cch=cch):
                ctx.wait(vector, f"a_nbc_{cch % 2}")
                ctx.inc(vector.tensor_copy(nbc[:, cch * 512:(cch + 1) * 512],
                                           ps_s[:, (cch % 2) * 512:(cch % 2 + 1) * 512]),
                        "VE", tag=f"a_nbcev_{cch % 2}")

        @S_("vector")
        def _(vector, i=i):
            ctx.wait(vector, "a_nbcev_0")
            ctx.wait(vector, "a_nbcev_1")
            for cch in range(4):
                ctx.wait_cnt(vector, "VE")
                ctx.inc(vector.tensor_tensor(o_sb[:, cch * 512:(cch + 1) * 512],
                                             ps_o[:, cch * 512:(cch + 1) * 512],
                                             nbc[:, cch * 512:(cch + 1) * 512],
                                             ALU.mult), "VE", tag="a_o_ev")

        for h in range(HEADS):
            par = "WL" + str(h % 2)
            wbuf = wob[h % 2]

            @S_("gpsimd")
            def _(gp, h=h, par=par, wbuf=wbuf):
                ctx.wait(gp, f"a_ocons_{par}")
                gp.dma_start(out=wbuf[:], in_=WoT[h * 128:(h + 1) * 128, :]).then_inc(ctx.sem[par], 16)
                ctx.bump(par, 16, tag=f"a_old_{h}")

            @S_("tensor")
            def _(tensor, h=h, par=par, wbuf=wbuf):
                ctx.wait(tensor, f"a_old_{h}")
                if h == 0:
                    ctx.wait(tensor, "a_o_ev")
                for oc in range(4):
                    mm = tensor.matmul(ps_o[:, oc * 512:(oc + 1) * 512],
                                       o_sb[:, h * 128:(h + 1) * 128],
                                       wbuf[:, oc * 512:(oc + 1) * 512],
                                       start=(h == 0), stop=(h == 15))
                n = ctx.inc(mm, "PE", tag="a_opmm")
                ctx.tag[f"a_ocons_{par}"] = ("PE", n)

        @S_("vector")
        def _(vector, i=i):
            ctx.wait(vector, "a_opmm")
            for cch in range(4):
                ctx.wait_cnt(vector, "VE")
                ctx.inc(vector.tensor_copy(dn_tm[:, cch * 512:(cch + 1) * 512],
                                           ps_o[:, cch * 512:(cch + 1) * 512]), "VE",
                        tag="a_opev")

        emit_post_ln_add(ctx, g, dn_tm, xt, nm, rs, t0, sqscr, xnbf, i,
                         xout_ap, tag=f"ap_{i}", dn_ready_tag="a_opev")


def emit_phase_C(ctx, g, xin_ap, xout_ap):
    S_ = ctx.sec
    nc = ctx.nc
    xsp = g["xsp"]
    ident = g["ident_sb"]
    src = xin_ap if xin_ap is not None else xsp
    WinT, WopT = g["WinT"], g["WopT"]
    a2a_xi, a2a_xo = g["a2a_xi"], g["a2a_xo"]
    a2a_yi, a2a_yo = g["a2a_yi"], g["a2a_yo"]
    ar_xi, ar_xo = g["ar_xi"], g["ar_xo"]
    z_dram, xn_dram = g["z_dram"], g["xn_dram"]
    xc_dr, bc_dr = g["xc_dr"], g["bc_dr"]
    H = S // 2

    P = g["phase_stack"].enter_context
    xt = P(nc.sbuf_tensor("c_xt", [128, DM], F32))
    xnbf = P(nc.sbuf_tensor("c_xnbf", [128, DM], BF16))
    sqscr = P(nc.sbuf_tensor("c_sqscr", [128, DM], F32))
    nm = P(nc.sbuf_tensor("c_nm", [128, 1], F32))
    rs = P(nc.sbuf_tensor("c_rs", [128, 1], F32))
    t0 = P(nc.sbuf_tensor("c_t0", [128, 1], F32))
    rs2 = P(nc.sbuf_tensor("c_rs2", [128, 1], F32))
    wi_b = [P(nc.sbuf_tensor(f"c_wi{j}", [128, 16 * 128], BF16)) for j in range(2)]
    wo_b = [P(nc.sbuf_tensor(f"c_wo{j}", [128, 16 * 128], BF16)) for j in range(2)]
    evb = P(nc.sbuf_tensor("c_evb", [128, 1024], BF16))
    nsc_sb = P(nc.sbuf_tensor("c_nsc", [128, 16], F32))
    carry = P(nc.sbuf_tensor("c_carry", [128, 16], F32))
    convw = P(nc.sbuf_tensor("c_convw", [128, 16], F32))
    convb = P(nc.sbuf_tensor("c_convb", [128, 4], F32))
    alogc = P(nc.sbuf_tensor("c_alog", [128, 64], F32))
    acol = P(nc.sbuf_tensor("c_acol", [128, 64], F32))
    dcol = P(nc.sbuf_tensor("c_dcol", [128, 4], F32))
    dtbc = P(nc.sbuf_tensor("c_dtbc", [128, 4], F32))
    wxp = P(nc.sbuf_tensor("c_wxp", [128, 4 * 160], BF16))
    wdt = P(nc.sbuf_tensor("c_wdt", [128, 512], BF16))
    xd1 = P(nc.sbuf_tensor("c_xd1", [128, 2048], F32))
    dn_tm = P(nc.sbuf_tensor("c_dntm", [128, DM], BF16))
    onesb = P(nc.sbuf_tensor("c_onesb", [1, 128], BF16))
    AR = P(nc.sbuf_tensor("c_arena", [128, 73760], BF16))
    ps_a = P(nc.psum_tensor("c_psa", [128, 1024], F32))
    ps_b = P(nc.psum_tensor("c_psb", [128, 1024], F32))
    ps_x1 = P(nc.psum_tensor("c_psx1", [32, 512], F32))
    ps_t = ps_a[:].bitcast(BF16)

    xmT = AR[:, 0:16384]
    xin_sb = AR[:, 16384:24579]
    xc_sl = AR[:, 24580:32772]
    accf = AR[:, 32772:40964].bitcast(F32)
    xpr0 = AR[:, 40964:57348].bitcast(F32)
    X_sl = AR[:, 0:8192]
    dl_sl = AR[:, 8192:16384]
    bcb = [AR[:, 40964 + j * 8192:40964 + (j + 1) * 8192] for j in range(2)]
    yacf = AR[:, 57348:65540].bitcast(F32)
    b_sl = AR[:, 65540:69636]
    h_sl = AR[:, 69636:73732]
    spr = [AR[:, 57348 + j * 1024:57348 + (j + 1) * 1024].bitcast(F32) for j in range(4)]
    ym = AR[:, 0:16384]
    opacc = AR[:, 16384:32768]
    zy_sl = AR[:, 65540:67588]

    @S_("vector")
    def _(vector):
        vector.memset(xin_sb[:, 0:3], 0.0)
        ctx.inc(vector.memset(onesb[:], 1.0), "VE", tag="c_pad")

    @S_("sync")
    def _(sync):
        sync.dma_start(allow_slow_non_contiguous=True, out=nsc_sb[:].rearrange("p (f one) -> p f one", one=1), in_=g["nsc"].rearrange("(f p) one -> p f one", p=128)).then_inc(ctx.sem["LD"], 16)
        sync.dma_start(allow_slow_non_contiguous=True, out=convw[:].rearrange("p (ct k) -> p ct k", k=4), in_=g["convw"].rearrange("(ct p) k -> p ct k", p=128)).then_inc(ctx.sem["LD"], 16)
        sync.dma_start(allow_slow_non_contiguous=True, out=convb[:].rearrange("p (ct one) -> p ct one", one=1), in_=g["convb"].rearrange("(ct p) one -> p ct one", p=128)).then_inc(ctx.sem["LD"], 16)
        sync.dma_start(allow_slow_non_contiguous=True, out=dcol[:].rearrange("p (ct one) -> p ct one", one=1), in_=g["Dvec"].rearrange("(ct p) one -> p ct one", p=128)).then_inc(ctx.sem["LD"], 16)
        sync.dma_start(allow_slow_non_contiguous=True, out=dtbc[:].rearrange("p (ct one) -> p ct one", one=1), in_=g["dtb"].rearrange("(ct p) one -> p ct one", p=128)).then_inc(ctx.sem["LD"], 16)
        sync.dma_start(allow_slow_non_contiguous=True, out=alogc[:].rearrange("p (ct s) -> p ct s", s=16), in_=g["Alog"].rearrange("(ct p) s -> p ct s", p=128)).then_inc(ctx.sem["LD"], 16)
        ctx.bump("LD", 16 * 6, tag="c_cols_ld")

    @S_("scalar")
    def _(scalar):
        ctx.wait(scalar, "c_cols_ld")
        ctx.inc(scalar.activation(acol[:], alogc[:], AF.Exp), "AC", tag="c_alog_e")

    @S_("vector")
    def _(vector):
        ctx.wait(vector, "c_alog_e")
        ctx.wait_cnt(vector, "VE")
        ctx.inc(vector.tensor_scalar(acol[:], acol[:], -1.0, None, ALU.mult),
                "VE", tag="c_acol")

    @S_("gpsimd")
    def _(gp):
        gp.dma_start(allow_slow_non_contiguous=True, out=wxp[:].rearrange("p (ct r) -> p ct r", r=160), in_=g["WxpT"].rearrange("(ct p) r -> p ct r", p=128)).then_inc(ctx.sem["LDG"], 16)
        ctx.bump("LDG", 16)
        gp.dma_start(out=wdt[:], in_=g["WdtT"]).then_inc(ctx.sem["LDG"], 16)
        ctx.bump("LDG", 16, tag="c_w_ld")

    for i in range(NT):
        @S_("sync")
        def _(sync, i=i):
            ctx.wait(sync, f"c_xm_{i-1}")
            ctx.wait(sync, f"c_sp_{i-1}")
            sync.dma_start(out=xt[:], in_=src[i * 128:(i + 1) * 128, :]).then_inc(ctx.sem["LD"], 16)
            ctx.bump("LD", 16, tag=f"c_ld_{i}")
            if xin_ap is not None:
                sync.wait_ge(ctx.sem["LD"], ctx.cnt["LD"])
                sync.dma_start(out=xsp[i * 128:(i + 1) * 128, :], in_=xt[:]).then_inc(ctx.sem["ST"], 16)
                ctx.bump("ST", 16, tag=f"c_sp_{i}")

        @S_("vector")
        def _(vector, i=i):
            ctx.wait(vector, f"c_ld_{i}")

        @S_("scalar")
        def _(scalar, i=i):
            ctx.wait(scalar, f"c_ld_{i}")

        _ln_sections(ctx, g, xt[:], nm, rs, t0, sqscr, DM, f"c_ln_{i}")

        @S_("vector")
        def _(vector, i=i):
            ctx.wait(vector, f"c_ln_{i}_rs")
            ctx.wait(vector, f"c_hst_{i-1}")
            ctx.inc(vector.tensor_scalar(xnbf[:], xt[:], nm[:], rs[:], ALU.add, ALU.mult),
                    "VE", tag=f"c_h_{i}")

        @S_("sync")
        def _(sync, i=i):
            ctx.wait(sync, f"c_h_{i}")
            sync.dma_start(out=xn_dram.ap()[i * 128:(i + 1) * 128, :], in_=xnbf[:]).then_inc(ctx.sem["ST"], 16)
            ctx.bump("ST", 16, tag=f"c_hst_{i}")

        _ln_sections(ctx, g, xnbf[:], nm, rs2, t0, sqscr, DM, f"c_rms_{i}", rms_only=True)

        @S_("vector")
        def _(vector, i=i):
            ctx.wait(vector, f"c_rms_{i}_rs")
            ctx.wait(vector, f"c_hst_{i}")
            ctx.wait(vector, f"c_tr_{i-1}")
            ctx.inc(vector.tensor_scalar(xnbf[:], xnbf[:], rs2[:], None, ALU.mult),
                    "VE", tag=f"c_xm_{i}")

        @S_("tensor")
        def _(tensor, i=i):
            ctx.wait(tensor, f"c_xm_{i}")
            ctx.wait(tensor, f"c_trev_{i-1}")
            for fh in range(2):
                for f8 in range(8):
                    f = fh * 8 + f8
                    mm = tensor.transpose(
                        ps_t[:, fh * 1024 + f8 * 128:fh * 1024 + (f8 + 1) * 128],
                        xnbf[:, f * 128:(f + 1) * 128], ident[:])
                ctx.inc(mm, "PE", tag=f"c_trp_{i}_{fh}")
            ctx.tag[f"c_tr_{i}"] = ctx.tag[f"c_trp_{i}_1"]

        @S_("vector")
        def _(vector, i=i):
            if i == 0:
                ctx.wait(vector, "c_cols_ld")
            for fh in range(2):
                ctx.wait(vector, f"c_trp_{i}_{fh}")
                for f8 in range(8):
                    f = fh * 8 + f8
                    ctx.inc(vector.tensor_scalar(
                        xmT[:, f * TPC + i * 128:f * TPC + (i + 1) * 128],
                        ps_t[:, fh * 1024 + f8 * 128:fh * 1024 + (f8 + 1) * 128],
                        nsc_sb[:, f:f + 1], None, ALU.mult), "VE", tag=f"c_trev_{i}")
    ctx.tag["c_xmT_ready"] = ctx.tag["c_trev_7"]

    for ec in range(64):
        par = "WL" + str(ec % 2)
        wbuf = wi_b[ec % 2]

        @S_("gpsimd")
        def _(gp, ec=ec, par=par, wbuf=wbuf):
            ctx.wait(gp, f"c_icons_{par}")
            srcw = WinT[:, ec * 128:(ec + 1) * 128].rearrange("(c p) o -> p c o", p=128)
            gp.dma_start(out=wbuf[:].rearrange("p (c o) -> p c o", o=128),
                         in_=srcw).then_inc(ctx.sem[par], 16)
            ctx.bump(par, 16, tag=f"c_ild_{ec}")

        @S_("tensor")
        def _(tensor, ec=ec, par=par, wbuf=wbuf):
            if ec == 0:
                ctx.wait(tensor, "c_xmT_ready")
            ctx.wait(tensor, f"c_ild_{ec}")
            ctx.wait(tensor, "c_iev")
            for hc in range(16):
                for tb in range(2):
                    mm = tensor.matmul(ps_b[:, tb * 512:(tb + 1) * 512],
                                       wbuf[:, hc * 128:(hc + 1) * 128],
                                       xmT[:, hc * TPC + tb * 512:hc * TPC + (tb + 1) * 512],
                                       start=(hc == 0), stop=(hc == 15))
            n = ctx.inc(mm, "PE", tag="c_imm")
            ctx.tag[f"c_icons_{par}"] = ("PE", n)

        @S_("vector")
        def _(vector, ec=ec):
            ctx.wait(vector, "c_imm")
            ctx.wait(vector, "c_ist")
            ctx.inc(vector.tensor_copy(evb[:], ps_b[:]), "VE", tag="c_iev")

        @S_("sync")
        def _(sync, ec=ec):
            ctx.wait(sync, "c_iev")
            if ec < 32:
                dst = a2a_xi.ap()[ec * 128:(ec + 1) * 128, :]
            else:
                dst = z_dram.ap()[(ec - 32) * 128:(ec - 31) * 128, :]
            sync.dma_start(out=dst, in_=evb[:]).then_inc(ctx.sem["ST"], 16)
            ctx.bump("ST", 16, tag="c_ist")

    @S_("gpsimd")
    def _(gp):
        ctx.wait(gp, "c_ist")
        with allow_ici_all_to_all():
            gp.collective_compute("AllToAll", ALU.bypass, replica_groups=[list(range(8))],
                                  ins=[a2a_xi.ap().opt()], outs=[a2a_xo.ap().opt()]).then_inc(ctx.sem["CC"])
        ctx.bump("CC", 1, tag="c_a2a_x")
        gp.wait_ge(ctx.sem["CC"], ctx.cnt["CC"])

    for ct in range(4):
        @S_("sync")
        def _(sync, ct=ct):
            if ct == 0:
                ctx.wait(sync, "c_a2a_x")
            ctx.wait(sync, f"c_xpdone_{ct-1}")
            flat = a2a_xo.ap().rearrange("r c -> (r c)").rearrange(
                "(s chp j) -> s chp j", s=8, chp=512)
            vv = flat[:, ct * 128:(ct + 1) * 128, :].rearrange(
                "s p (i tau) -> p i s tau", tau=128)
            dst = xin_sb[:, 3:3 + S].rearrange("p (i s tau) -> p i s tau", i=8, s=8)
            sync.dma_start(out=dst, in_=vv).then_inc(ctx.sem["LD"], 16)
            ctx.bump("LD", 16, tag=f"c_xinld_{ct}")

        for half in range(2):
            o0 = half * H

            @S_("vector")
            def _(vector, ct=ct, half=half, o0=o0):
                ctx.wait(vector, f"c_xinld_{ct}")
                ctx.wait(vector, "c_pad")
                ctx.wait(vector, f"c_silu_{ct}_{half-1}" if half else f"c_silu_{ct-1}_1")
                ctx.wait_cnt(vector, "VE")
                ctx.inc(vector.tensor_scalar(accf[:], xin_sb[:, o0:o0 + H],
                                             convw[:, ct * 4:ct * 4 + 1], None, ALU.mult), "VE")
                for k in range(1, 4):
                    ctx.wait_cnt(vector, "VE")
                    ctx.inc(vector.scalar_tensor_tensor(accf[:], xin_sb[:, o0 + k:o0 + k + H],
                                                        convw[:, ct * 4 + k:ct * 4 + k + 1],
                                                        accf[:], ALU.mult, ALU.add), "VE")
                ctx.tag[f"c_conv_{ct}_{half}"] = ("VE", ctx.cnt["VE"])

            @S_("scalar")
            def _(scalar, ct=ct, half=half, o0=o0):
                ctx.wait(scalar, f"c_conv_{ct}_{half}")
                ctx.inc(scalar.activation(xc_sl[:, o0:o0 + H], accf[:], AF.Silu,
                                          bias=convb[:, ct:ct + 1]), "AC",
                        tag=f"c_silu_{ct}_{half}")

        @S_("sync")
        def _(sync, ct=ct):
            ctx.wait(sync, f"c_silu_{ct}_1")
            sync.dma_start(out=xc_dr.ap()[ct * 128:(ct + 1) * 128, :], in_=xc_sl[:]).then_inc(ctx.sem["ST"], 16)
            ctx.bump("ST", 16, tag=f"c_xcst_{ct}")

        for tc in range(16):
            @S_("tensor")
            def _(tensor, ct=ct, tc=tc):
                ctx.wait(tensor, f"c_silu_{ct}_1")
                if ct == 0 and tc == 0:
                    ctx.wait(tensor, "c_w_ld")
                ctx.wait(tensor, f"c_xpev_0")
                ctx.wait(tensor, f"c_xpev_1")
                ctx.inc(tensor.matmul(ps_a[:, (tc % 2) * 512:(tc % 2 + 1) * 512],
                                      wxp[:, ct * 160:ct * 160 + 128],
                                      xc_sl[:, tc * 512:(tc + 1) * 512],
                                      start=True, stop=True), "PE")
                ctx.inc(tensor.matmul(ps_x1[:, 0:512],
                                      wxp[:, ct * 160 + 128:ct * 160 + 160],
                                      xc_sl[:, tc * 512:(tc + 1) * 512],
                                      start=True, stop=True), "PE", tag=f"c_xpmm_{tc % 2}")

            @S_("vector")
            def _(vector, ct=ct, tc=tc):
                ctx.wait(vector, f"c_xpmm_{tc % 2}")
                ctx.wait_cnt(vector, "VE")
                pa = ps_a[:, (tc % 2) * 512:(tc % 2 + 1) * 512]
                x1dst = xd1[(tc % 4) * 32:(tc % 4 + 1) * 32, (tc // 4) * 512:(tc // 4 + 1) * 512]
                if ct == 0:
                    vector.tensor_copy(xpr0[:, tc * 512:(tc + 1) * 512], pa)
                    ctx.inc(vector.tensor_copy(x1dst, ps_x1[:, 0:512]), "VE",
                            tag=f"c_xpev_{tc % 2}")
                else:
                    vector.tensor_tensor(xpr0[:, tc * 512:(tc + 1) * 512],
                                         xpr0[:, tc * 512:(tc + 1) * 512], pa, ALU.add)
                    ctx.inc(vector.tensor_tensor(x1dst, x1dst, ps_x1[:, 0:512], ALU.add),
                            "VE", tag=f"c_xpev_{tc % 2}")
        ctx.tag[f"c_xpdone_{ct}"] = ctx.tag[f"c_xpev_1"]

    @S_("sync")
    def _(sync):
        ctx.wait(sync, "c_xpdone_3")
        sync.dma_start(out=ar_xi.ap()[0:128, :], in_=xpr0[:]).then_inc(ctx.sem["ST"], 16)
        ctx.bump("ST", 16)
        for q in range(4):
            xsrc = xd1[q * 32:(q + 1) * 32, :].rearrange("r (blk c) -> r blk c", blk=4)
            xdst = ar_xi.ap()[128:160, :].rearrange("r (blk qq c) -> r blk qq c",
                                                    blk=4, qq=4)[:, :, q, :]
            sync.dma_start(out=xdst, in_=xsrc).then_inc(ctx.sem["ST"], 16)
            ctx.bump("ST", 16)
        ctx.tag["c_xp_st"] = ("ST", ctx.cnt["ST"])

    @S_("gpsimd")
    def _(gp):
        ctx.wait(gp, "c_xp_st")
        gp.collective_compute("AllReduce", ALU.add, replica_groups=[list(range(8))],
                              ins=[ar_xi.ap().opt()], outs=[ar_xo.ap().opt()]).then_inc(ctx.sem["CC"])
        ctx.bump("CC", 1, tag="c_ar")
        gp.wait_ge(ctx.sem["CC"], ctx.cnt["CC"])

    @S_("gpsimd")
    def _(gp):
        ctx.wait(gp, "c_ar")
        for s in range(2 * DST):
            row = ar_xo.ap()[128 + s:129 + s, :]
            gp.dma_start(out=bc_dr.ap()[s * 128:(s + 1) * 128, :],
                         in_=row.broadcast_to([128, S])).then_inc(ctx.sem["LDG"], 16)
            ctx.bump("LDG", 16)
        ctx.tag["c_bc_built"] = ("LDG", ctx.cnt["LDG"])

    for ct in range(4):
        @S_("gpsimd")
        def _(gp, ct=ct):
            ctx.wait(gp, "c_ar")
            ctx.wait(gp, f"c_Xfree_{ct-1}")
            gp.dma_start(out=X_sl, in_=ar_xo.ap()[0:128, :]).then_inc(ctx.sem["LDG"], 16)
            ctx.bump("LDG", 16, tag=f"c_dtlld_{ct}")

        @S_("sync")
        def _(sync, ct=ct):
            ctx.wait(sync, f"c_xcfree_{ct-1}")
            sync.dma_start(out=xc_sl, in_=xc_dr.ap()[ct * 128:(ct + 1) * 128, :]).then_inc(ctx.sem["LD"], 16)
            ctx.bump("LD", 16, tag=f"c_xcld_{ct}")

        for tc in range(16):
            pp = tc % 2

            @S_("tensor")
            def _(tensor, ct=ct, tc=tc, pp=pp):
                if tc == 0:
                    ctx.wait(tensor, f"c_dtlld_{ct}")
                    if ct == 0:
                        ctx.wait(tensor, "c_iev")
                ctx.wait(tensor, f"c_dtu_{pp}")
                ctx.inc(tensor.matmul(ps_b[:, pp * 512:(pp + 1) * 512],
                                      wdt[:, ct * 128:(ct + 1) * 128],
                                      X_sl[:, tc * 512:(tc + 1) * 512],
                                      start=True, stop=True), "PE", tag=f"c_dtmm_{pp}")

            @S_("scalar")
            def _(scalar, ct=ct, tc=tc, pp=pp):
                ctx.wait(scalar, f"c_dtmm_{pp}")
                ctx.wait(scalar, f"c_dtudone_{pp}")
                ctx.inc(scalar.activation(accf[:, pp * 512:(pp + 1) * 512],
                                          ps_b[:, pp * 512:(pp + 1) * 512],
                                          AF.Identity, bias=dtbc[:, ct:ct + 1]), "AC",
                        tag=f"c_dtu_{pp}")

            @S_("vector")
            def _(vector, tc=tc, pp=pp):
                ctx.wait(vector, f"c_dtu_{pp}")
                ctx.wait_cnt(vector, "VE")
                ctx.inc(vector.tensor_scalar(spr[pp][:], accf[:, pp * 512:(pp + 1) * 512],
                                             0.0, None, ALU.max), "VE")
                ctx.wait_cnt(vector, "VE")
                ctx.inc(vector.tensor_scalar(spr[2 + pp][:], accf[:, pp * 512:(pp + 1) * 512],
                                             -1.0, None, ALU.mult), "VE")
                ctx.wait_cnt(vector, "VE")
                ctx.inc(vector.tensor_tensor(spr[2 + pp][:], spr[2 + pp][:],
                                             accf[:, pp * 512:(pp + 1) * 512], ALU.min),
                        "VE", tag=f"c_dtm_{pp}")

            @S_("scalar")
            def _(scalar, tc=tc, pp=pp):
                ctx.wait(scalar, f"c_dtm_{pp}")
                ctx.inc(scalar.activation(spr[2 + pp][:], spr[2 + pp][:], AF.Exp), "AC")
                ctx.wait_cnt(scalar, "AC")
                ctx.inc(scalar.activation(spr[2 + pp][:], spr[2 + pp][:], AF.Ln,
                                          bias=g["cONE"]), "AC", tag=f"c_dtln_{pp}")

            @S_("vector")
            def _(vector, ct=ct, tc=tc, pp=pp):
                ctx.wait(vector, f"c_dtln_{pp}")
                ctx.wait_cnt(vector, "VE")
                n = ctx.inc(vector.tensor_tensor(dl_sl[:, tc * 512:(tc + 1) * 512],
                                                 spr[pp][:], spr[2 + pp][:], ALU.add),
                            "VE", tag=f"c_dtev_{pp}")
                ctx.tag[f"c_dtudone_{pp}"] = ("VE", n)
        ctx.tag["c_delta"] = ctx.tag[f"c_dtev_1"]

        @S_("vector")
        def _(vector, ct=ct):
            ctx.wait(vector, "c_delta")
            ctx.wait(vector, f"c_xcld_{ct}")
            ctx.wait(vector, f"c_dtmm_0")
            ctx.wait(vector, f"c_dtmm_1")
            ctx.wait_cnt(vector, "VE")
            ctx.inc(vector.tensor_tensor(X_sl, dl_sl, xc_sl, ALU.mult), "VE", tag="c_du")

        for half in range(2):
            o0 = half * H

            @S_("vector")
            def _(vector, half=half):
                ctx.wait(vector, "c_yst")
                ctx.wait_cnt(vector, "VE")
                ctx.inc(vector.memset(yacf[:], 0.0), "VE", tag="c_yz")

            for s in range(DST):
                bp = s % 2

                @S_("gpsimd")
                def _(gp, ct=ct, half=half, s=s, bp=bp, o0=o0):
                    ctx.wait(gp, "c_bc_built")
                    ctx.wait(gp, f"c_bcuse_{bp}")
                    gp.dma_start(out=bcb[bp][:, 0:H],
                                 in_=bc_dr.ap()[s * 128:(s + 1) * 128, o0:o0 + H]).then_inc(ctx.sem["SD" + str(bp)], 16)
                    ctx.bump("SD" + str(bp), 16)
                    gp.dma_start(out=bcb[bp][:, H:2 * H],
                                 in_=bc_dr.ap()[(16 + s) * 128:(17 + s) * 128, o0:o0 + H]).then_inc(ctx.sem["SD" + str(bp)], 16)
                    ctx.bump("SD" + str(bp), 16, tag=f"c_bcld_{bp}")

                @S_("vector")
                def _(vector, ct=ct, s=s, o0=o0):
                    ctx.wait(vector, "c_acol")
                    ctx.wait(vector, "c_du")
                    ctx.wait(vector, "c_yz")
                    ctx.wait(vector, "c_dAe")
                    ctx.wait_cnt(vector, "VE")
                    ctx.inc(vector.tensor_scalar(accf[:], dl_sl[:, o0:o0 + H],
                                                 acol[:, ct * 16 + s:ct * 16 + s + 1],
                                                 None, ALU.mult), "VE", tag="c_dAa")

                @S_("scalar")
                def _(scalar, s=s):
                    ctx.wait(scalar, "c_dAa")
                    ctx.wait(scalar, "c_scan")
                    ctx.inc(scalar.activation(accf[:], accf[:], AF.Exp), "AC", tag="c_dAe")

                @S_("vector")
                def _(vector, ct=ct, half=half, s=s, bp=bp, o0=o0):
                    ctx.wait(vector, f"c_bcld_{bp}")
                    ctx.wait_cnt(vector, "VE")
                    ctx.inc(vector.tensor_tensor(b_sl, X_sl[:, o0:o0 + H],
                                                 bcb[bp][:, 0:H], ALU.mult), "VE")
                    ctx.wait(vector, "c_dAe")
                    ctx.wait_cnt(vector, "VE")
                    init = 0.0 if half == 0 else carry[:, s:s + 1]
                    ctx.inc(vector.tensor_tensor_scan(h_sl, accf[:].bitcast(F32)
                                                      if False else accf[:], b_sl, init,
                                                      ALU.mult, ALU.add), "VE", tag="c_scan")
                    ctx.wait_cnt(vector, "VE")
                    ctx.inc(vector.tensor_copy(carry[:, s:s + 1], h_sl[:, H - 1:H]), "VE")
                    ctx.wait_cnt(vector, "VE")
                    ctx.inc(vector.tensor_tensor(b_sl, h_sl, bcb[bp][:, H:2 * H],
                                                 ALU.mult), "VE")
                    ctx.wait_cnt(vector, "VE")
                    n = ctx.inc(vector.tensor_tensor(yacf[:], yacf[:], b_sl, ALU.add),
                                "VE", tag="c_yadd")
                    ctx.tag[f"c_bcuse_{bp}"] = ("VE", n)

            @S_("vector")
            def _(vector, ct=ct, half=half, o0=o0):
                ctx.wait(vector, "c_yadd")
                ctx.wait_cnt(vector, "VE")
                ctx.inc(vector.scalar_tensor_tensor(b_sl, xc_sl[:, o0:o0 + H],
                                                    dcol[:, ct:ct + 1], yacf[:],
                                                    ALU.mult, ALU.add), "VE",
                        tag=f"c_yo_{ct}_{half}")

            @S_("sync")
            def _(sync, ct=ct, half=half):
                ctx.wait(sync, f"c_yo_{ct}_{half}")
                for d in range(8):
                    srcv = b_sl.rearrange("p (i dd tau) -> p dd i tau", dd=8, tau=128)[:, d]
                    dstv = a2a_yi.ap()[d * 512 + ct * 128:d * 512 + (ct + 1) * 128,
                                       half * 512:(half + 1) * 512].rearrange(
                        "p (i tau) -> p i tau", tau=128)
                    sync.dma_start(out=dstv, in_=srcv).then_inc(ctx.sem["ST"], 16)
                    ctx.bump("ST", 16)
                ctx.tag["c_yst"] = ("ST", ctx.cnt["ST"])
        ctx.tag[f"c_Xfree_{ct}"] = ctx.tag["c_scan"]
        ctx.tag[f"c_xcfree_{ct}"] = ctx.tag[f"c_yo_{ct}_1"]

    @S_("gpsimd")
    def _(gp):
        ctx.wait(gp, "c_yst")
        with allow_ici_all_to_all():
            gp.collective_compute("AllToAll", ALU.bypass, replica_groups=[list(range(8))],
                                  ins=[a2a_yi.ap().opt()], outs=[a2a_yo.ap().opt()]).then_inc(ctx.sem["CC"])
        ctx.bump("CC", 1, tag="c_a2a_y")
        gp.wait_ge(ctx.sem["CC"], ctx.cnt["CC"])

    if "dbg_yi" in g:
        @S_("gpsimd")
        def _(gp):
            ctx.wait(gp, "c_a2a_y")
            gp.dma_start(out=g["dbg_yi"].ap(), in_=a2a_yi.ap()).then_inc(ctx.sem["LDG"], 16)
            ctx.bump("LDG", 16)
            gp.dma_start(out=g["dbg_ar"].ap(), in_=ar_xo.ap()).then_inc(ctx.sem["LDG"], 16)
            ctx.bump("LDG", 16)
            gp.wait_ge(ctx.sem["LDG"], ctx.cnt["LDG"])

    for zh in range(2):
        for zc in range(16):
            ch = zh * 16 + zc

            @S_("sync")
            def _(sync, ch=ch, zc=zc, zh=zh):
                if zh == 0 and zc == 0:
                    ctx.wait(sync, "c_a2a_y")
                if zc == 0:
                    ctx.wait(sync, f"c_ymused_{zh-1}")
                ctx.wait(sync, "c_zs")
                sync.dma_start(out=zy_sl[:, 0:1024], in_=z_dram.ap()[ch * 128:(ch + 1) * 128, :]).then_inc(ctx.sem["LD"], 16)
                ctx.bump("LD", 16)
                sync.dma_start(out=zy_sl[:, 1024:2048], in_=a2a_yo.ap()[ch * 128:(ch + 1) * 128, :]).then_inc(ctx.sem["LD"], 16)
                ctx.bump("LD", 16, tag="c_zyld")

            @S_("scalar")
            def _(scalar, zc=zc):
                ctx.wait(scalar, "c_zyld")
                ctx.wait(scalar, "c_zs_used")
                ctx.inc(scalar.activation(evb[:, 0:1024], zy_sl[:, 0:1024], AF.Silu),
                        "AC", tag="c_zs")

            @S_("vector")
            def _(vector, zc=zc, zh=zh):
                ctx.wait(vector, "c_zs")
                ctx.wait_cnt(vector, "VE")
                n = ctx.inc(vector.tensor_tensor(ym[:, zc * 1024:(zc + 1) * 1024],
                                                 zy_sl[:, 1024:2048], evb[:, 0:1024],
                                                 ALU.mult), "VE", tag="c_ym")
                ctx.tag["c_zs_used"] = ("VE", n)
        ctx.tag[f"c_ym_{zh}"] = ctx.tag["c_ym"]

        for oc in range(16):
            par = "WL" + str(oc % 2)
            wbuf = wo_b[oc % 2]

            @S_("gpsimd")
            def _(gp, oc=oc, zh=zh, par=par, wbuf=wbuf):
                ctx.wait(gp, f"c_ocons_{par}")
                srcw = WopT[zh * 2048:(zh + 1) * 2048, oc * 128:(oc + 1) * 128]
                gp.dma_start(out=wbuf[:].rearrange("p (c o) -> p c o", o=128),
                             in_=srcw.rearrange("(c p) o -> p c o", p=128)).then_inc(ctx.sem[par], 16)
                ctx.bump(par, 16, tag=f"c_old_{zh}_{oc}")

            @S_("tensor")
            def _(tensor, oc=oc, zh=zh, par=par, wbuf=wbuf):
                ctx.wait(tensor, f"c_old_{zh}_{oc}")
                ctx.wait(tensor, f"c_ym_{zh}")
                ctx.wait(tensor, f"c_opev_0")
                ctx.wait(tensor, f"c_opev_1")
                for cc in range(16):
                    for tb in range(2):
                        mm = tensor.matmul(ps_b[:, tb * 512:(tb + 1) * 512],
                                           wbuf[:, cc * 128:(cc + 1) * 128],
                                           ym[:, cc * 1024 + tb * 512:cc * 1024 + (tb + 1) * 512],
                                           start=(cc == 0), stop=(cc == 15))
                n = ctx.inc(mm, "PE", tag=f"c_opmm_{oc % 2}")
                ctx.tag[f"c_ocons_{par}"] = ("PE", n)
                if oc == 15:
                    ctx.tag[f"c_ymused_{zh}"] = ("PE", n)

            @S_("vector")
            def _(vector, oc=oc, zh=zh):
                ctx.wait(vector, f"c_opmm_{oc % 2}")
                ctx.wait_cnt(vector, "VE")
                if zh == 0:
                    ctx.inc(vector.tensor_copy(opacc[:, oc * 1024:(oc + 1) * 1024],
                                               ps_b[:]), "VE", tag=f"c_opev_{oc % 2}")
                else:
                    ctx.inc(vector.tensor_tensor(opacc[:, oc * 1024:(oc + 1) * 1024],
                                                 opacc[:, oc * 1024:(oc + 1) * 1024],
                                                 ps_b[:], ALU.add), "VE",
                            tag=f"c_opev_{oc % 2}")
    ctx.tag["c_opdone"] = ("VE", ctx.cnt["VE"])

    for i in range(NT):
        @S_("tensor")
        def _(tensor, i=i):
            ctx.wait(tensor, "c_opdone")
            ctx.wait(tensor, "c_ottev")
            for fh in range(2):
                for f8 in range(8):
                    f = fh * 8 + f8
                    mm = tensor.transpose(
                        ps_t[:, fh * 1024 + f8 * 128:fh * 1024 + (f8 + 1) * 128],
                        opacc[:, f * 1024 + i * 128:f * 1024 + (i + 1) * 128],
                        ident[:])
                ctx.inc(mm, "PE", tag=f"c_ot_{fh}")

        @S_("sync")
        def _(sync, i=i):
            ctx.wait(sync, "c_hld_used")
            sync.dma_start(out=xnbf[:], in_=xn_dram.ap()[i * 128:(i + 1) * 128, :]).then_inc(ctx.sem["LD"], 16)
            ctx.bump("LD", 16, tag="c_hld")

        @S_("vector")
        def _(vector, i=i):
            ctx.wait(vector, "c_hld")
            for fh in range(2):
                ctx.wait(vector, f"c_ot_{fh}")
                ctx.wait_cnt(vector, "VE")
                ctx.inc(vector.tensor_tensor(dn_tm[:, fh * 1024:(fh + 1) * 1024],
                                             ps_t[:, fh * 1024:(fh + 1) * 1024],
                                             xnbf[:, fh * 1024:(fh + 1) * 1024],
                                             ALU.add), "VE", tag="c_ottev")
            ctx.tag["c_hld_used"] = ("VE", ctx.cnt["VE"])

        emit_post_ln_add(ctx, g, dn_tm, xt, nm, rs, t0, sqscr, xnbf, i,
                         xout_ap, tag=f"cp_{i}", dn_ready_tag="c_ottev")


def emit_phase_D(ctx, g, xin_ap, xout_ap):
    S_ = ctx.sec
    nc = ctx.nc
    xsp = g["xsp"]
    ident = g["ident_sb"]
    H128 = g["H128_sb"]
    sdram = g["sdram"]
    src = xin_ap if xin_ap is not None else xsp

    P = g["phase_stack"].enter_context
    xt = P(nc.sbuf_tensor("d_xt", [128, DM], F32))
    xnbf = P(nc.sbuf_tensor("d_xnbf", [128, DM], BF16))
    sqscr = P(nc.sbuf_tensor("d_sqscr", [128, DM], F32))
    nm = P(nc.sbuf_tensor("d_nm", [128, 1], F32))
    rs = P(nc.sbuf_tensor("d_rs", [128, 1], F32))
    t0 = P(nc.sbuf_tensor("d_t0", [128, 1], F32))
    xq = P(nc.sbuf_tensor("d_xq", [128, 16 * 512], BF16))
    hA = P(nc.sbuf_tensor("d_hA", [128, 64 * 512], BF16))
    hB = P(nc.sbuf_tensor("d_hB", [128, 64 * 512], BF16))
    wq = [P(nc.sbuf_tensor(f"d_wq{j}", [128, 16 * 128], BF16)) for j in range(2)]
    wscr = P(nc.sbuf_tensor("d_wscr", [128, 1024], F32))
    gsil = P(nc.sbuf_tensor("d_gsil", [128, 512], BF16))
    dn_tm = P(nc.sbuf_tensor("d_dntm", [128, DM], BF16))
    qsc = P(nc.sbuf_tensor("d_qsc", [128, 512], F32))
    sbc = P(nc.sbuf_tensor("d_sbc", [128, 512], BF16))
    invbc = P(nc.sbuf_tensor("d_invbc", [128, 512], F32))
    tmax = P(nc.sbuf_tensor("d_tmax", [128, 512], BF16))
    tmin = P(nc.sbuf_tensor("d_tmin", [128, 512], BF16))
    colsc = P(nc.sbuf_tensor("d_colsc", [128, 8], F32))
    colbf = P(nc.sbuf_tensor("d_colbf", [128, 8], BF16))
    rowsc = P(nc.sbuf_tensor("d_rowsc", [1, 512], F32))
    rowbf = P(nc.sbuf_tensor("d_rowbf", [1, 512], BF16))
    swcols = P(nc.sbuf_tensor("d_swcols", [128, 8], F32))
    sacc = P(nc.sbuf_tensor("d_sacc", [128, 2], F32))
    onesb = P(nc.sbuf_tensor("d_onesb", [1, 128], BF16))
    onesf = P(nc.sbuf_tensor("d_onesf", [128, 1], F32))
    ps_t = P(nc.psum_tensor("d_pst", [128, 2048], BF16))
    ps_gu = P(nc.psum_tensor("d_psgu", [128, 1024], F32))
    ps_d = P(nc.psum_tensor("d_psd", [128, 512], F32))
    ps_fw = P(nc.psum_tensor("d_psfw", [128, 1024], F32))
    ps_s = P(nc.psum_tensor("d_pss", [128, 512], F32))

    wdb = hA[:, 8192:16384]
    wqf = [hB[:, 0:4096].bitcast(F32), hB[:, 4096:8192].bitcast(F32)]
    wdf = hA[:, 16384:32768].bitcast(F32)
    dn_fm = hA[:, 0:8192]
    W = {"gt": g["Wg1T"], "up": g["Wu1T"], "dn": g["Wd1T"]}

    @S_("vector")
    def _(vector):
        vector.memset(onesb[:], 1.0)
        ctx.inc(vector.memset(onesf[:], 1.0), "VE", tag="d_ones")

    for wi, wk in enumerate(("gt", "up", "dn")):
        nel = DM * DI

        @S_("vector")
        def _(vector, wi=wi):
            ctx.inc(vector.memset(sacc[:, 0:1], 0.0), "VE", tag=f"d_sz_{wi}")

        for ci in range(16):
            par = "WL" + str(ci % 2)

            @S_("gpsimd")
            def _(gp, wk=wk, ci=ci, wi=wi, par=par):
                ctx.wait(gp, f"d_swuse_{wi}_{ci-2}")
                flat = W[wk].rearrange("r c -> (r c)").rearrange("(n p e) -> n p e",
                                                                 p=128, e=8192)
                dst = hB[:, (ci % 2) * 16384:(ci % 2 + 1) * 16384].bitcast(F32)
                gp.dma_start(out=dst, in_=flat[ci]).then_inc(ctx.sem[par], 16)
                ctx.bump(par, 16, tag=f"d_swld_{wi}_{ci}")

            @S_("vector")
            def _(vector, ci=ci, wi=wi):
                ctx.wait(vector, f"d_swld_{wi}_{ci}")
                ctx.wait(vector, f"d_sz_{wi}")
                ctx.wait_cnt(vector, "VE")
                ctx.inc(vector.tensor_reduce(sacc[:, 1:2],
                                             hB[:, (ci % 2) * 16384:(ci % 2 + 1) * 16384].bitcast(F32),
                                             AX.X, ALU.add, apply_absolute_value=True),
                        "VE", tag=f"d_swuse_{wi}_{ci}")
                ctx.wait_cnt(vector, "VE")
                ctx.inc(vector.tensor_tensor(sacc[:, 0:1], sacc[:, 0:1], sacc[:, 1:2],
                                             ALU.add), "VE", tag=f"d_swsum_{wi}_{ci}")

        @S_("tensor")
        def _(tensor, wi=wi):
            ctx.wait(tensor, f"d_swsum_{wi}_15")
            ctx.wait(tensor, f"d_swbc_done_{wi-1}")
            ctx.wait(tensor, "d_ones")
            ctx.inc(tensor.matmul(ps_s[0:1, 0:1], sacc[:, 0:1], onesf[:],
                                  start=True, stop=True), "PE", tag=f"d_swred_{wi}")

        @S_("vector")
        def _(vector, wi=wi, nel=nel):
            ctx.wait(vector, f"d_swred_{wi}")
            ctx.inc(vector.tensor_scalar(rowsc[0:1, 0:1], ps_s[0:1, 0:1], 1.0 / nel, EPS,
                                         ALU.mult, ALU.max), "VE")
            ctx.wait_cnt(vector, "VE")
            ctx.inc(vector.reciprocal(rowsc[0:1, 1:2], rowsc[0:1, 0:1]), "VE",
                    tag=f"d_swval_{wi}")

        @S_("sync")
        def _(sync, wi=wi):
            ctx.wait(sync, f"d_swval_{wi}")
            sync.dma_start(out=sdram[0:1, 2 * wi:2 * wi + 2], in_=rowsc[0:1, 0:2]).then_inc(ctx.sem["ST"], 16)
            ctx.bump("ST", 16)
            sync.wait_ge(ctx.sem["ST"], ctx.cnt["ST"])
            bsrc = sdram[0:1, 2 * wi:2 * wi + 2].broadcast_to([128, 2])
            sync.dma_start(out=swcols[:, 2 * wi:2 * wi + 2], in_=bsrc).then_inc(ctx.sem["LD"], 16)
            ctx.bump("LD", 16, tag=f"d_swbc_done_{wi}")

    @S_("vector")
    def _(vector):
        for wi in range(3):
            ctx.wait(vector, f"d_swbc_done_{wi}")

    ctx.tag["d_parcons_WL0"] = ("VE", ctx.cnt["VE"])
    ctx.tag["d_parcons_WL1"] = ("VE", ctx.cnt["VE"])

    def swcol(wi):
        return swcols[:, 2 * wi:2 * wi + 1]

    def swinv(wi):
        return swcols[:, 2 * wi + 1:2 * wi + 2]

    def ternarize(srcf_ap, dst_ap, ncols, wi, tg):
        np_ = (ncols + 1023) // 1024
        @S_("vector")
        def _(vector):
            for pp in range(np_):
                c0, c1 = pp * 1024, min(ncols, (pp + 1) * 1024)
                ctx.wait_cnt(vector, "VE")
                ctx.inc(vector.tensor_scalar(wscr[:, 0:c1 - c0], srcf_ap[:, c0:c1],
                                             swinv(wi), RND, ALU.mult, ALU.add), "VE")
                ctx.wait_cnt(vector, "VE")
                ctx.inc(vector.tensor_scalar(wscr[:, 0:c1 - c0], wscr[:, 0:c1 - c0],
                                             RND, -1.0, ALU.subtract, ALU.max), "VE")
                ctx.wait_cnt(vector, "VE")
                ctx.inc(vector.tensor_scalar(dst_ap[:, c0:c1], wscr[:, 0:c1 - c0],
                                             1.0, None, ALU.min), "VE", tag=tg)

    def quantize(Xbuf, nch, lam, tg):
        @S_("vector")
        def _(vector):
            ctx.wait_cnt(vector, "VE")
            vector.tensor_copy(tmax[:], Xbuf[:, 0:512])
            ctx.inc(vector.tensor_copy(tmin[:], Xbuf[:, 0:512]), "VE")
            for ch in range(1, nch):
                ctx.wait_cnt(vector, "VE")
                vector.tensor_tensor(tmax[:], tmax[:], Xbuf[:, ch * 512:(ch + 1) * 512],
                                     ALU.max)
                ctx.inc(vector.tensor_tensor(tmin[:], tmin[:], Xbuf[:, ch * 512:(ch + 1) * 512],
                                             ALU.min), "VE")
            ctx.tag[tg + "_tmax"] = ("VE", ctx.cnt["VE"])

        @S_("tensor")
        def _(tensor):
            ctx.wait(tensor, tg + "_tmax")
            ctx.wait(tensor, tg + "_mxev_prev")
            for st in range(4):
                tensor.transpose(ps_t[:, st * 256:st * 256 + 128],
                                 tmax[:, st * 128:(st + 1) * 128], ident[:])
                ctx.inc(tensor.transpose(ps_t[:, st * 256 + 128:st * 256 + 256],
                                         tmin[:, st * 128:(st + 1) * 128], ident[:]),
                        "PE", tag=tg + f"_mxt_{st}")

        @S_("vector")
        def _(vector):
            ctx.wait(vector, tg + "_mxt_3")
            for st in range(4):
                ctx.inc(vector.tensor_reduce(colsc[:, st:st + 1],
                                             ps_t[:, st * 256:(st + 1) * 256],
                                             AX.X, ALU.max, apply_absolute_value=True),
                        "VE")
            ctx.tag[tg + "_mxev_prev"] = ("VE", ctx.cnt["VE"])
            for st in range(4):
                ctx.wait_cnt(vector, "VE")
                ctx.inc(vector.tensor_scalar(colbf[:, st:st + 1], colsc[:, st:st + 1],
                                             EPS / lam, None, ALU.max), "VE",
                        tag=tg + f"_mxbf_{st}")

        @S_("tensor")
        def _(tensor):
            ctx.wait(tensor, tg + "_mxbf_3")
            ctx.wait(tensor, tg + "_rowev_prev")
            for st in range(4):
                ctx.inc(tensor.matmul(ps_s[0:1, st * 128:(st + 1) * 128],
                                      colbf[:, st:st + 1],
                                      ident[:], start=True, stop=True), "PE",
                        tag=tg + f"_mxrow_{st}")

        @S_("vector")
        def _(vector):
            ctx.wait(vector, tg + "_mxrow_3")
            ctx.inc(vector.tensor_copy(rowbf[0:1, 0:512], ps_s[0:1, 0:512]), "VE",
                    tag=tg + "_rowev_prev")

        @S_("tensor")
        def _(tensor):
            ctx.wait(tensor, tg + "_rowev_prev")
            ctx.wait(tensor, tg + "_bcev_prev")
            ctx.inc(tensor.matmul(ps_s[:, 0:512], onesb[:], rowbf[0:1, 0:512],
                                  start=True, stop=True), "PE", tag=tg + "_bc")

        @S_("vector")
        def _(vector, lam=lam):
            ctx.wait(vector, tg + "_bc")
            ctx.inc(vector.tensor_copy(qsc[:], ps_s[:, 0:512]), "VE",
                    tag=tg + "_bcev_prev")
            ctx.wait_cnt(vector, "VE")
            ctx.inc(vector.reciprocal(invbc[:], qsc[:]), "VE")
            ctx.wait_cnt(vector, "VE")
            ctx.inc(vector.tensor_scalar(sbc[:], invbc[:], 127.0, None, ALU.mult), "VE")
            ctx.wait_cnt(vector, "VE")
            ctx.inc(vector.tensor_scalar(invbc[:], qsc[:], lam / 127.0, None, ALU.mult),
                    "VE", tag=tg + "_sc")

        @S_("vector")
        def _(vector):
            ctx.wait(vector, tg + "_sc")
            for ch in range(nch):
                xap = Xbuf[:, ch * 512:(ch + 1) * 512]
                ctx.wait_cnt(vector, "VE")
                ctx.inc(vector.tensor_tensor(qsc[:], xap, sbc[:], ALU.mult), "VE")
                ctx.wait_cnt(vector, "VE")
                ctx.inc(vector.tensor_scalar(qsc[:], qsc[:], RND, RND, ALU.add,
                                             ALU.subtract), "VE")
                ctx.wait_cnt(vector, "VE")
                ctx.inc(vector.tensor_scalar(qsc[:], qsc[:], -128.0, 127.0, ALU.max,
                                             ALU.min), "VE")
                ctx.wait_cnt(vector, "VE")
                ctx.inc(vector.tensor_tensor(xap, qsc[:], invbc[:], ALU.mult), "VE")
            ctx.tag[tg + "_done"] = ("VE", ctx.cnt["VE"])

    def fwht(Xsrc, Xdst, nch, tg):
        nst = int(math.log2(nch))
        ctx.tag[tg + "_fwev_-2"] = ctx.tag.get("d_psfw_last", ("VE", 0))
        ctx.tag[tg + "_fwev_-1"] = ctx.tag.get("d_psfw_last", ("VE", 0))
        for ch in range(nch):
            bank = ch % 2

            @S_("tensor")
            def _(tensor, ch=ch, bank=bank):
                if ch == 0:
                    ctx.wait(tensor, tg + "_src_ready")
                ctx.wait(tensor, tg + f"_fwev_{ch-2}")
                ctx.inc(tensor.matmul(ps_fw[:, bank * 512:(bank + 1) * 512], H128[:],
                                      Xsrc[:, ch * 512:(ch + 1) * 512],
                                      start=True, stop=True), "PE", tag=tg + f"_fwmm_{ch}")

            @S_("vector")
            def _(vector, ch=ch, bank=bank):
                ctx.wait(vector, tg + f"_fwmm_{ch}")
                ctx.inc(vector.tensor_copy(Xdst[:, ch * 512:(ch + 1) * 512],
                                           ps_fw[:, bank * 512:(bank + 1) * 512]), "VE",
                        tag=tg + f"_fwev_{ch}")
        ctx.tag[tg + "_stA"] = ctx.tag[tg + f"_fwev_{nch-1}"]
        ctx.tag["d_psfw_last"] = ctx.tag[tg + "_stA"]

        bufs = [Xdst, Xsrc]
        for st in range(nst):
            d = 1 << st
            src_b = bufs[st % 2]
            dst_b = bufs[(st + 1) % 2]
            gcount = nch // (2 * d)

            @S_("vector")
            def _(vector, d=d, src_b=src_b, dst_b=dst_b, gcount=gcount, st=st):
                ctx.wait(vector, tg + ("_stA" if st == 0 else f"_st{st-1}"))
                sv = src_b[:].rearrange("p (g two dd e) -> p g two dd e", two=2, dd=d,
                                        e=512)[:, 0:gcount]
                dv = dst_b[:].rearrange("p (g two dd e) -> p g two dd e", two=2, dd=d,
                                        e=512)[:, 0:gcount]
                ctx.inc(vector.tensor_tensor(dv[:, :, 0], sv[:, :, 0], sv[:, :, 1],
                                             ALU.add), "VE")
                ctx.inc(vector.tensor_tensor(dv[:, :, 1], sv[:, :, 0], sv[:, :, 1],
                                             ALU.subtract), "VE", tag=tg + f"_st{st}")

        assert nst % 2 == 0, "result must land back in Xdst"
        ctx.tag[tg + "_done"] = ctx.tag[tg + f"_st{nst-1}"]

    for hf in range(2):
        base = hf * 4

        for tt in range(4):
            i = base + tt

            @S_("sync")
            def _(sync, i=i):
                ctx.wait_cnt(sync, "ST")
                ctx.wait(sync, f"d_xn_{i-1}")
                sync.dma_start(out=xt[:], in_=src[i * 128:(i + 1) * 128, :]).then_inc(ctx.sem["LD"], 16)
                ctx.bump("LD", 16, tag=f"d_ld_{i}")
                if xin_ap is not None:
                    sync.wait_ge(ctx.sem["LD"], ctx.cnt["LD"])
                    sync.dma_start(out=xsp[i * 128:(i + 1) * 128, :], in_=xt[:]).then_inc(ctx.sem["ST"], 16)
                    ctx.bump("ST", 16, tag=f"d_sp_{i}")

            @S_("vector")
            def _(vector, i=i):
                ctx.wait(vector, f"d_ld_{i}")

            @S_("scalar")
            def _(scalar, i=i):
                ctx.wait(scalar, f"d_ld_{i}")

            _ln_sections(ctx, g, xt[:], nm, rs, t0, sqscr, DM, f"d_ln_{i}")

            @S_("vector")
            def _(vector, i=i):
                ctx.wait(vector, f"d_ln_{i}_rs")
                ctx.wait(vector, f"d_tr_{i-1}")
                ctx.inc(vector.tensor_scalar(xnbf[:], xt[:], nm[:], rs[:], ALU.add,
                                             ALU.mult), "VE", tag=f"d_xn_{i}")

            @S_("tensor")
            def _(tensor, i=i, tt=tt, hf=hf):
                ctx.wait(tensor, f"d_xn_{i}")
                ctx.wait(tensor, f"d_trev_{i-1}")
                if tt == 0:
                    ctx.wait(tensor, f"d_xqfree_{hf}")
                for fh in range(2):
                    for f8 in range(8):
                        f = fh * 8 + f8
                        mm = tensor.transpose(
                            ps_t[:, fh * 1024 + f8 * 128:fh * 1024 + (f8 + 1) * 128],
                            xnbf[:, f * 128:(f + 1) * 128], ident[:])
                    ctx.inc(mm, "PE", tag=f"d_trp_{i}_{fh}")
                ctx.tag[f"d_tr_{i}"] = ctx.tag[f"d_trp_{i}_1"]

            @S_("vector")
            def _(vector, i=i, tt=tt):
                for fh in range(2):
                    ctx.wait(vector, f"d_trp_{i}_{fh}")
                    dst = xq[:].rearrange("p (f t) -> p f t", t=512)[:, fh * 8:(fh + 1) * 8,
                                                                    tt * 128:(tt + 1) * 128]
                    srcv = ps_t[:, fh * 1024:(fh + 1) * 1024].rearrange("p (f t) -> p f t",
                                                                       t=128)
                    ctx.inc(vector.tensor_copy(dst, srcv), "VE", tag=f"d_trev_{i}")
        ctx.tag[f"d_xq_ready_{hf}"] = ctx.tag[f"d_trev_{base+3}"]

        x2A = hB[:, 0:8192]
        ctx.tag[f"dfw2_{hf}_src_ready"] = ctx.tag[f"d_xq_ready_{hf}"]
        fwht(xq, x2A, 16, f"dfw2_{hf}")

        ctx.tag[f"dq2_{hf}_mxev_prev"] = ctx.tag.get("d_mxev_last", ("VE", 0))
        ctx.tag[f"dq2_{hf}_rowev_prev"] = ctx.tag.get("d_rowev_last", ("VE", 0))
        ctx.tag[f"dq2_{hf}_bcev_prev"] = ctx.tag.get("d_bcev_last", ("VE", 0))
        quantize(x2A, 16, LAM11, f"dq2_{hf}")
        ctx.tag["d_mxev_last"] = ctx.tag[f"dq2_{hf}_mxev_prev"]
        ctx.tag["d_rowev_last"] = ctx.tag[f"dq2_{hf}_rowev_prev"]
        ctx.tag["d_bcev_last"] = ctx.tag[f"dq2_{hf}_bcev_prev"]

        @S_("vector")
        def _(vector, hf=hf, x2A=x2A):
            ctx.wait(vector, f"dq2_{hf}_done")
            ctx.wait(vector, f"d_xq_ready_{hf}")
            ctx.inc(vector.tensor_copy(xq[:], x2A), "VE", tag=f"d_xqs_{hf}")

        ctx.tag["d_parcons_WL0"] = ctx.tag[f"d_xqs_{hf}"]
        ctx.tag["d_parcons_WL1"] = ctx.tag[f"d_xqs_{hf}"]
        for oc in range(64):
            for gu in range(2):
                wi = gu
                wbuf = wq[(oc * 2 + gu) % 2]
                par = "WL" + str((oc * 2 + gu) % 2)
                wt = W["gt"] if gu == 0 else W["up"]

                wfbuf = wqf[(oc * 2 + gu) % 2]

                @S_("gpsimd")
                def _(gp, wfbuf=wfbuf, oc=oc, gu=gu, hf=hf, par=par, wt=wt):
                    ctx.wait(gp, f"d_parcons_{par}")
                    srcw = wt[:, oc * 128:(oc + 1) * 128].rearrange("(c p) o -> p c o",
                                                                    p=128)
                    gp.dma_start(out=wfbuf.rearrange("p (c o) -> p c o", o=128),
                                 in_=srcw).then_inc(ctx.sem[par], 16)
                    ctx.bump(par, 16, tag=f"d_wld_{hf}_{oc}_{gu}")

                @S_("vector")
                def _(vector, hf=hf, oc=oc, gu=gu):
                    ctx.wait(vector, f"d_wld_{hf}_{oc}_{gu}")
                    ctx.wait(vector, f"d_tern_pe_{par}")

                ternarize(wfbuf, wbuf[:], 2048, gu, f"d_tern_{hf}_{oc}_{gu}")

                @S_("tensor")
                def _(tensor, wbuf=wbuf, oc=oc, gu=gu, hf=hf, par=par):
                    ctx.wait(tensor, f"d_tern_{hf}_{oc}_{gu}")
                    if oc == 0 and gu == 0:
                        ctx.wait(tensor, f"d_xqs_{hf}")
                    ctx.wait(tensor, "d_silu_used" if gu == 0 else "d_h_written")
                    for fc in range(16):
                        mm = tensor.matmul(ps_gu[:, gu * 512:gu * 512 + 512],
                                           wbuf[:, fc * 128:(fc + 1) * 128],
                                           xq[:, fc * 512:(fc + 1) * 512],
                                           start=(fc == 0), stop=(fc == 15))
                    n = ctx.inc(mm, "PE", tag=f"d_gu_{gu}")
                    ctx.tag[f"d_parcons_{par}"] = ("PE", n)
                    ctx.tag[f"d_tern_pe_{par}"] = ("PE", n)

                if gu == 0:
                    @S_("scalar")
                    def _(scalar):
                        ctx.wait(scalar, "d_gu_0")
                        ctx.wait(scalar, "d_silu_read")
                        ctx.inc(scalar.activation(gsil[:], ps_gu[:, 0:512], AF.Silu,
                                                  scale=swcol(0)), "AC",
                                tag="d_silu_used")
                else:
                    @S_("vector")
                    def _(vector, oc=oc):
                        ctx.wait(vector, "d_gu_1")
                        ctx.wait(vector, "d_silu_used")
                        ctx.wait(vector, f"d_hfree_{hf}")
                        n = ctx.inc(vector.scalar_tensor_tensor(
                            hA[:, oc * 512:(oc + 1) * 512], ps_gu[:, 512:1024],
                            swcol(1), gsil[:], ALU.mult, ALU.mult), "VE",
                            tag="d_h_written")
                        ctx.tag["d_silu_read"] = ("VE", n)
        ctx.tag[f"d_xqfree_{hf+1}"] = ctx.tag["d_gu_1"]
        ctx.tag[f"d_h_ready_{hf}"] = ctx.tag["d_h_written"]

        ctx.tag[f"dfw8_{hf}_src_ready"] = ctx.tag[f"d_h_ready_{hf}"]
        fwht(hA, hB, 64, f"dfw8_{hf}")
        ctx.tag[f"dq8_{hf}_mxev_prev"] = ctx.tag["d_mxev_last"]
        ctx.tag[f"dq8_{hf}_rowev_prev"] = ctx.tag["d_rowev_last"]
        ctx.tag[f"dq8_{hf}_bcev_prev"] = ctx.tag["d_bcev_last"]
        quantize(hB, 64, LAM12, f"dq8_{hf}")
        ctx.tag["d_mxev_last"] = ctx.tag[f"dq8_{hf}_mxev_prev"]
        ctx.tag["d_rowev_last"] = ctx.tag[f"dq8_{hf}_rowev_prev"]
        ctx.tag["d_bcev_last"] = ctx.tag[f"dq8_{hf}_bcev_prev"]

        ctx.tag["d_dn_pe"] = ctx.tag[f"dfw8_{hf}_st5"]
        for o2 in range(16):
            @S_("gpsimd")
            def _(gp, o2=o2, hf=hf):
                ctx.wait(gp, "d_dn_pe")
                srcw = W["dn"][:, o2 * 128:(o2 + 1) * 128].rearrange("(c p) o -> p c o",
                                                                     p=128)
                gp.dma_start(out=wdf.rearrange("p (c o) -> p c o", o=128),
                             in_=srcw).then_inc(ctx.sem["WL0"], 16)
                ctx.bump("WL0", 16, tag=f"d_wdld_{hf}_{o2}")

            @S_("vector")
            def _(vector, hf=hf, o2=o2):
                ctx.wait(vector, f"d_wdld_{hf}_{o2}")
                ctx.wait(vector, "d_dn_pe")

            ternarize(wdf, wdb, 8192, 2, f"d_dtern_{hf}_{o2}")

            @S_("tensor")
            def _(tensor, o2=o2, hf=hf):
                ctx.wait(tensor, f"d_dtern_{hf}_{o2}")
                if o2 == 0:
                    ctx.wait(tensor, f"dq8_{hf}_done")
                ctx.wait(tensor, "d_dnev")
                for ic in range(64):
                    mm = tensor.matmul(ps_d[:], wdb[:, ic * 128:(ic + 1) * 128],
                                       hB[:, ic * 512:(ic + 1) * 512],
                                       start=(ic == 0), stop=(ic == 63))
                n = ctx.inc(mm, "PE", tag="d_dnmm")
                ctx.tag["d_dn_pe"] = ("PE", n)

            @S_("vector")
            def _(vector, o2=o2):
                ctx.wait(vector, "d_dnmm")
                ctx.wait(vector, "d_dnfm_free")
                ctx.inc(vector.tensor_scalar(dn_fm[:, o2 * 512:(o2 + 1) * 512], ps_d[:],
                                             swcol(2), None, ALU.mult), "VE",
                        tag="d_dnev")
        ctx.tag["d_parcons_WL0"] = ctx.tag["d_dnmm"]

        for tt in range(4):
            i = base + tt

            @S_("tensor")
            def _(tensor, tt=tt, hf=hf):
                ctx.wait(tensor, "d_dnev")
                ctx.wait(tensor, "d_dtev")
                for fh in range(2):
                    for f8 in range(8):
                        f = fh * 8 + f8
                        mm = tensor.transpose(
                            ps_t[:, fh * 1024 + f8 * 128:fh * 1024 + (f8 + 1) * 128],
                            dn_fm[:, f * 512 + tt * 128:f * 512 + (tt + 1) * 128],
                            ident[:])
                    ctx.inc(mm, "PE", tag=f"d_dtp_{fh}")
                ctx.tag["d_dnfm_free"] = ctx.tag["d_dtp_1"]

            @S_("vector")
            def _(vector, tt=tt):
                for fh in range(2):
                    ctx.wait(vector, f"d_dtp_{fh}")
                    ctx.inc(vector.tensor_copy(dn_tm[:, fh * 1024:(fh + 1) * 1024],
                                               ps_t[:, fh * 1024:(fh + 1) * 1024]),
                            "VE", tag="d_dtev")

            emit_post_ln_add(ctx, g, dn_tm, xt, nm, rs, t0, sqscr, xnbf, i,
                             xout_ap, tag=f"dp_{i}", dn_ready_tag="d_dtev")
        ctx.tag[f"d_hfree_{hf+1}"] = ctx.tag["d_dtp_1"]



def build_program(phases=PHASES):
    nc = bass.Bass("TRN2", target_bir_lowering=False)

    ap = {}

    def inp(name, shape, dt=F32):
        ap[name] = nc.dram_tensor(name, list(shape), dt, kind="ExternalInput").ap()

    inp("xin", (TPC, DM))
    inp("ident", (128, 128), BF16)
    if "A" in phases:
        inp("WqT", (DM, DM)); inp("WkT", (DM, 512)); inp("WvT", (DM, 512)); inp("WoT", (DM, DM))
        inp("SD", (72 * 128, 512), BF16)
    if "B" in phases:
        inp("WguT", (DM, 2 * DI)); inp("Wd0T", (DI, DM))
    if "C" in phases:
        CS = DIN // NCORE
        inp("WinT", (DM, EM)); inp("convw", (CS, 4)); inp("convb", (CS, 1))
        inp("WxpT", (CS, 160)); inp("WdtT", (DTR, CS)); inp("dtb", (CS, 1))
        inp("WopT", (DIN, DM)); inp("Alog", (CS, DST)); inp("Dvec", (CS, 1))
        inp("nsc", (DM, 1))
    if "D" in phases:
        inp("Wg1T", (DM, DI)); inp("Wu1T", (DM, DI)); inp("Wd1T", (DI, DM))
        inp("H128", (128, 128), BF16)

    xout = nc.dram_tensor("xout", [TPC, DM], F32, kind="ExternalOutput").ap()
    g = dict(ap)
    g["xsp"] = nc.dram_tensor("xsp", [TPC, DM], F32).ap()
    if "A" in phases:
        g["kag_in"] = nc.dram_tensor("kag_in", [512, TPC], BF16)
        g["kag_out"] = nc.dram_tensor("kag_out", [8 * 512, TPC], BF16)
        g["vag_in"] = nc.dram_tensor("vag_in", [TPC, 512], BF16)
        g["vag_out"] = nc.dram_tensor("vag_out", [8 * TPC, 512], BF16)
    if "C" in phases:
        kindd = "ExternalOutput" if DEBUG_C else "Internal"
        g["a2a_xi"] = nc.dram_tensor("a2a_xi", [DIN, TPC], BF16)
        g["a2a_xo"] = nc.dram_tensor("a2a_xo", [DIN, TPC], BF16)
        g["ar_xi"] = nc.dram_tensor("ar_xi", [160, S], F32)
        g["ar_xo"] = nc.dram_tensor("ar_xo", [160, S], F32)
        g["a2a_yi"] = nc.dram_tensor("a2a_yi", [DIN, TPC], BF16)
        g["a2a_yo"] = nc.dram_tensor("a2a_yo", [DIN, TPC], BF16)
        g["z_dram"] = nc.dram_tensor("z_dram", [DIN, TPC], BF16, kind=kindd)
        g["xn_dram"] = nc.dram_tensor("xn_dram", [TPC, DM], BF16)
        g["xc_dr"] = nc.dram_tensor("xc_dr", [512, S], BF16, kind=kindd)
        g["bc_dr"] = nc.dram_tensor("bc_dr", [32 * 128, S], BF16)
        if DEBUG_C:
            g["dbg_yi"] = nc.dram_tensor("dbg_yi", [DIN, TPC], BF16, kind="ExternalOutput")
            g["dbg_ar"] = nc.dram_tensor("dbg_ar", [160, S], F32, kind="ExternalOutput")

    with contextlib.ExitStack() as stack:
        E = stack.enter_context
        sem = {k: E(nc.semaphore(k)) for k in
               ("LD", "LDG", "PE", "VE", "AC", "PO", "ST", "CC", "WL0", "WL1",
                "KL0", "KL1", "VL0", "VL1", "SD0", "SD1")}
        block = E(nc.Block())
        ctx = Ctx(nc, block, sem)
        ident_sb = E(nc.sbuf_tensor("ident_sb", [128, 128], BF16))
        g["ident_sb"] = ident_sb
        kconst = E(nc.sbuf_tensor("kconst", [128, 8], F32))
        g["cEPS"] = kconst[:, 0:1]
        g["cONE"] = kconst[:, 1:2]

        @block.sync
        def _(sync):
            sync.dma_start(out=ident_sb[:], in_=ap["ident"]).then_inc(sem["LD"], 16)
            ctx.bump("LD", 16)

        @block.vector
        def _(vector):
            vector.memset(kconst[:, 0:1], EPS)
            ctx.inc(vector.memset(kconst[:, 1:2], 1.0), "VE", tag="konst")

        if "D" in phases:
            H128_sb = E(nc.sbuf_tensor("H128_sb", [128, 128], BF16))
            g["H128_sb"] = H128_sb
            g["sdram"] = nc.dram_tensor("sdram", [1, 16], F32).ap()

            @block.sync
            def _(sync):
                sync.dma_start(out=H128_sb[:], in_=ap["H128"]).then_inc(sem["LD"], 16)
                ctx.bump("LD", 16)

        emitters = {"A": emit_phase_A, "B": emit_phase_B, "C": emit_phase_C,
                    "D": emit_phase_D}
        for pi, ph in enumerate(phases):
            with contextlib.ExitStack() as phase_stack:
                g["phase_stack"] = phase_stack
                xin_ap = ap["xin"] if pi == 0 else None
                xout_ap = xout if pi == len(phases) - 1 else None
                if pi > 0:
                    for eng in ("sync", "vector", "scalar", "tensor", "gpsimd"):
                        @getattr(block, eng)
                        def _(e):
                            for k in ("LD", "LDG", "PE", "VE", "AC", "ST"):
                                e.wait_ge(sem[k], ctx.cnt[k])
                emitters[ph](ctx, g, xin_ap, xout_ap)

        @block.sync
        def _(sync):
            sync.wait_ge(sem["ST"], ctx.cnt["ST"])

    return nc



def _hadamard(n):
    H = np.array([[1.0]])
    while H.shape[0] < n:
        H = np.block([[H, H], [H, -H]])
    return H


def _sd_table(c):
    out = np.zeros((72, 128, 512), np.float32)
    idx = 0
    for i in range(NT):
        q0 = (8 * i + c) * 128
        q = q0 + np.arange(128)[:, None]
        for kc in range(2 * (i + 1)):
            k = kc * 512 + np.arange(512)[None, :]
            d = q - k
            sd = np.sqrt(np.maximum(d, 0).astype(np.float64)).astype(np.float32)
            sd = np.where(d < 0, np.float32(1e30), sd)
            out[idx] = sd
            idx += 1
    return out.reshape(72 * 128, 512).astype(ml_dtypes.bfloat16)


def host_inputs(inputs, phases):
    x = np.ascontiguousarray(np.asarray(inputs["L0_attn_x"], dtype=np.float32).reshape(S, DM))
    xt = x.reshape(64, 128, DM)
    base = {"ident": np.eye(128).astype(ml_dtypes.bfloat16)}
    if "A" in phases:
        base["WqT"] = np.ascontiguousarray(np.asarray(inputs["L0_attn_Wq"]).T)
        base["WkT"] = np.ascontiguousarray(np.asarray(inputs["L0_attn_Wk"]).T)
        base["WvT"] = np.ascontiguousarray(np.asarray(inputs["L0_attn_Wv"]).T)
        base["WoT"] = np.ascontiguousarray(np.asarray(inputs["L0_attn_Wo"]).T)
    if "B" in phases:
        base["WguT"] = np.ascontiguousarray(np.asarray(inputs["L0_ffn_w_gate_up"]).T)
        base["Wd0T"] = np.ascontiguousarray(np.asarray(inputs["L0_ffn_w_down"]).T)
    if "C" in phases:
        base["WinT"] = np.ascontiguousarray(np.asarray(inputs["L1_attn_in_proj_w"]).T)
        base["WopT"] = np.ascontiguousarray(np.asarray(inputs["L1_attn_out_proj_w"]).T)
        base["nsc"] = np.ascontiguousarray(np.asarray(inputs["L1_attn_norm_scale"]).reshape(DM, 1))
    if "D" in phases:
        base["Wg1T"] = np.ascontiguousarray(np.asarray(inputs["L1_ffn_w_gate"]).T)
        base["Wu1T"] = np.ascontiguousarray(np.asarray(inputs["L1_ffn_w_up"]).T)
        base["Wd1T"] = np.ascontiguousarray(np.asarray(inputs["L1_ffn_w_down"]).T)
        base["H128"] = _hadamard(128).astype(ml_dtypes.bfloat16)
    per = []
    for c in range(NCORE):
        m = dict(base)
        m["xin"] = np.ascontiguousarray(xt[c::8].reshape(TPC, DM))
        if "A" in phases:
            m["SD"] = _sd_table(c)
        if "C" in phases:
            CS = DIN // NCORE
            sl = slice(c * CS, (c + 1) * CS)
            m["convw"] = np.ascontiguousarray(np.asarray(inputs["L1_attn_conv_w"]).reshape(DIN, 4)[sl])
            m["convb"] = np.ascontiguousarray(np.asarray(inputs["L1_attn_conv_b"]).reshape(DIN, 1)[sl])
            m["WxpT"] = np.ascontiguousarray(np.asarray(inputs["L1_attn_x_proj_w"]).T[sl])
            m["WdtT"] = np.ascontiguousarray(np.asarray(inputs["L1_attn_dt_proj_w"]).T[:, sl])
            m["dtb"] = np.ascontiguousarray(np.asarray(inputs["L1_attn_dt_proj_b"]).reshape(DIN, 1)[sl])
            m["Alog"] = np.ascontiguousarray(np.asarray(inputs["L1_attn_A_log"])[sl])
            m["Dvec"] = np.ascontiguousarray(np.asarray(inputs["L1_attn_D"]).reshape(DIN, 1)[sl])
        per.append(m)
    return per


def assemble(results, key="xout"):
    full = np.zeros((64, 128, DM), np.float32)
    for c in range(NCORE):
        full[c::8] = np.asarray(results[c][key], dtype=np.float32).reshape(NT, 128, DM)
    return full.reshape(1, S, DM)


def kernel(**inputs):
    if FUSED:
        nc = build_program(PHASES)
        in_maps = host_inputs(inputs, PHASES)
        res = run_bass_kernel_spmd(nc, in_maps, core_ids=list(range(NCORE)))
        return _cast_out(assemble(res.results))
    x = None
    for ph in PHASES:
        nc = build_program((ph,))
        in_maps = host_inputs(inputs, (ph,))
        if x is not None:
            xt = x.reshape(64, 128, DM)
            for c in range(NCORE):
                in_maps[c]["xin"] = np.ascontiguousarray(xt[c::8].reshape(TPC, DM))
        res = run_bass_kernel_spmd(nc, in_maps, core_ids=list(range(NCORE)))
        x = assemble(res.results).reshape(S, DM)
    return _cast_out(x.reshape(1, S, DM))


def _cast_out(x):
    return np.ascontiguousarray(x, dtype=np.float32)


# version 04
# speedup vs baseline: 1.0139x; 1.0139x over previous
import contextlib
import math

import numpy as np
import ml_dtypes
import concourse.bass as bass
import concourse.mybir as mybir
from concourse.bass_utils import run_bass_kernel_spmd
from concourse.aot_env import allow_ici_all_to_all

F32 = mybir.dt.float32
BF16 = mybir.dt.bfloat16
AF = mybir.ActivationFunctionType
ALU = mybir.AluOpType
AX = mybir.AxisListType

NCORE = 8
S = 8192
DM = 2048
TPC = 1024
NT = 8
DI = 8192
EM = 8192
DIN = 4096
DST = 16
DTR = 128
HEADS = 16
GRP = 4
HD = 128
RND = 12582912.0
EPS = 1e-5
LAM11 = 1.0 / math.sqrt(2048.0)
LAM12 = 1.0 / math.sqrt(8192.0)

PHASES = ("A", "B", "C", "D")
FUSED = True
DEBUG_C = False



class Ctx:
    def __init__(self, nc, block, sem):
        self.nc = nc
        self.block = block
        self.sem = sem
        self.cnt = {k: 0 for k in sem}
        self.tag = {}
        self.hwm = {}

    def sec(self, eng):
        return getattr(self.block, eng)

    def inc(self, instr, key, n=1, tag=None):
        instr.then_inc(self.sem[key], n)
        self.cnt[key] += n
        if tag is not None:
            self.tag[tag] = (key, self.cnt[key])
        return self.cnt[key]

    def bump(self, key, n, tag=None):
        self.cnt[key] += n
        if tag is not None:
            self.tag[tag] = (key, self.cnt[key])
        return self.cnt[key]

    def _wait_raw(self, eng, key, v):
        ek = (id(eng.engine) if hasattr(eng, "engine") else id(eng), key)
        if self.hwm.get(ek, -1) >= v:
            return
        self.hwm[ek] = v
        eng.wait_ge(self.sem[key], v)

    def wait(self, eng, tag):
        if tag in self.tag:
            key, v = self.tag[tag]
            self._wait_raw(eng, key, v)

    def wait_cnt(self, eng, key):
        self._wait_raw(eng, key, self.cnt[key])



def _ln_sections(ctx, g, src_ap, nm, rs, t0, sqscr, n_feat, tag, rms_only=False):
    S_ = ctx.sec

    @S_("vector")
    def _(vector):
        if not rms_only:
            ctx.inc(vector.tensor_reduce(nm[:], src_ap, AX.X, ALU.add),
                    "VE", tag=tag + "_sum")

    @S_("scalar")
    def _(scalar):
        ctx.inc(scalar.activation(sqscr[:, 0:n_feat], src_ap, AF.Square,
                                  accum_out=t0[:]), "AC", tag=tag + "_sq")

    @S_("vector")
    def _(vector):
        ctx.wait(vector, tag + "_sq")
        if rms_only:
            ctx.inc(vector.tensor_scalar(t0[:], t0[:], 1.0 / n_feat, None, ALU.mult),
                    "VE", tag=tag + "_var")
        else:
            ctx.wait(vector, tag + "_sum")
            ctx.inc(vector.tensor_scalar(nm[:], nm[:], -1.0 / n_feat, None, ALU.mult), "VE")
            ctx.wait_cnt(vector, "VE")
            ctx.inc(vector.tensor_tensor(rs[:], nm[:], nm[:], ALU.mult), "VE")
            ctx.wait_cnt(vector, "VE")
            ctx.inc(vector.scalar_tensor_tensor(t0[:], t0[:], 1.0 / n_feat, rs[:],
                                                ALU.mult, ALU.subtract),
                    "VE", tag=tag + "_var")

    @S_("scalar")
    def _(scalar):
        ctx.wait(scalar, tag + "_var")
        ctx.inc(scalar.activation(t0[:], t0[:], AF.Ln, bias=g["cEPS"]), "AC")
        ctx.wait_cnt(scalar, "AC")
        ctx.inc(scalar.activation(rs[:], t0[:], AF.Exp, scale=-0.5),
                "AC", tag=tag + "_rs")



def emit_phase_B(ctx, g, xin_ap, xout_ap):
    S_ = ctx.sec
    xsp = g["xsp"]
    WguT, Wd0T = g["WguT"], g["Wd0T"]
    ident = g["ident_sb"]
    src = xin_ap if xin_ap is not None else xsp

    P = g["phase_stack"].enter_context
    nc = ctx.nc
    xt = P(nc.sbuf_tensor("b_xt", [128, DM], F32))
    xnbf = P(nc.sbuf_tensor("b_xnbf", [128, DM], BF16))
    sqscr = P(nc.sbuf_tensor("b_sqscr", [128, DM], F32))
    nm = P(nc.sbuf_tensor("b_nm", [128, 1], F32))
    rs = P(nc.sbuf_tensor("b_rs", [128, 1], F32))
    t0 = P(nc.sbuf_tensor("b_t0", [128, 1], F32))
    xn2T = P(nc.sbuf_tensor("b_xn2T", [128, 16 * TPC], BF16))
    wg = [P(nc.sbuf_tensor(f"b_wg{j}", [128, 16 * 128], BF16)) for j in range(2)]
    wd = [P(nc.sbuf_tensor(f"b_wd{j}", [128, 64 * 128], BF16)) for j in range(2)]
    h_sb = P(nc.sbuf_tensor("b_h", [128, 64 * 512], BF16))
    silu_sb = P(nc.sbuf_tensor("b_silu", [128, 512], BF16))
    dn_fm = P(nc.sbuf_tensor("b_dnfm", [128, 16 * 512], BF16))
    dn_tm = P(nc.sbuf_tensor("b_dntm", [128, DM], BF16))
    ps_t = P(nc.psum_tensor("b_pst", [128, 2048], BF16))
    ps_gu = P(nc.psum_tensor("b_psgu", [128, 1024], F32))
    ps_d = P(nc.psum_tensor("b_psd", [128, 512], F32))

    for i in range(NT):
        @S_("sync")
        def _(sync, i=i):
            ctx.wait(sync, f"b_xn_{i-1}")
            ctx.wait(sync, f"b_sp_{i-1}")
            sync.dma_start(out=xt[:], in_=src[i * 128:(i + 1) * 128, :]).then_inc(ctx.sem["LD"], 16)
            ctx.bump("LD", 16, tag=f"b_ld_{i}")
            if xin_ap is not None:
                sync.wait_ge(ctx.sem["LD"], ctx.cnt["LD"])
                sync.dma_start(out=xsp[i * 128:(i + 1) * 128, :], in_=xt[:]).then_inc(ctx.sem["ST"], 16)
                ctx.bump("ST", 16, tag=f"b_sp_{i}")

        @S_("vector")
        def _(vector, i=i):
            ctx.wait(vector, f"b_ld_{i}")

        @S_("scalar")
        def _(scalar, i=i):
            ctx.wait(scalar, f"b_ld_{i}")

        _ln_sections(ctx, g, xt[:], nm, rs, t0, sqscr, DM, f"b_ln_{i}")

        @S_("vector")
        def _(vector, i=i):
            ctx.wait(vector, f"b_ln_{i}_rs")
            ctx.wait(vector, f"b_tr_{i-1}")
            ctx.inc(vector.tensor_scalar(xnbf[:], xt[:], nm[:], rs[:], ALU.add, ALU.mult),
                    "VE", tag=f"b_xn_{i}")

        @S_("tensor")
        def _(tensor, i=i):
            ctx.wait(tensor, f"b_xn_{i}")
            ctx.wait(tensor, f"b_trev_{i-1}")
            for fh in range(2):
                for f8 in range(8):
                    f = fh * 8 + f8
                    mm = tensor.transpose(ps_t[:, fh * 1024 + f8 * 128:fh * 1024 + (f8 + 1) * 128],
                                          xnbf[:, f * 128:(f + 1) * 128], ident[:])
                ctx.inc(mm, "PE", tag=f"b_trp_{i}_{fh}")
            ctx.tag[f"b_tr_{i}"] = ctx.tag[f"b_trp_{i}_1"]

        @S_("vector")
        def _(vector, i=i):
            for fh in range(2):
                ctx.wait(vector, f"b_trp_{i}_{fh}")
                dst = xn2T[:].rearrange("p (f t) -> p f t", t=TPC)[:, fh * 8:(fh + 1) * 8,
                                                                  i * 128:(i + 1) * 128]
                srcv = ps_t[:, fh * 1024:(fh + 1) * 1024].rearrange("p (f t) -> p f t", t=128)
                ctx.inc(vector.tensor_copy(dst, srcv), "VE", tag=f"b_trev_{i}")

    for hf in range(2):
        tok0 = hf * 512

        for oc in range(64):
            for gu in range(2):
                wbuf = wg[(oc * 2 + gu) % 2]
                col0 = gu * DI + oc * 128

                par = "WL" + str((oc * 2 + gu) % 2)

                @S_("gpsimd")
                def _(gp, wbuf=wbuf, col0=col0, oc=oc, gu=gu, hf=hf, par=par):
                    ctx.wait(gp, f"b_parcons_{par}")
                    srcw = WguT[:, col0:col0 + 128].rearrange("(c p) o -> p c o", p=128)
                    gp.dma_start(out=wbuf[:].rearrange("p (c o) -> p c o", o=128),
                                 in_=srcw).then_inc(ctx.sem[par], 16)
                    ctx.bump(par, 16, tag=f"b_wld_{hf}_{oc}_{gu}")

                @S_("tensor")
                def _(tensor, wbuf=wbuf, oc=oc, gu=gu, hf=hf, tok0=tok0, par=par):
                    if hf == 0 and oc == 0 and gu == 0:
                        ctx.wait(tensor, "b_trev_7")
                    ctx.wait(tensor, f"b_wld_{hf}_{oc}_{gu}")
                    ctx.wait(tensor, "b_silu_used" if gu == 0 else "b_h_written")
                    for fc in range(16):
                        mm = tensor.matmul(ps_gu[:, gu * 512:gu * 512 + 512],
                                           wbuf[:, fc * 128:(fc + 1) * 128],
                                           xn2T[:, fc * TPC + tok0:fc * TPC + tok0 + 512],
                                           start=(fc == 0), stop=(fc == 15))
                    ctx.inc(mm, "PE", tag=f"b_wmm_{id(wbuf)}")
                    ctx.tag[f"b_gu_{gu}"] = ctx.tag[f"b_wmm_{id(wbuf)}"]
                    ctx.tag[f"b_parcons_{par}"] = ctx.tag[f"b_wmm_{id(wbuf)}"]

                if gu == 0:
                    @S_("scalar")
                    def _(scalar):
                        ctx.wait(scalar, "b_gu_0")
                        ctx.wait(scalar, "b_silu_read")
                        ctx.inc(scalar.activation(silu_sb[:], ps_gu[:, 0:512], AF.Silu),
                                "AC", tag="b_silu_used")
                else:
                    @S_("vector")
                    def _(vector, oc=oc, hf=hf):
                        ctx.wait(vector, "b_gu_1")
                        ctx.wait(vector, "b_silu_used")
                        ctx.wait(vector, f"b_hWAR_{hf}")
                        n = ctx.inc(vector.tensor_tensor(h_sb[:, oc * 512:(oc + 1) * 512],
                                                         ps_gu[:, 512:1024], silu_sb[:],
                                                         ALU.mult),
                                    "VE", tag="b_h_written")
                        ctx.tag["b_silu_read"] = ("VE", n)

        ctx.tag[f"b_hdone_{hf}"] = ctx.tag["b_h_written"]

        for o2 in range(16):
            wbuf = wd[o2 % 2]

            par = "WL" + str(o2 % 2)

            @S_("gpsimd")
            def _(gp, wbuf=wbuf, o2=o2, hf=hf, par=par):
                ctx.wait(gp, f"b_parcons_{par}")
                srcw = Wd0T[:, o2 * 128:(o2 + 1) * 128].rearrange("(c p) o -> p c o", p=128)
                gp.dma_start(out=wbuf[:].rearrange("p (c o) -> p c o", o=128),
                             in_=srcw).then_inc(ctx.sem[par], 16)
                ctx.bump(par, 16, tag=f"b_wdld_{hf}_{o2}")

            @S_("tensor")
            def _(tensor, wbuf=wbuf, o2=o2, hf=hf, par=par):
                ctx.wait(tensor, f"b_wdld_{hf}_{o2}")
                ctx.wait(tensor, f"b_hdone_{hf}")
                ctx.wait(tensor, "b_dnev")
                for ic in range(64):
                    mm = tensor.matmul(ps_d[:], wbuf[:, ic * 128:(ic + 1) * 128],
                                       h_sb[:, ic * 512:(ic + 1) * 512],
                                       start=(ic == 0), stop=(ic == 63))
                ctx.inc(mm, "PE", tag=f"b_wmm_{id(wbuf)}")
                ctx.tag["b_dnmm"] = ctx.tag[f"b_wmm_{id(wbuf)}"]
                ctx.tag[f"b_parcons_{par}"] = ctx.tag[f"b_wmm_{id(wbuf)}"]

            @S_("vector")
            def _(vector, o2=o2, hf=hf):
                ctx.wait(vector, "b_dnmm")
                ctx.wait(vector, "b_dnfm_free")
                ctx.inc(vector.tensor_copy(dn_fm[:, o2 * 512:(o2 + 1) * 512], ps_d[:]),
                        "VE", tag="b_dnev")
        ctx.tag[f"b_hWAR_{1 - hf}"] = ctx.tag["b_dnmm"]

        for tt in range(4):
            i = hf * 4 + tt

            @S_("tensor")
            def _(tensor, tt=tt, hf=hf):
                ctx.wait(tensor, "b_dnev")
                ctx.wait(tensor, f"b_dtev")
                for fh in range(2):
                    for f8 in range(8):
                        f = fh * 8 + f8
                        mm = tensor.transpose(ps_t[:, fh * 1024 + f8 * 128:fh * 1024 + (f8 + 1) * 128],
                                              dn_fm[:, f * 512 + tt * 128:f * 512 + (tt + 1) * 128],
                                              ident[:])
                    ctx.inc(mm, "PE", tag=f"b_dtp_{fh}")
                ctx.tag["b_dnfm_free"] = ctx.tag["b_dtp_1"]

            @S_("vector")
            def _(vector, tt=tt):
                for fh in range(2):
                    ctx.wait(vector, f"b_dtp_{fh}")
                    ctx.inc(vector.tensor_copy(dn_tm[:, fh * 1024:(fh + 1) * 1024],
                                               ps_t[:, fh * 1024:(fh + 1) * 1024]), "VE", tag="b_dtev")

            emit_post_ln_add(ctx, g, dn_tm, xt, nm, rs, t0, sqscr, xnbf, i,
                             xout_ap, tag=f"bp_{i}", dn_ready_tag="b_dtev")


def emit_post_ln_add(ctx, g, dn_tm, xt, nm, rs, t0, sqscr, obf, i, xout_ap, tag,
                     dn_ready_tag):
    S_ = ctx.sec
    xsp = g["xsp"]

    @S_("sync")
    def _(sync, i=i):
        ctx.wait_cnt(sync, "ST")
        ctx.wait(sync, dn_ready_tag)
        sync.dma_start(out=xt[:], in_=xsp[i * 128:(i + 1) * 128, :]).then_inc(ctx.sem["LD"], 16)
        ctx.bump("LD", 16, tag=f"{tag}_ld")

    @S_("vector")
    def _(vector):
        ctx.wait(vector, dn_ready_tag)

    @S_("scalar")
    def _(scalar):
        ctx.wait(scalar, dn_ready_tag)

    _ln_sections(ctx, g, dn_tm[:], nm, rs, t0, sqscr, DM, tag + "_ln")

    @S_("vector")
    def _(vector):
        ctx.wait(vector, tag + "_ln_rs")
        ctx.wait(vector, f"{tag}_ld")
        ctx.inc(vector.tensor_scalar(obf[:], dn_tm[:], nm[:], rs[:], ALU.add, ALU.mult), "VE")
        ctx.wait_cnt(vector, "VE")
        ctx.inc(vector.tensor_tensor(xt[:], xt[:], obf[:], ALU.add),
                "VE", tag=f"{tag}_new")

    @S_("sync")
    def _(sync, i=i):
        ctx.wait(sync, f"{tag}_new")
        sync.dma_start(out=xsp[i * 128:(i + 1) * 128, :], in_=xt[:]).then_inc(ctx.sem["ST"], 16)
        n = ctx.bump("ST", 16)
        if xout_ap is not None:
            sync.dma_start(out=xout_ap[i * 128:(i + 1) * 128, :], in_=xt[:]).then_inc(ctx.sem["ST"], 16)
            n = ctx.bump("ST", 16)
        ctx.tag[f"{tag}_st"] = ("ST", n)
        ctx.tag[_next_xused(tag)] = ("ST", n)


def _next_xused(tag):
    head, idx = tag.rsplit("_", 1)
    return f"{head}_{int(idx) + 1}_xused_prev"



def emit_phase_A(ctx, g, xin_ap, xout_ap):
    S_ = ctx.sec
    nc = ctx.nc
    xsp = g["xsp"]
    ident = g["ident_sb"]
    src = xin_ap if xin_ap is not None else xsp
    WqT, WkT, WvT, WoT, SD = g["WqT"], g["WkT"], g["WvT"], g["WoT"], g["SD"]
    kag_in, kag_out = g["kag_in"], g["kag_out"]
    vag_in, vag_out = g["vag_in"], g["vag_out"]

    P = g["phase_stack"].enter_context
    xt = P(nc.sbuf_tensor("a_xt", [128, DM], F32))
    xnbf = P(nc.sbuf_tensor("a_xnbf", [128, DM], BF16))
    sqscr = P(nc.sbuf_tensor("a_sqscr", [128, DM], F32))
    nm = P(nc.sbuf_tensor("a_nm", [128, 1], F32))
    rs = P(nc.sbuf_tensor("a_rs", [128, 1], F32))
    t0 = P(nc.sbuf_tensor("a_t0", [128, 1], F32))
    abig = P(nc.sbuf_tensor("a_big", [128, 4 * 8192], BF16))
    xn1T = abig[:, 0:16384]
    wkv = abig[:, 16384:32768]
    scores = abig[:]
    qsb = P(nc.sbuf_tensor("a_q", [128, 16 * TPC], BF16))
    wqb = [P(nc.sbuf_tensor(f"a_wq{j}", [128, 16 * 128], BF16)) for j in range(2)]
    wob = [P(nc.sbuf_tensor(f"a_wo{j}", [128, 2048], BF16)) for j in range(2)]
    kvown = P(nc.sbuf_tensor("a_kvown", [128, 1024], BF16))
    ksb = [P(nc.sbuf_tensor(f"a_k{j}", [128, 512], BF16)) for j in range(2)]
    vsb = [P(nc.sbuf_tensor(f"a_v{j}", [128, 512], BF16)) for j in range(2)]
    sdb = [P(nc.sbuf_tensor(f"a_sd{j}", [128, 512], BF16)) for j in range(2)]
    pt_sb = [P(nc.sbuf_tensor(f"a_pt{j}", [128, 512], BF16)) for j in range(2)]
    gmx = P(nc.sbuf_tensor("a_gmx", [128, 8], F32))
    sumc = P(nc.sbuf_tensor("a_sumc", [128, 8], F32))
    bcol = P(nc.sbuf_tensor("a_bcol", [128, 8], F32))
    rcolbf = P(nc.sbuf_tensor("a_rcolbf", [128, 16], BF16))
    rowbf = P(nc.sbuf_tensor("a_rowbf", [1, 2048], BF16))
    nbc = P(nc.sbuf_tensor("a_nbc", [128, 2048], BF16))
    o_sb = P(nc.sbuf_tensor("a_o", [128, 2048], BF16))
    dn_tm = P(nc.sbuf_tensor("a_dntm", [128, DM], BF16))
    onesb_a = P(nc.sbuf_tensor("a_onesb", [1, 128], BF16))
    g["onesb_a"] = onesb_a
    ps_kv = P(nc.psum_tensor("a_pskv", [128, 1024], F32))
    ps_pt = P(nc.psum_tensor("a_pspt", [128, 512], BF16))
    ps_o = P(nc.psum_tensor("a_pso", [128, 2048], F32))
    ps_s = ps_kv
    ps_t = ps_o[:].bitcast(BF16)

    def slope(h):
        return 2.0 ** (-0.5 * (h + 1))

    @S_("vector")
    def _(vector):
        ctx.inc(vector.memset(onesb_a[:], 1.0), "VE", tag="a_ones")

    def sd_idx(i, kc):
        return sum(2 * (j + 1) for j in range(i)) + kc

    def owner(a):
        return a % 8, a // 8

    for i in range(NT):
        @S_("sync")
        def _(sync, i=i):
            ctx.wait(sync, f"a_xn_{i-1}")
            ctx.wait(sync, f"a_sp_{i-1}")
            sync.dma_start(out=xt[:], in_=src[i * 128:(i + 1) * 128, :]).then_inc(ctx.sem["LD"], 16)
            ctx.bump("LD", 16, tag=f"a_ld_{i}")
            if xin_ap is not None:
                sync.wait_ge(ctx.sem["LD"], ctx.cnt["LD"])
                sync.dma_start(out=xsp[i * 128:(i + 1) * 128, :], in_=xt[:]).then_inc(ctx.sem["ST"], 16)
                ctx.bump("ST", 16, tag=f"a_sp_{i}")

        @S_("vector")
        def _(vector, i=i):
            ctx.wait(vector, f"a_ld_{i}")

        @S_("scalar")
        def _(scalar, i=i):
            ctx.wait(scalar, f"a_ld_{i}")

        _ln_sections(ctx, g, xt[:], nm, rs, t0, sqscr, DM, f"a_ln_{i}")

        @S_("vector")
        def _(vector, i=i):
            ctx.wait(vector, f"a_ln_{i}_rs")
            ctx.wait(vector, f"a_tr_{i-1}")
            ctx.inc(vector.tensor_scalar(xnbf[:], xt[:], nm[:], rs[:], ALU.add, ALU.mult),
                    "VE", tag=f"a_xn_{i}")

        @S_("tensor")
        def _(tensor, i=i):
            ctx.wait(tensor, f"a_xn_{i}")
            ctx.wait(tensor, f"a_trev_{i-1}")
            for fh in range(2):
                for f8 in range(8):
                    f = fh * 8 + f8
                    mm = tensor.transpose(
                        ps_t[:, fh * 1024 + f8 * 128:fh * 1024 + (f8 + 1) * 128],
                        xnbf[:, f * 128:(f + 1) * 128], ident[:])
                ctx.inc(mm, "PE", tag=f"a_trp_{i}_{fh}")
            ctx.tag[f"a_tr_{i}"] = ctx.tag[f"a_trp_{i}_1"]

        @S_("vector")
        def _(vector, i=i):
            for fh in range(2):
                ctx.wait(vector, f"a_trp_{i}_{fh}")
                dst = xn1T.rearrange("p (f t) -> p f t", t=TPC)[:, fh * 8:(fh + 1) * 8,
                                                                  i * 128:(i + 1) * 128]
                srcv = ps_t[:, fh * 1024:(fh + 1) * 1024].rearrange("p (f t) -> p f t", t=128)
                ctx.inc(vector.tensor_copy(dst, srcv), "VE", tag=f"a_trev_{i}")
    ctx.tag["a_xnT_ready"] = ctx.tag["a_trev_7"]

    @S_("gpsimd")
    def _(gp):
        sk = WkT.rearrange("(hc p) k -> p hc k", p=128)
        gp.dma_start(out=wkv.rearrange("p (hc e) -> p hc e", e=1024)[:, :, 0:512],
                     in_=sk).then_inc(ctx.sem["LDG"], 16)
        ctx.bump("LDG", 16)
        sv = WvT.rearrange("(hc p) k -> p hc k", p=128)
        gp.dma_start(out=wkv.rearrange("p (hc e) -> p hc e", e=1024)[:, :, 512:1024],
                     in_=sv).then_inc(ctx.sem["LDG"], 16)
        ctx.bump("LDG", 16, tag="a_wkv_ld")

    for gg in range(GRP):
        @S_("tensor")
        def _(tensor, gg=gg):
            if gg == 0:
                ctx.wait(tensor, "a_xnT_ready")
                ctx.wait(tensor, "a_wkv_ld")
            ctx.wait(tensor, "a_kv_ev")
            for hc in range(16):
                for tb in range(2):
                    mm = tensor.matmul(ps_kv[:, tb * 512:(tb + 1) * 512],
                                       wkv[:, hc * 1024 + gg * 128:hc * 1024 + gg * 128 + 128],
                                       xn1T[:, hc * TPC + tb * 512:hc * TPC + (tb + 1) * 512],
                                       start=(hc == 0), stop=(hc == 15))
            ctx.inc(mm, "PE", tag="a_kv_mm")

        @S_("vector")
        def _(vector, gg=gg):
            ctx.wait(vector, "a_kv_mm")
            ctx.wait(vector, "a_kv_st")
            ctx.inc(vector.tensor_copy(kvown[:], ps_kv[:]), "VE", tag="a_kv_ev")

        @S_("sync")
        def _(sync, gg=gg):
            ctx.wait(sync, "a_kv_ev")
            sync.dma_start(out=kag_in.ap()[gg * 128:(gg + 1) * 128, :], in_=kvown[:]).then_inc(ctx.sem["ST"], 16)
            ctx.bump("ST", 16, tag="a_kv_st")

    for tt in range(NT):
        @S_("tensor")
        def _(tensor, tt=tt):
            ctx.wait(tensor, "a_kv_ev")
            for hc in range(16):
                mm = tensor.matmul(ps_kv[:, 0:512],
                                   xn1T[:, hc * TPC + tt * 128:hc * TPC + (tt + 1) * 128],
                                   wkv.rearrange("p (hc e) -> p hc e", e=1024)[:, hc, 512:1024],
                                   start=(hc == 0), stop=(hc == 15))
            ctx.inc(mm, "PE", tag="a_kv_mm")

        @S_("vector")
        def _(vector, tt=tt):
            ctx.wait(vector, "a_kv_mm")
            ctx.wait(vector, "a_kv_st")
            ctx.inc(vector.tensor_copy(kvown[:, 0:512], ps_kv[:, 0:512]), "VE", tag="a_kv_ev")

        @S_("sync")
        def _(sync, tt=tt):
            ctx.wait(sync, "a_kv_ev")
            sync.dma_start(out=vag_in.ap()[tt * 128:(tt + 1) * 128, :], in_=kvown[:, 0:512]).then_inc(ctx.sem["ST"], 16)
            ctx.bump("ST", 16, tag="a_kv_st")

    @S_("gpsimd")
    def _(gp):
        ctx.wait(gp, "a_kv_st")
        gp.collective_compute("AllGather", ALU.bypass, replica_groups=[list(range(8))],
                              ins=[kag_in.ap().opt()], outs=[kag_out.ap().opt()]).then_inc(ctx.sem["CC"])
        ctx.bump("CC", 1)
        gp.wait_ge(ctx.sem["CC"], ctx.cnt["CC"])
        gp.collective_compute("AllGather", ALU.bypass, replica_groups=[list(range(8))],
                              ins=[vag_in.ap().opt()], outs=[vag_out.ap().opt()]).then_inc(ctx.sem["CC"])
        ctx.bump("CC", 1, tag="a_gather")
        gp.wait_ge(ctx.sem["CC"], ctx.cnt["CC"])

    for h in range(HEADS):
        par = "WL" + str(h % 2)
        wbuf = wqb[h % 2]

        @S_("gpsimd")
        def _(gp, h=h, par=par, wbuf=wbuf):
            ctx.wait(gp, f"a_qcons_{par}")
            srcw = WqT[:, h * 128:(h + 1) * 128].rearrange("(c p) o -> p c o", p=128)
            gp.dma_start(out=wbuf[:].rearrange("p (c o) -> p c o", o=128),
                         in_=srcw).then_inc(ctx.sem[par], 16)
            ctx.bump(par, 16, tag=f"a_qld_{h}")

        @S_("tensor")
        def _(tensor, h=h, par=par, wbuf=wbuf):
            ctx.wait(tensor, f"a_qld_{h}")
            ctx.wait(tensor, "a_q_ev")
            if h == 0:
                ctx.wait(tensor, "a_kv_ev")
            for hc in range(16):
                for tb in range(2):
                    mm = tensor.matmul(ps_kv[:, tb * 512:(tb + 1) * 512],
                                       wbuf[:, hc * 128:(hc + 1) * 128],
                                       xn1T[:, hc * TPC + tb * 512:hc * TPC + (tb + 1) * 512],
                                       start=(hc == 0), stop=(hc == 15))
            n = ctx.inc(mm, "PE", tag="a_q_mm")
            ctx.tag[f"a_qcons_{par}"] = ("PE", n)

        @S_("vector")
        def _(vector, h=h):
            ctx.wait(vector, "a_q_mm")
            ctx.inc(vector.tensor_scalar(qsb[:, h * TPC:(h + 1) * TPC], ps_kv[:],
                                         1.0 / (slope(h) * math.sqrt(HD)), None,
                                         ALU.mult), "VE", tag="a_q_ev")
    ctx.tag["a_q_ready"] = ctx.tag["a_q_ev"]

    for i in range(NT):
        nkc = 2 * (i + 1)
        for gg in range(GRP):
            for kc in range(nkc):
                kp = kc % 2

                @S_("sync")
                def _(sync, i=i, gg=gg, kc=kc, kp=kp):
                    if i == 0 and gg == 0 and kc == 0:
                        ctx.wait(sync, "a_gather")
                    ctx.wait(sync, f"a_kuse_{kp}")
                    for sub in range(4):
                        a = kc * 4 + sub
                        rnk, slot = owner(a)
                        sync.dma_start(
                            out=ksb[kp][:, sub * 128:(sub + 1) * 128],
                            in_=kag_out.ap()[rnk * 512 + gg * 128:rnk * 512 + (gg + 1) * 128,
                                             slot * 128:(slot + 1) * 128]).then_inc(ctx.sem["KL" + str(kp)], 16)
                        ctx.bump("KL" + str(kp), 16)
                    ctx.tag[f"a_kld_{kp}"] = ("KL" + str(kp), ctx.cnt["KL" + str(kp)])

                @S_("scalar")
                def _(scalar, i=i, kc=kc, kp=kp):
                    ctx.wait(scalar, f"a_sduse_{kp}")
                    scalar.dma_start(out=sdb[kp][:],
                                     in_=SD[sd_idx(i, kc) * 128:(sd_idx(i, kc) + 1) * 128, :]).then_inc(ctx.sem["SD" + str(kp)], 16)
                    ctx.bump("SD" + str(kp), 16, tag=f"a_sdld_{kp}")

                for r in range(4):
                    h = gg * 4 + r
                    sp = (kc * 4 + r) % 2

                    @S_("tensor")
                    def _(tensor, i=i, h=h, kc=kc, kp=kp, sp=sp):
                        if kc == 0 and r == 0:
                            ctx.wait(tensor, "a_q_ready")
                        ctx.wait(tensor, f"a_kld_{kp}")
                        ctx.wait(tensor, f"a_sev_{sp}")
                        ctx.inc(tensor.matmul(ps_s[:, sp * 512:(sp + 1) * 512],
                                              qsb[:, h * TPC + i * 128:h * TPC + (i + 1) * 128],
                                              ksb[kp][:], start=True, stop=True),
                                "PE", tag=f"a_smm_{sp}")
                        if r == 3:
                            ctx.tag[f"a_kuse_{kp}"] = ("PE", ctx.cnt["PE"])

                    @S_("vector")
                    def _(vector, r=r, kc=kc, kp=kp, sp=sp):
                        ctx.wait(vector, f"a_smm_{sp}")
                        ctx.wait(vector, f"a_sdld_{kp}")
                        ctx.wait(vector, "a_sc_free")
                        n = ctx.inc(vector.tensor_tensor(
                            scores[:, r * 8192 + kc * 512:r * 8192 + (kc + 1) * 512],
                            ps_s[:, sp * 512:(sp + 1) * 512], sdb[kp][:], ALU.subtract),
                            "VE", tag=f"a_sev_{sp}")
                        if r == 3:
                            ctx.tag[f"a_sduse_{kp}"] = ("VE", n)
                ctx.tag[f"a_scores_{gg}_{kc}"] = ctx.tag[f"a_sev_{(kc * 4 + 3) % 2}"]
            ctx.tag["a_passA"] = ("VE", ctx.cnt["VE"])

            @S_("vector")
            def _(vector, i=i, gg=gg, nkc=nkc):
                ctx.wait(vector, "a_passA")
                for r in range(4):
                    ctx.wait_cnt(vector, "VE")
                    ctx.inc(vector.reduce_max(gmx[:, r:r + 1],
                                              scores[:, r * 8192:r * 8192 + nkc * 512],
                                              AX.X), "VE", tag=f"a_mx_{r}")

            @S_("vector")
            def _(vector, gg=gg):
                for r in range(4):
                    ctx.wait(vector, f"a_mx_{r}")
                    ctx.wait_cnt(vector, "VE")
                    ctx.inc(vector.tensor_scalar(bcol[:, r:r + 1], gmx[:, r:r + 1],
                                                 -slope(gg * 4 + r), None, ALU.mult),
                            "VE", tag=f"a_bc_{r}")

            for r in range(4):
                @S_("scalar")
                def _(scalar, r=r, gg=gg, nkc=nkc):
                    ctx.wait(scalar, f"a_bc_{r}")
                    ctx.wait(scalar, "a_rc_done")
                    ctx.inc(scalar.activation(scores[:, r * 8192:r * 8192 + nkc * 512],
                                              scores[:, r * 8192:r * 8192 + nkc * 512],
                                              AF.Exp, bias=bcol[:, r:r + 1],
                                              scale=slope(gg * 4 + r),
                                              accum_out=sumc[:, r:r + 1]), "AC",
                            tag=f"a_exp_{r}")

            for kc in range(nkc):
                kp = kc % 2

                @S_("sync")
                def _(sync, gg=gg, kc=kc, kp=kp):
                    ctx.wait(sync, f"a_vuse_{kp}")
                    for sub in range(4):
                        a = kc * 4 + sub
                        rnk, slot = owner(a)
                        sync.dma_start(
                            out=vsb[kp][:, sub * 128:(sub + 1) * 128],
                            in_=vag_out.ap()[rnk * 1024 + slot * 128:rnk * 1024 + (slot + 1) * 128,
                                             gg * 128:(gg + 1) * 128]).then_inc(ctx.sem["VL" + str(kp)], 16)
                        ctx.bump("VL" + str(kp), 16)
                    ctx.tag[f"a_vld_{kp}"] = ("VL" + str(kp), ctx.cnt["VL" + str(kp)])

                for sub in range(4):
                    pp = (kc * 4 + sub) % 2

                    @S_("tensor")
                    def _(tensor, gg=gg, kc=kc, sub=sub, pp=pp, i=i):
                        for r in range(4):
                            if r == 0:
                                ctx.wait(tensor, "a_exp_3")
                                ctx.wait(tensor, f"a_ptev_{pp}")
                            mm = tensor.transpose(
                                ps_pt[:, r * 128:(r + 1) * 128],
                                scores[:, r * 8192 + kc * 512 + sub * 128:r * 8192 + kc * 512 + (sub + 1) * 128],
                                ident[:])
                        ctx.inc(mm, "PE", tag=f"a_pt_{pp}")

                    @S_("vector")
                    def _(vector, pp=pp):
                        ctx.wait(vector, f"a_pt_{pp}")
                        ctx.wait(vector, f"a_ptuse_{pp}")
                        ctx.inc(vector.tensor_copy(pt_sb[pp][:], ps_pt[:]), "VE",
                                tag=f"a_ptev_{pp}")

                    @S_("tensor")
                    def _(tensor, gg=gg, kc=kc, sub=sub, pp=pp, kp=kp):
                        ctx.wait(tensor, f"a_ptev_{pp}")
                        ctx.wait(tensor, f"a_vld_{kp}")
                        if kc == 0 and sub == 0:
                            ctx.wait(tensor, "a_opev")
                        n = ctx.inc(tensor.matmul(ps_o[:, gg * 512:(gg + 1) * 512],
                                                  vsb[kp][:, sub * 128:(sub + 1) * 128],
                                                  pt_sb[pp][:],
                                                  start=(kc == 0 and sub == 0),
                                                  stop=(kc == nkc - 1 and sub == 3)),
                                    "PE", tag=f"a_pv_{pp}")
                        ctx.tag[f"a_ptuse_{pp}"] = ("PE", n)
                        if sub == 3:
                            ctx.tag[f"a_vuse_{kp}"] = ("PE", n)
            ctx.tag["a_sc_free"] = ("PE", ctx.cnt["PE"])

            @S_("vector")
            def _(vector, gg=gg):
                ctx.wait(vector, "a_exp_3")
                for r in range(4):
                    ctx.wait_cnt(vector, "VE")
                    ctx.inc(vector.reciprocal(bcol[:, 4 + r:5 + r], sumc[:, r:r + 1]), "VE")
                    ctx.wait_cnt(vector, "VE")
                    ctx.inc(vector.tensor_copy(rcolbf[:, gg * 4 + r:gg * 4 + r + 1],
                                               bcol[:, 4 + r:5 + r]),
                            "VE", tag="a_rc_done")

        for rnd in range(2):
            @S_("tensor")
            def _(tensor, rnd=rnd):
                ctx.wait(tensor, "a_rc_done")
                ctx.wait(tensor, "a_rowev")
                for h8 in range(8):
                    h = rnd * 8 + h8
                    mm = tensor.matmul(ps_s[0:1, h8 * 128:(h8 + 1) * 128],
                                       rcolbf[:, h:h + 1], ident[:],
                                       start=True, stop=True)
                ctx.inc(mm, "PE", tag="a_nrow")

            @S_("vector")
            def _(vector, rnd=rnd):
                ctx.wait(vector, "a_nrow")
                ctx.inc(vector.tensor_copy(rowbf[0:1, rnd * 1024:(rnd + 1) * 1024],
                                           ps_s[0:1, 0:1024]), "VE", tag="a_rowev")

        for cch in range(4):
            @S_("tensor")
            def _(tensor, cch=cch):
                ctx.wait(tensor, "a_rowev")
                ctx.wait(tensor, f"a_nbcev_{(cch - 2) % 2}")
                ctx.inc(tensor.matmul(ps_s[:, (cch % 2) * 512:(cch % 2 + 1) * 512],
                                      g["onesb_a"][:], rowbf[0:1, cch * 512:(cch + 1) * 512],
                                      start=True, stop=True), "PE", tag=f"a_nbc_{cch % 2}")

            @S_("vector")
            def _(vector, cch=cch):
                ctx.wait(vector, f"a_nbc_{cch % 2}")
                ctx.inc(vector.tensor_copy(nbc[:, cch * 512:(cch + 1) * 512],
                                           ps_s[:, (cch % 2) * 512:(cch % 2 + 1) * 512]),
                        "VE", tag=f"a_nbcev_{cch % 2}")

        @S_("vector")
        def _(vector, i=i):
            ctx.wait(vector, "a_nbcev_0")
            ctx.wait(vector, "a_nbcev_1")
            for cch in range(4):
                ctx.wait_cnt(vector, "VE")
                ctx.inc(vector.tensor_tensor(o_sb[:, cch * 512:(cch + 1) * 512],
                                             ps_o[:, cch * 512:(cch + 1) * 512],
                                             nbc[:, cch * 512:(cch + 1) * 512],
                                             ALU.mult), "VE", tag="a_o_ev")

        for h in range(HEADS):
            par = "WL" + str(h % 2)
            wbuf = wob[h % 2]

            @S_("gpsimd")
            def _(gp, h=h, par=par, wbuf=wbuf):
                ctx.wait(gp, f"a_ocons_{par}")
                gp.dma_start(out=wbuf[:], in_=WoT[h * 128:(h + 1) * 128, :]).then_inc(ctx.sem[par], 16)
                ctx.bump(par, 16, tag=f"a_old_{h}")

            @S_("tensor")
            def _(tensor, h=h, par=par, wbuf=wbuf):
                ctx.wait(tensor, f"a_old_{h}")
                if h == 0:
                    ctx.wait(tensor, "a_o_ev")
                for oc in range(4):
                    mm = tensor.matmul(ps_o[:, oc * 512:(oc + 1) * 512],
                                       o_sb[:, h * 128:(h + 1) * 128],
                                       wbuf[:, oc * 512:(oc + 1) * 512],
                                       start=(h == 0), stop=(h == 15))
                n = ctx.inc(mm, "PE", tag="a_opmm")
                ctx.tag[f"a_ocons_{par}"] = ("PE", n)

        @S_("vector")
        def _(vector, i=i):
            ctx.wait(vector, "a_opmm")
            for cch in range(4):
                ctx.wait_cnt(vector, "VE")
                ctx.inc(vector.tensor_copy(dn_tm[:, cch * 512:(cch + 1) * 512],
                                           ps_o[:, cch * 512:(cch + 1) * 512]), "VE",
                        tag="a_opev")

        emit_post_ln_add(ctx, g, dn_tm, xt, nm, rs, t0, sqscr, xnbf, i,
                         xout_ap, tag=f"ap_{i}", dn_ready_tag="a_opev")


def emit_phase_C(ctx, g, xin_ap, xout_ap):
    S_ = ctx.sec
    nc = ctx.nc
    xsp = g["xsp"]
    ident = g["ident_sb"]
    src = xin_ap if xin_ap is not None else xsp
    WinT, WopT = g["WinT"], g["WopT"]
    a2a_xi, a2a_xo = g["a2a_xi"], g["a2a_xo"]
    a2a_yi, a2a_yo = g["a2a_yi"], g["a2a_yo"]
    ar_xi, ar_xo = g["ar_xi"], g["ar_xo"]
    z_dram, xn_dram = g["z_dram"], g["xn_dram"]
    xc_dr, bc_dr = g["xc_dr"], g["bc_dr"]
    H = S // 2

    P = g["phase_stack"].enter_context
    xt = P(nc.sbuf_tensor("c_xt", [128, DM], F32))
    xnbf = P(nc.sbuf_tensor("c_xnbf", [128, DM], BF16))
    sqscr = P(nc.sbuf_tensor("c_sqscr", [128, DM], F32))
    nm = P(nc.sbuf_tensor("c_nm", [128, 1], F32))
    rs = P(nc.sbuf_tensor("c_rs", [128, 1], F32))
    t0 = P(nc.sbuf_tensor("c_t0", [128, 1], F32))
    rs2 = P(nc.sbuf_tensor("c_rs2", [128, 1], F32))
    wi_b = [P(nc.sbuf_tensor(f"c_wi{j}", [128, 16 * 128], BF16)) for j in range(2)]
    wo_b = [P(nc.sbuf_tensor(f"c_wo{j}", [128, 16 * 128], BF16)) for j in range(2)]
    evb = P(nc.sbuf_tensor("c_evb", [128, 1024], BF16))
    nsc_sb = P(nc.sbuf_tensor("c_nsc", [128, 16], F32))
    carry = P(nc.sbuf_tensor("c_carry", [128, 16], F32))
    convw = P(nc.sbuf_tensor("c_convw", [128, 16], F32))
    convb = P(nc.sbuf_tensor("c_convb", [128, 4], F32))
    alogc = P(nc.sbuf_tensor("c_alog", [128, 64], F32))
    acol = P(nc.sbuf_tensor("c_acol", [128, 64], F32))
    dcol = P(nc.sbuf_tensor("c_dcol", [128, 4], F32))
    dtbc = P(nc.sbuf_tensor("c_dtbc", [128, 4], F32))
    wxp = P(nc.sbuf_tensor("c_wxp", [128, 4 * 160], BF16))
    wdt = P(nc.sbuf_tensor("c_wdt", [128, 512], BF16))
    xd1 = P(nc.sbuf_tensor("c_xd1", [128, 2048], F32))
    dn_tm = P(nc.sbuf_tensor("c_dntm", [128, DM], BF16))
    onesb = P(nc.sbuf_tensor("c_onesb", [1, 128], BF16))
    AR = P(nc.sbuf_tensor("c_arena", [128, 73760], BF16))
    ps_a = P(nc.psum_tensor("c_psa", [128, 1024], F32))
    ps_b = P(nc.psum_tensor("c_psb", [128, 1024], F32))
    ps_x1 = P(nc.psum_tensor("c_psx1", [32, 512], F32))
    ps_t = ps_a[:].bitcast(BF16)

    xmT = AR[:, 0:16384]
    xin_sb = AR[:, 16384:24579]
    xc_sl = AR[:, 24580:32772]
    accf = AR[:, 32772:40964].bitcast(F32)
    xpr0 = AR[:, 40964:57348].bitcast(F32)
    X_sl = AR[:, 0:8192]
    dl_sl = AR[:, 8192:16384]
    bcb = [AR[:, 40964 + j * 8192:40964 + (j + 1) * 8192] for j in range(2)]
    yacf = AR[:, 57348:65540].bitcast(F32)
    b_sl = AR[:, 65540:69636]
    h_sl = AR[:, 69636:73732]
    spr = [AR[:, 57348 + j * 1024:57348 + (j + 1) * 1024].bitcast(F32) for j in range(4)]
    ym = AR[:, 0:16384]
    opacc = AR[:, 16384:32768]
    zy_sl = AR[:, 65540:67588]

    @S_("vector")
    def _(vector):
        vector.memset(xin_sb[:, 0:3], 0.0)
        ctx.inc(vector.memset(onesb[:], 1.0), "VE", tag="c_pad")

    @S_("sync")
    def _(sync):
        sync.dma_start(allow_slow_non_contiguous=True, out=nsc_sb[:].rearrange("p (f one) -> p f one", one=1), in_=g["nsc"].rearrange("(f p) one -> p f one", p=128)).then_inc(ctx.sem["LD"], 16)
        sync.dma_start(allow_slow_non_contiguous=True, out=convw[:].rearrange("p (ct k) -> p ct k", k=4), in_=g["convw"].rearrange("(ct p) k -> p ct k", p=128)).then_inc(ctx.sem["LD"], 16)
        sync.dma_start(allow_slow_non_contiguous=True, out=convb[:].rearrange("p (ct one) -> p ct one", one=1), in_=g["convb"].rearrange("(ct p) one -> p ct one", p=128)).then_inc(ctx.sem["LD"], 16)
        sync.dma_start(allow_slow_non_contiguous=True, out=dcol[:].rearrange("p (ct one) -> p ct one", one=1), in_=g["Dvec"].rearrange("(ct p) one -> p ct one", p=128)).then_inc(ctx.sem["LD"], 16)
        sync.dma_start(allow_slow_non_contiguous=True, out=dtbc[:].rearrange("p (ct one) -> p ct one", one=1), in_=g["dtb"].rearrange("(ct p) one -> p ct one", p=128)).then_inc(ctx.sem["LD"], 16)
        sync.dma_start(allow_slow_non_contiguous=True, out=alogc[:].rearrange("p (ct s) -> p ct s", s=16), in_=g["Alog"].rearrange("(ct p) s -> p ct s", p=128)).then_inc(ctx.sem["LD"], 16)
        ctx.bump("LD", 16 * 6, tag="c_cols_ld")

    @S_("scalar")
    def _(scalar):
        ctx.wait(scalar, "c_cols_ld")
        ctx.inc(scalar.activation(acol[:], alogc[:], AF.Exp), "AC", tag="c_alog_e")

    @S_("vector")
    def _(vector):
        ctx.wait(vector, "c_alog_e")
        ctx.wait_cnt(vector, "VE")
        ctx.inc(vector.tensor_scalar(acol[:], acol[:], -1.0, None, ALU.mult),
                "VE", tag="c_acol")

    @S_("gpsimd")
    def _(gp):
        gp.dma_start(allow_slow_non_contiguous=True, out=wxp[:].rearrange("p (ct r) -> p ct r", r=160), in_=g["WxpT"].rearrange("(ct p) r -> p ct r", p=128)).then_inc(ctx.sem["LDG"], 16)
        ctx.bump("LDG", 16)
        gp.dma_start(out=wdt[:], in_=g["WdtT"]).then_inc(ctx.sem["LDG"], 16)
        ctx.bump("LDG", 16, tag="c_w_ld")

    for i in range(NT):
        @S_("sync")
        def _(sync, i=i):
            ctx.wait(sync, f"c_xm_{i-1}")
            ctx.wait(sync, f"c_sp_{i-1}")
            sync.dma_start(out=xt[:], in_=src[i * 128:(i + 1) * 128, :]).then_inc(ctx.sem["LD"], 16)
            ctx.bump("LD", 16, tag=f"c_ld_{i}")
            if xin_ap is not None:
                sync.wait_ge(ctx.sem["LD"], ctx.cnt["LD"])
                sync.dma_start(out=xsp[i * 128:(i + 1) * 128, :], in_=xt[:]).then_inc(ctx.sem["ST"], 16)
                ctx.bump("ST", 16, tag=f"c_sp_{i}")

        @S_("vector")
        def _(vector, i=i):
            ctx.wait(vector, f"c_ld_{i}")

        @S_("scalar")
        def _(scalar, i=i):
            ctx.wait(scalar, f"c_ld_{i}")

        _ln_sections(ctx, g, xt[:], nm, rs, t0, sqscr, DM, f"c_ln_{i}")

        @S_("vector")
        def _(vector, i=i):
            ctx.wait(vector, f"c_ln_{i}_rs")
            ctx.wait(vector, f"c_hst_{i-1}")
            ctx.inc(vector.tensor_scalar(xnbf[:], xt[:], nm[:], rs[:], ALU.add, ALU.mult),
                    "VE", tag=f"c_h_{i}")

        @S_("sync")
        def _(sync, i=i):
            ctx.wait(sync, f"c_h_{i}")
            sync.dma_start(out=xn_dram.ap()[i * 128:(i + 1) * 128, :], in_=xnbf[:]).then_inc(ctx.sem["ST"], 16)
            ctx.bump("ST", 16, tag=f"c_hst_{i}")

        _ln_sections(ctx, g, xnbf[:], nm, rs2, t0, sqscr, DM, f"c_rms_{i}", rms_only=True)

        @S_("vector")
        def _(vector, i=i):
            ctx.wait(vector, f"c_rms_{i}_rs")
            ctx.wait(vector, f"c_hst_{i}")
            ctx.wait(vector, f"c_tr_{i-1}")
            ctx.inc(vector.tensor_scalar(xnbf[:], xnbf[:], rs2[:], None, ALU.mult),
                    "VE", tag=f"c_xm_{i}")

        @S_("tensor")
        def _(tensor, i=i):
            ctx.wait(tensor, f"c_xm_{i}")
            ctx.wait(tensor, f"c_trev_{i-1}")
            for fh in range(2):
                for f8 in range(8):
                    f = fh * 8 + f8
                    mm = tensor.transpose(
                        ps_t[:, fh * 1024 + f8 * 128:fh * 1024 + (f8 + 1) * 128],
                        xnbf[:, f * 128:(f + 1) * 128], ident[:])
                ctx.inc(mm, "PE", tag=f"c_trp_{i}_{fh}")
            ctx.tag[f"c_tr_{i}"] = ctx.tag[f"c_trp_{i}_1"]

        @S_("vector")
        def _(vector, i=i):
            if i == 0:
                ctx.wait(vector, "c_cols_ld")
            for fh in range(2):
                ctx.wait(vector, f"c_trp_{i}_{fh}")
                for f8 in range(8):
                    f = fh * 8 + f8
                    ctx.inc(vector.tensor_scalar(
                        xmT[:, f * TPC + i * 128:f * TPC + (i + 1) * 128],
                        ps_t[:, fh * 1024 + f8 * 128:fh * 1024 + (f8 + 1) * 128],
                        nsc_sb[:, f:f + 1], None, ALU.mult), "VE", tag=f"c_trev_{i}")
    ctx.tag["c_xmT_ready"] = ctx.tag["c_trev_7"]

    for ec in range(64):
        par = "WL" + str(ec % 2)
        wbuf = wi_b[ec % 2]

        @S_("gpsimd")
        def _(gp, ec=ec, par=par, wbuf=wbuf):
            ctx.wait(gp, f"c_icons_{par}")
            srcw = WinT[:, ec * 128:(ec + 1) * 128].rearrange("(c p) o -> p c o", p=128)
            gp.dma_start(out=wbuf[:].rearrange("p (c o) -> p c o", o=128),
                         in_=srcw).then_inc(ctx.sem[par], 16)
            ctx.bump(par, 16, tag=f"c_ild_{ec}")

        @S_("tensor")
        def _(tensor, ec=ec, par=par, wbuf=wbuf):
            if ec == 0:
                ctx.wait(tensor, "c_xmT_ready")
            ctx.wait(tensor, f"c_ild_{ec}")
            ctx.wait(tensor, "c_iev")
            for hc in range(16):
                for tb in range(2):
                    mm = tensor.matmul(ps_b[:, tb * 512:(tb + 1) * 512],
                                       wbuf[:, hc * 128:(hc + 1) * 128],
                                       xmT[:, hc * TPC + tb * 512:hc * TPC + (tb + 1) * 512],
                                       start=(hc == 0), stop=(hc == 15))
            n = ctx.inc(mm, "PE", tag="c_imm")
            ctx.tag[f"c_icons_{par}"] = ("PE", n)

        @S_("vector")
        def _(vector, ec=ec):
            ctx.wait(vector, "c_imm")
            ctx.wait(vector, "c_ist")
            ctx.inc(vector.tensor_copy(evb[:], ps_b[:]), "VE", tag="c_iev")

        @S_("sync")
        def _(sync, ec=ec):
            ctx.wait(sync, "c_iev")
            if ec < 32:
                dst = a2a_xi.ap()[ec * 128:(ec + 1) * 128, :]
            else:
                dst = z_dram.ap()[(ec - 32) * 128:(ec - 31) * 128, :]
            sync.dma_start(out=dst, in_=evb[:]).then_inc(ctx.sem["ST"], 16)
            ctx.bump("ST", 16, tag="c_ist")

    @S_("gpsimd")
    def _(gp):
        ctx.wait(gp, "c_ist")
        with allow_ici_all_to_all():
            gp.collective_compute("AllToAll", ALU.bypass, replica_groups=[list(range(8))],
                                  ins=[a2a_xi.ap().opt()], outs=[a2a_xo.ap().opt()]).then_inc(ctx.sem["CC"])
        ctx.bump("CC", 1, tag="c_a2a_x")
        gp.wait_ge(ctx.sem["CC"], ctx.cnt["CC"])

    for ct in range(4):
        @S_("sync")
        def _(sync, ct=ct):
            if ct == 0:
                ctx.wait(sync, "c_a2a_x")
            ctx.wait(sync, f"c_xpdone_{ct-1}")
            flat = a2a_xo.ap().rearrange("r c -> (r c)").rearrange(
                "(s chp j) -> s chp j", s=8, chp=512)
            vv = flat[:, ct * 128:(ct + 1) * 128, :].rearrange(
                "s p (i tau) -> p i s tau", tau=128)
            dst = xin_sb[:, 3:3 + S].rearrange("p (i s tau) -> p i s tau", i=8, s=8)
            sync.dma_start(out=dst, in_=vv).then_inc(ctx.sem["LD"], 16)
            ctx.bump("LD", 16, tag=f"c_xinld_{ct}")

        for half in range(2):
            o0 = half * H

            @S_("vector")
            def _(vector, ct=ct, half=half, o0=o0):
                ctx.wait(vector, f"c_xinld_{ct}")
                ctx.wait(vector, "c_pad")
                ctx.wait(vector, f"c_silu_{ct}_{half-1}" if half else f"c_silu_{ct-1}_1")
                ctx.wait_cnt(vector, "VE")
                ctx.inc(vector.tensor_scalar(accf[:], xin_sb[:, o0:o0 + H],
                                             convw[:, ct * 4:ct * 4 + 1], None, ALU.mult), "VE")
                for k in range(1, 4):
                    ctx.wait_cnt(vector, "VE")
                    ctx.inc(vector.scalar_tensor_tensor(accf[:], xin_sb[:, o0 + k:o0 + k + H],
                                                        convw[:, ct * 4 + k:ct * 4 + k + 1],
                                                        accf[:], ALU.mult, ALU.add), "VE")
                ctx.tag[f"c_conv_{ct}_{half}"] = ("VE", ctx.cnt["VE"])

            @S_("scalar")
            def _(scalar, ct=ct, half=half, o0=o0):
                ctx.wait(scalar, f"c_conv_{ct}_{half}")
                ctx.inc(scalar.activation(xc_sl[:, o0:o0 + H], accf[:], AF.Silu,
                                          bias=convb[:, ct:ct + 1]), "AC",
                        tag=f"c_silu_{ct}_{half}")

        @S_("sync")
        def _(sync, ct=ct):
            ctx.wait(sync, f"c_silu_{ct}_1")
            sync.dma_start(out=xc_dr.ap()[ct * 128:(ct + 1) * 128, :], in_=xc_sl[:]).then_inc(ctx.sem["ST"], 16)
            ctx.bump("ST", 16, tag=f"c_xcst_{ct}")

        for tc in range(16):
            @S_("tensor")
            def _(tensor, ct=ct, tc=tc):
                ctx.wait(tensor, f"c_silu_{ct}_1")
                if ct == 0 and tc == 0:
                    ctx.wait(tensor, "c_w_ld")
                ctx.wait(tensor, f"c_xpev_0")
                ctx.wait(tensor, f"c_xpev_1")
                ctx.inc(tensor.matmul(ps_a[:, (tc % 2) * 512:(tc % 2 + 1) * 512],
                                      wxp[:, ct * 160:ct * 160 + 128],
                                      xc_sl[:, tc * 512:(tc + 1) * 512],
                                      start=True, stop=True), "PE")
                ctx.inc(tensor.matmul(ps_x1[:, 0:512],
                                      wxp[:, ct * 160 + 128:ct * 160 + 160],
                                      xc_sl[:, tc * 512:(tc + 1) * 512],
                                      start=True, stop=True), "PE", tag=f"c_xpmm_{tc % 2}")

            @S_("vector")
            def _(vector, ct=ct, tc=tc):
                ctx.wait(vector, f"c_xpmm_{tc % 2}")
                ctx.wait_cnt(vector, "VE")
                pa = ps_a[:, (tc % 2) * 512:(tc % 2 + 1) * 512]
                x1dst = xd1[(tc % 4) * 32:(tc % 4 + 1) * 32, (tc // 4) * 512:(tc // 4 + 1) * 512]
                if ct == 0:
                    vector.tensor_copy(xpr0[:, tc * 512:(tc + 1) * 512], pa)
                    ctx.inc(vector.tensor_copy(x1dst, ps_x1[:, 0:512]), "VE",
                            tag=f"c_xpev_{tc % 2}")
                else:
                    vector.tensor_tensor(xpr0[:, tc * 512:(tc + 1) * 512],
                                         xpr0[:, tc * 512:(tc + 1) * 512], pa, ALU.add)
                    ctx.inc(vector.tensor_tensor(x1dst, x1dst, ps_x1[:, 0:512], ALU.add),
                            "VE", tag=f"c_xpev_{tc % 2}")
        ctx.tag[f"c_xpdone_{ct}"] = ctx.tag[f"c_xpev_1"]

    @S_("sync")
    def _(sync):
        ctx.wait(sync, "c_xpdone_3")
        sync.dma_start(out=ar_xi.ap()[0:128, :], in_=xpr0[:]).then_inc(ctx.sem["ST"], 16)
        ctx.bump("ST", 16)
        for q in range(4):
            xsrc = xd1[q * 32:(q + 1) * 32, :].rearrange("r (blk c) -> r blk c", blk=4)
            xdst = ar_xi.ap()[128:160, :].rearrange("r (blk qq c) -> r blk qq c",
                                                    blk=4, qq=4)[:, :, q, :]
            sync.dma_start(out=xdst, in_=xsrc).then_inc(ctx.sem["ST"], 16)
            ctx.bump("ST", 16)
        ctx.tag["c_xp_st"] = ("ST", ctx.cnt["ST"])

    @S_("gpsimd")
    def _(gp):
        ctx.wait(gp, "c_xp_st")
        gp.collective_compute("AllReduce", ALU.add, replica_groups=[list(range(8))],
                              ins=[ar_xi.ap().opt()], outs=[ar_xo.ap().opt()]).then_inc(ctx.sem["CC"])
        ctx.bump("CC", 1, tag="c_ar")
        gp.wait_ge(ctx.sem["CC"], ctx.cnt["CC"])

    @S_("gpsimd")
    def _(gp):
        ctx.wait(gp, "c_ar")
        for s in range(2 * DST):
            row = ar_xo.ap()[128 + s:129 + s, :]
            gp.dma_start(out=bc_dr.ap()[s * 128:(s + 1) * 128, :],
                         in_=row.broadcast_to([128, S])).then_inc(ctx.sem["LDG"], 16)
            ctx.bump("LDG", 16)
        ctx.tag["c_bc_built"] = ("LDG", ctx.cnt["LDG"])

    for ct in range(4):
        @S_("gpsimd")
        def _(gp, ct=ct):
            ctx.wait(gp, "c_ar")
            ctx.wait(gp, f"c_Xfree_{ct-1}")
            gp.dma_start(out=X_sl, in_=ar_xo.ap()[0:128, :]).then_inc(ctx.sem["LDG"], 16)
            ctx.bump("LDG", 16, tag=f"c_dtlld_{ct}")

        @S_("sync")
        def _(sync, ct=ct):
            ctx.wait(sync, f"c_xcfree_{ct-1}")
            sync.dma_start(out=xc_sl, in_=xc_dr.ap()[ct * 128:(ct + 1) * 128, :]).then_inc(ctx.sem["LD"], 16)
            ctx.bump("LD", 16, tag=f"c_xcld_{ct}")

        for tc in range(16):
            pp = tc % 2

            @S_("tensor")
            def _(tensor, ct=ct, tc=tc, pp=pp):
                if tc == 0:
                    ctx.wait(tensor, f"c_dtlld_{ct}")
                    if ct == 0:
                        ctx.wait(tensor, "c_iev")
                ctx.wait(tensor, f"c_dtu_{pp}")
                ctx.inc(tensor.matmul(ps_b[:, pp * 512:(pp + 1) * 512],
                                      wdt[:, ct * 128:(ct + 1) * 128],
                                      X_sl[:, tc * 512:(tc + 1) * 512],
                                      start=True, stop=True), "PE", tag=f"c_dtmm_{pp}")

            @S_("scalar")
            def _(scalar, ct=ct, tc=tc, pp=pp):
                ctx.wait(scalar, f"c_dtmm_{pp}")
                ctx.wait(scalar, f"c_dtudone_{pp}")
                ctx.inc(scalar.activation(accf[:, pp * 512:(pp + 1) * 512],
                                          ps_b[:, pp * 512:(pp + 1) * 512],
                                          AF.Identity, bias=dtbc[:, ct:ct + 1]), "AC",
                        tag=f"c_dtu_{pp}")

            @S_("vector")
            def _(vector, tc=tc, pp=pp):
                ctx.wait(vector, f"c_dtu_{pp}")
                ctx.wait_cnt(vector, "VE")
                ctx.inc(vector.tensor_scalar(spr[pp][:], accf[:, pp * 512:(pp + 1) * 512],
                                             0.0, None, ALU.max), "VE")
                ctx.wait_cnt(vector, "VE")
                ctx.inc(vector.tensor_scalar(spr[2 + pp][:], accf[:, pp * 512:(pp + 1) * 512],
                                             -1.0, None, ALU.mult), "VE")
                ctx.wait_cnt(vector, "VE")
                ctx.inc(vector.tensor_tensor(spr[2 + pp][:], spr[2 + pp][:],
                                             accf[:, pp * 512:(pp + 1) * 512], ALU.min),
                        "VE", tag=f"c_dtm_{pp}")

            @S_("scalar")
            def _(scalar, tc=tc, pp=pp):
                ctx.wait(scalar, f"c_dtm_{pp}")
                ctx.inc(scalar.activation(spr[2 + pp][:], spr[2 + pp][:], AF.Exp), "AC")
                ctx.wait_cnt(scalar, "AC")
                ctx.inc(scalar.activation(spr[2 + pp][:], spr[2 + pp][:], AF.Ln,
                                          bias=g["cONE"]), "AC", tag=f"c_dtln_{pp}")

            @S_("vector")
            def _(vector, ct=ct, tc=tc, pp=pp):
                ctx.wait(vector, f"c_dtln_{pp}")
                ctx.wait_cnt(vector, "VE")
                n = ctx.inc(vector.tensor_tensor(dl_sl[:, tc * 512:(tc + 1) * 512],
                                                 spr[pp][:], spr[2 + pp][:], ALU.add),
                            "VE", tag=f"c_dtev_{pp}")
                ctx.tag[f"c_dtudone_{pp}"] = ("VE", n)
        ctx.tag["c_delta"] = ctx.tag[f"c_dtev_1"]

        @S_("vector")
        def _(vector, ct=ct):
            ctx.wait(vector, "c_delta")
            ctx.wait(vector, f"c_xcld_{ct}")
            ctx.wait(vector, f"c_dtmm_0")
            ctx.wait(vector, f"c_dtmm_1")
            ctx.wait_cnt(vector, "VE")
            ctx.inc(vector.tensor_tensor(X_sl, dl_sl, xc_sl, ALU.mult), "VE", tag="c_du")

        for half in range(2):
            o0 = half * H

            @S_("vector")
            def _(vector, half=half):
                ctx.wait(vector, "c_yst")
                ctx.wait_cnt(vector, "VE")
                ctx.inc(vector.memset(yacf[:], 0.0), "VE", tag="c_yz")

            for s in range(DST):
                bp = s % 2

                @S_("gpsimd")
                def _(gp, ct=ct, half=half, s=s, bp=bp, o0=o0):
                    ctx.wait(gp, "c_bc_built")
                    ctx.wait(gp, f"c_bcuse_{bp}")
                    gp.dma_start(out=bcb[bp][:, 0:H],
                                 in_=bc_dr.ap()[s * 128:(s + 1) * 128, o0:o0 + H]).then_inc(ctx.sem["SD" + str(bp)], 16)
                    ctx.bump("SD" + str(bp), 16)
                    gp.dma_start(out=bcb[bp][:, H:2 * H],
                                 in_=bc_dr.ap()[(16 + s) * 128:(17 + s) * 128, o0:o0 + H]).then_inc(ctx.sem["SD" + str(bp)], 16)
                    ctx.bump("SD" + str(bp), 16, tag=f"c_bcld_{bp}")

                @S_("vector")
                def _(vector, ct=ct, s=s, o0=o0):
                    ctx.wait(vector, "c_acol")
                    ctx.wait(vector, "c_du")
                    ctx.wait(vector, "c_yz")
                    ctx.wait(vector, "c_dAe")
                    ctx.wait_cnt(vector, "VE")
                    ctx.inc(vector.tensor_scalar(accf[:], dl_sl[:, o0:o0 + H],
                                                 acol[:, ct * 16 + s:ct * 16 + s + 1],
                                                 None, ALU.mult), "VE", tag="c_dAa")

                @S_("scalar")
                def _(scalar, s=s):
                    ctx.wait(scalar, "c_dAa")
                    ctx.wait(scalar, "c_scan")
                    ctx.inc(scalar.activation(accf[:], accf[:], AF.Exp), "AC", tag="c_dAe")

                @S_("vector")
                def _(vector, ct=ct, half=half, s=s, bp=bp, o0=o0):
                    ctx.wait(vector, f"c_bcld_{bp}")
                    ctx.wait_cnt(vector, "VE")
                    ctx.inc(vector.tensor_tensor(b_sl, X_sl[:, o0:o0 + H],
                                                 bcb[bp][:, 0:H], ALU.mult), "VE")
                    ctx.wait(vector, "c_dAe")
                    ctx.wait_cnt(vector, "VE")
                    init = 0.0 if half == 0 else carry[:, s:s + 1]
                    ctx.inc(vector.tensor_tensor_scan(h_sl, accf[:].bitcast(F32)
                                                      if False else accf[:], b_sl, init,
                                                      ALU.mult, ALU.add), "VE", tag="c_scan")
                    ctx.wait_cnt(vector, "VE")
                    ctx.inc(vector.tensor_copy(carry[:, s:s + 1], h_sl[:, H - 1:H]), "VE")
                    ctx.wait_cnt(vector, "VE")
                    ctx.inc(vector.tensor_tensor(b_sl, h_sl, bcb[bp][:, H:2 * H],
                                                 ALU.mult), "VE")
                    ctx.wait_cnt(vector, "VE")
                    n = ctx.inc(vector.tensor_tensor(yacf[:], yacf[:], b_sl, ALU.add),
                                "VE", tag="c_yadd")
                    ctx.tag[f"c_bcuse_{bp}"] = ("VE", n)

            @S_("vector")
            def _(vector, ct=ct, half=half, o0=o0):
                ctx.wait(vector, "c_yadd")
                ctx.wait_cnt(vector, "VE")
                ctx.inc(vector.scalar_tensor_tensor(b_sl, xc_sl[:, o0:o0 + H],
                                                    dcol[:, ct:ct + 1], yacf[:],
                                                    ALU.mult, ALU.add), "VE",
                        tag=f"c_yo_{ct}_{half}")

            @S_("sync")
            def _(sync, ct=ct, half=half):
                ctx.wait(sync, f"c_yo_{ct}_{half}")
                for d in range(8):
                    srcv = b_sl.rearrange("p (i dd tau) -> p dd i tau", dd=8, tau=128)[:, d]
                    dstv = a2a_yi.ap()[d * 512 + ct * 128:d * 512 + (ct + 1) * 128,
                                       half * 512:(half + 1) * 512].rearrange(
                        "p (i tau) -> p i tau", tau=128)
                    sync.dma_start(out=dstv, in_=srcv).then_inc(ctx.sem["ST"], 16)
                    ctx.bump("ST", 16)
                ctx.tag["c_yst"] = ("ST", ctx.cnt["ST"])
        ctx.tag[f"c_Xfree_{ct}"] = ctx.tag["c_scan"]
        ctx.tag[f"c_xcfree_{ct}"] = ctx.tag[f"c_yo_{ct}_1"]

    @S_("gpsimd")
    def _(gp):
        ctx.wait(gp, "c_yst")
        with allow_ici_all_to_all():
            gp.collective_compute("AllToAll", ALU.bypass, replica_groups=[list(range(8))],
                                  ins=[a2a_yi.ap().opt()], outs=[a2a_yo.ap().opt()]).then_inc(ctx.sem["CC"])
        ctx.bump("CC", 1, tag="c_a2a_y")
        gp.wait_ge(ctx.sem["CC"], ctx.cnt["CC"])

    if "dbg_yi" in g:
        @S_("gpsimd")
        def _(gp):
            ctx.wait(gp, "c_a2a_y")
            gp.dma_start(out=g["dbg_yi"].ap(), in_=a2a_yi.ap()).then_inc(ctx.sem["LDG"], 16)
            ctx.bump("LDG", 16)
            gp.dma_start(out=g["dbg_ar"].ap(), in_=ar_xo.ap()).then_inc(ctx.sem["LDG"], 16)
            ctx.bump("LDG", 16)
            gp.wait_ge(ctx.sem["LDG"], ctx.cnt["LDG"])

    for zh in range(2):
        for zc in range(16):
            ch = zh * 16 + zc

            @S_("sync")
            def _(sync, ch=ch, zc=zc, zh=zh):
                if zh == 0 and zc == 0:
                    ctx.wait(sync, "c_a2a_y")
                if zc == 0:
                    ctx.wait(sync, f"c_ymused_{zh-1}")
                ctx.wait(sync, "c_zs")
                sync.dma_start(out=zy_sl[:, 0:1024], in_=z_dram.ap()[ch * 128:(ch + 1) * 128, :]).then_inc(ctx.sem["LD"], 16)
                ctx.bump("LD", 16)
                sync.dma_start(out=zy_sl[:, 1024:2048], in_=a2a_yo.ap()[ch * 128:(ch + 1) * 128, :]).then_inc(ctx.sem["LD"], 16)
                ctx.bump("LD", 16, tag="c_zyld")

            @S_("scalar")
            def _(scalar, zc=zc):
                ctx.wait(scalar, "c_zyld")
                ctx.wait(scalar, "c_zs_used")
                ctx.inc(scalar.activation(evb[:, 0:1024], zy_sl[:, 0:1024], AF.Silu),
                        "AC", tag="c_zs")

            @S_("vector")
            def _(vector, zc=zc, zh=zh):
                ctx.wait(vector, "c_zs")
                ctx.wait_cnt(vector, "VE")
                n = ctx.inc(vector.tensor_tensor(ym[:, zc * 1024:(zc + 1) * 1024],
                                                 zy_sl[:, 1024:2048], evb[:, 0:1024],
                                                 ALU.mult), "VE", tag="c_ym")
                ctx.tag["c_zs_used"] = ("VE", n)
        ctx.tag[f"c_ym_{zh}"] = ctx.tag["c_ym"]

        for oc in range(16):
            par = "WL" + str(oc % 2)
            wbuf = wo_b[oc % 2]

            @S_("gpsimd")
            def _(gp, oc=oc, zh=zh, par=par, wbuf=wbuf):
                ctx.wait(gp, f"c_ocons_{par}")
                srcw = WopT[zh * 2048:(zh + 1) * 2048, oc * 128:(oc + 1) * 128]
                gp.dma_start(out=wbuf[:].rearrange("p (c o) -> p c o", o=128),
                             in_=srcw.rearrange("(c p) o -> p c o", p=128)).then_inc(ctx.sem[par], 16)
                ctx.bump(par, 16, tag=f"c_old_{zh}_{oc}")

            @S_("tensor")
            def _(tensor, oc=oc, zh=zh, par=par, wbuf=wbuf):
                ctx.wait(tensor, f"c_old_{zh}_{oc}")
                ctx.wait(tensor, f"c_ym_{zh}")
                ctx.wait(tensor, f"c_opev_0")
                ctx.wait(tensor, f"c_opev_1")
                for cc in range(16):
                    for tb in range(2):
                        mm = tensor.matmul(ps_b[:, tb * 512:(tb + 1) * 512],
                                           wbuf[:, cc * 128:(cc + 1) * 128],
                                           ym[:, cc * 1024 + tb * 512:cc * 1024 + (tb + 1) * 512],
                                           start=(cc == 0), stop=(cc == 15))
                n = ctx.inc(mm, "PE", tag=f"c_opmm_{oc % 2}")
                ctx.tag[f"c_ocons_{par}"] = ("PE", n)
                if oc == 15:
                    ctx.tag[f"c_ymused_{zh}"] = ("PE", n)

            @S_("vector")
            def _(vector, oc=oc, zh=zh):
                ctx.wait(vector, f"c_opmm_{oc % 2}")
                ctx.wait_cnt(vector, "VE")
                if zh == 0:
                    ctx.inc(vector.tensor_copy(opacc[:, oc * 1024:(oc + 1) * 1024],
                                               ps_b[:]), "VE", tag=f"c_opev_{oc % 2}")
                else:
                    ctx.inc(vector.tensor_tensor(opacc[:, oc * 1024:(oc + 1) * 1024],
                                                 opacc[:, oc * 1024:(oc + 1) * 1024],
                                                 ps_b[:], ALU.add), "VE",
                            tag=f"c_opev_{oc % 2}")
    ctx.tag["c_opdone"] = ("VE", ctx.cnt["VE"])

    for i in range(NT):
        @S_("tensor")
        def _(tensor, i=i):
            ctx.wait(tensor, "c_opdone")
            ctx.wait(tensor, "c_ottev")
            for fh in range(2):
                for f8 in range(8):
                    f = fh * 8 + f8
                    mm = tensor.transpose(
                        ps_t[:, fh * 1024 + f8 * 128:fh * 1024 + (f8 + 1) * 128],
                        opacc[:, f * 1024 + i * 128:f * 1024 + (i + 1) * 128],
                        ident[:])
                ctx.inc(mm, "PE", tag=f"c_ot_{fh}")

        @S_("sync")
        def _(sync, i=i):
            ctx.wait(sync, "c_hld_used")
            sync.dma_start(out=xnbf[:], in_=xn_dram.ap()[i * 128:(i + 1) * 128, :]).then_inc(ctx.sem["LD"], 16)
            ctx.bump("LD", 16, tag="c_hld")

        @S_("vector")
        def _(vector, i=i):
            ctx.wait(vector, "c_hld")
            for fh in range(2):
                ctx.wait(vector, f"c_ot_{fh}")
                ctx.wait_cnt(vector, "VE")
                ctx.inc(vector.tensor_tensor(dn_tm[:, fh * 1024:(fh + 1) * 1024],
                                             ps_t[:, fh * 1024:(fh + 1) * 1024],
                                             xnbf[:, fh * 1024:(fh + 1) * 1024],
                                             ALU.add), "VE", tag="c_ottev")
            ctx.tag["c_hld_used"] = ("VE", ctx.cnt["VE"])

        emit_post_ln_add(ctx, g, dn_tm, xt, nm, rs, t0, sqscr, xnbf, i,
                         xout_ap, tag=f"cp_{i}", dn_ready_tag="c_ottev")


def emit_phase_D(ctx, g, xin_ap, xout_ap):
    S_ = ctx.sec
    nc = ctx.nc
    xsp = g["xsp"]
    ident = g["ident_sb"]
    H128 = g["H128_sb"]
    sdram = g["sdram"]
    src = xin_ap if xin_ap is not None else xsp

    P = g["phase_stack"].enter_context
    xt = P(nc.sbuf_tensor("d_xt", [128, DM], F32))
    xnbf = P(nc.sbuf_tensor("d_xnbf", [128, DM], BF16))
    sqscr = P(nc.sbuf_tensor("d_sqscr", [128, DM], F32))
    nm = P(nc.sbuf_tensor("d_nm", [128, 1], F32))
    rs = P(nc.sbuf_tensor("d_rs", [128, 1], F32))
    t0 = P(nc.sbuf_tensor("d_t0", [128, 1], F32))
    xq = P(nc.sbuf_tensor("d_xq", [128, 16 * 512], BF16))
    hA = P(nc.sbuf_tensor("d_hA", [128, 64 * 512], BF16))
    hB = P(nc.sbuf_tensor("d_hB", [128, 64 * 512], BF16))
    wq = [P(nc.sbuf_tensor(f"d_wq{j}", [128, 16 * 128], BF16)) for j in range(2)]
    wscr = P(nc.sbuf_tensor("d_wscr", [128, 1024], F32))
    gsil = P(nc.sbuf_tensor("d_gsil", [128, 512], BF16))
    dn_tm = P(nc.sbuf_tensor("d_dntm", [128, DM], BF16))
    qsc = P(nc.sbuf_tensor("d_qsc", [128, 512], F32))
    sbc = P(nc.sbuf_tensor("d_sbc", [128, 512], BF16))
    invbc = P(nc.sbuf_tensor("d_invbc", [128, 512], F32))
    tmax = P(nc.sbuf_tensor("d_tmax", [128, 512], BF16))
    tmin = P(nc.sbuf_tensor("d_tmin", [128, 512], BF16))
    colsc = P(nc.sbuf_tensor("d_colsc", [128, 8], F32))
    colbf = P(nc.sbuf_tensor("d_colbf", [128, 8], BF16))
    rowsc = P(nc.sbuf_tensor("d_rowsc", [1, 512], F32))
    rowbf = P(nc.sbuf_tensor("d_rowbf", [1, 512], BF16))
    swcols = P(nc.sbuf_tensor("d_swcols", [128, 8], F32))
    sacc = P(nc.sbuf_tensor("d_sacc", [128, 2], F32))
    onesb = P(nc.sbuf_tensor("d_onesb", [1, 128], BF16))
    onesf = P(nc.sbuf_tensor("d_onesf", [128, 1], F32))
    ps_t = P(nc.psum_tensor("d_pst", [128, 2048], BF16))
    ps_gu = P(nc.psum_tensor("d_psgu", [128, 1024], F32))
    ps_d = P(nc.psum_tensor("d_psd", [128, 512], F32))
    ps_fw = P(nc.psum_tensor("d_psfw", [128, 1024], F32))
    ps_s = P(nc.psum_tensor("d_pss", [128, 512], F32))

    wdb = hA[:, 8192:16384]
    wqf = [hB[:, 0:4096].bitcast(F32), hB[:, 4096:8192].bitcast(F32)]
    wdf = hA[:, 16384:32768].bitcast(F32)
    dn_fm = hA[:, 0:8192]
    W = {"gt": g["Wg1T"], "up": g["Wu1T"], "dn": g["Wd1T"]}

    @S_("vector")
    def _(vector):
        vector.memset(onesb[:], 1.0)
        ctx.inc(vector.memset(onesf[:], 1.0), "VE", tag="d_ones")

    for wi, wk in enumerate(("gt", "up", "dn")):
        nel = DM * DI

        @S_("vector")
        def _(vector, wi=wi):
            ctx.inc(vector.memset(sacc[:, 0:1], 0.0), "VE", tag=f"d_sz_{wi}")

        for ci in range(16):
            par = "WL" + str(ci % 2)

            @S_("gpsimd")
            def _(gp, wk=wk, ci=ci, wi=wi, par=par):
                ctx.wait(gp, f"d_swuse_{wi}_{ci-2}")
                flat = W[wk].rearrange("r c -> (r c)").rearrange("(n p e) -> n p e",
                                                                 p=128, e=8192)
                dst = hB[:, (ci % 2) * 16384:(ci % 2 + 1) * 16384].bitcast(F32)
                gp.dma_start(out=dst, in_=flat[ci]).then_inc(ctx.sem[par], 16)
                ctx.bump(par, 16, tag=f"d_swld_{wi}_{ci}")

            @S_("vector")
            def _(vector, ci=ci, wi=wi):
                ctx.wait(vector, f"d_swld_{wi}_{ci}")
                ctx.wait(vector, f"d_sz_{wi}")
                ctx.wait_cnt(vector, "VE")
                ctx.inc(vector.tensor_reduce(sacc[:, 1:2],
                                             hB[:, (ci % 2) * 16384:(ci % 2 + 1) * 16384].bitcast(F32),
                                             AX.X, ALU.add, apply_absolute_value=True),
                        "VE", tag=f"d_swuse_{wi}_{ci}")
                ctx.wait_cnt(vector, "VE")
                ctx.inc(vector.tensor_tensor(sacc[:, 0:1], sacc[:, 0:1], sacc[:, 1:2],
                                             ALU.add), "VE", tag=f"d_swsum_{wi}_{ci}")

        @S_("tensor")
        def _(tensor, wi=wi):
            ctx.wait(tensor, f"d_swsum_{wi}_15")
            ctx.wait(tensor, f"d_swbc_done_{wi-1}")
            ctx.wait(tensor, "d_ones")
            ctx.inc(tensor.matmul(ps_s[0:1, 0:1], sacc[:, 0:1], onesf[:],
                                  start=True, stop=True), "PE", tag=f"d_swred_{wi}")

        @S_("vector")
        def _(vector, wi=wi, nel=nel):
            ctx.wait(vector, f"d_swred_{wi}")
            ctx.inc(vector.tensor_scalar(rowsc[0:1, 0:1], ps_s[0:1, 0:1], 1.0 / nel, EPS,
                                         ALU.mult, ALU.max), "VE")
            ctx.wait_cnt(vector, "VE")
            ctx.inc(vector.reciprocal(rowsc[0:1, 1:2], rowsc[0:1, 0:1]), "VE",
                    tag=f"d_swval_{wi}")

        @S_("sync")
        def _(sync, wi=wi):
            ctx.wait(sync, f"d_swval_{wi}")
            sync.dma_start(out=sdram[0:1, 2 * wi:2 * wi + 2], in_=rowsc[0:1, 0:2]).then_inc(ctx.sem["ST"], 16)
            ctx.bump("ST", 16)
            sync.wait_ge(ctx.sem["ST"], ctx.cnt["ST"])
            bsrc = sdram[0:1, 2 * wi:2 * wi + 2].broadcast_to([128, 2])
            sync.dma_start(out=swcols[:, 2 * wi:2 * wi + 2], in_=bsrc).then_inc(ctx.sem["LD"], 16)
            ctx.bump("LD", 16, tag=f"d_swbc_done_{wi}")

    @S_("vector")
    def _(vector):
        for wi in range(3):
            ctx.wait(vector, f"d_swbc_done_{wi}")

    ctx.tag["d_parcons_WL0"] = ("VE", ctx.cnt["VE"])
    ctx.tag["d_parcons_WL1"] = ("VE", ctx.cnt["VE"])

    def swcol(wi):
        return swcols[:, 2 * wi:2 * wi + 1]

    def swinv(wi):
        return swcols[:, 2 * wi + 1:2 * wi + 2]

    def ternarize(srcf_ap, dst_ap, ncols, wi, tg):
        np_ = (ncols + 1023) // 1024
        @S_("vector")
        def _(vector):
            for pp in range(np_):
                c0, c1 = pp * 1024, min(ncols, (pp + 1) * 1024)
                ctx.wait_cnt(vector, "VE")
                ctx.inc(vector.tensor_scalar(wscr[:, 0:c1 - c0], srcf_ap[:, c0:c1],
                                             swinv(wi), RND, ALU.mult, ALU.add), "VE")
                ctx.wait_cnt(vector, "VE")
                ctx.inc(vector.tensor_scalar(wscr[:, 0:c1 - c0], wscr[:, 0:c1 - c0],
                                             RND, -1.0, ALU.subtract, ALU.max), "VE")
                ctx.wait_cnt(vector, "VE")
                ctx.inc(vector.tensor_scalar(dst_ap[:, c0:c1], wscr[:, 0:c1 - c0],
                                             1.0, None, ALU.min), "VE", tag=tg)

    def quantize(Xbuf, nch, lam, tg):
        @S_("vector")
        def _(vector):
            ctx.wait_cnt(vector, "VE")
            vector.tensor_copy(tmax[:], Xbuf[:, 0:512])
            ctx.inc(vector.tensor_copy(tmin[:], Xbuf[:, 0:512]), "VE")
            for ch in range(1, nch):
                ctx.wait_cnt(vector, "VE")
                vector.tensor_tensor(tmax[:], tmax[:], Xbuf[:, ch * 512:(ch + 1) * 512],
                                     ALU.max)
                ctx.inc(vector.tensor_tensor(tmin[:], tmin[:], Xbuf[:, ch * 512:(ch + 1) * 512],
                                             ALU.min), "VE")
            ctx.tag[tg + "_tmax"] = ("VE", ctx.cnt["VE"])

        @S_("tensor")
        def _(tensor):
            ctx.wait(tensor, tg + "_tmax")
            ctx.wait(tensor, tg + "_mxev_prev")
            for st in range(4):
                tensor.transpose(ps_t[:, st * 256:st * 256 + 128],
                                 tmax[:, st * 128:(st + 1) * 128], ident[:])
                ctx.inc(tensor.transpose(ps_t[:, st * 256 + 128:st * 256 + 256],
                                         tmin[:, st * 128:(st + 1) * 128], ident[:]),
                        "PE", tag=tg + f"_mxt_{st}")

        @S_("vector")
        def _(vector):
            ctx.wait(vector, tg + "_mxt_3")
            for st in range(4):
                ctx.inc(vector.tensor_reduce(colsc[:, st:st + 1],
                                             ps_t[:, st * 256:(st + 1) * 256],
                                             AX.X, ALU.max, apply_absolute_value=True),
                        "VE")
            ctx.tag[tg + "_mxev_prev"] = ("VE", ctx.cnt["VE"])
            for st in range(4):
                ctx.wait_cnt(vector, "VE")
                ctx.inc(vector.tensor_scalar(colbf[:, st:st + 1], colsc[:, st:st + 1],
                                             EPS / lam, None, ALU.max), "VE",
                        tag=tg + f"_mxbf_{st}")

        @S_("tensor")
        def _(tensor):
            ctx.wait(tensor, tg + "_mxbf_3")
            ctx.wait(tensor, tg + "_rowev_prev")
            for st in range(4):
                ctx.inc(tensor.matmul(ps_s[0:1, st * 128:(st + 1) * 128],
                                      colbf[:, st:st + 1],
                                      ident[:], start=True, stop=True), "PE",
                        tag=tg + f"_mxrow_{st}")

        @S_("vector")
        def _(vector):
            ctx.wait(vector, tg + "_mxrow_3")
            ctx.inc(vector.tensor_copy(rowbf[0:1, 0:512], ps_s[0:1, 0:512]), "VE",
                    tag=tg + "_rowev_prev")

        @S_("tensor")
        def _(tensor):
            ctx.wait(tensor, tg + "_rowev_prev")
            ctx.wait(tensor, tg + "_bcev_prev")
            ctx.inc(tensor.matmul(ps_s[:, 0:512], onesb[:], rowbf[0:1, 0:512],
                                  start=True, stop=True), "PE", tag=tg + "_bc")

        @S_("vector")
        def _(vector, lam=lam):
            ctx.wait(vector, tg + "_bc")
            ctx.inc(vector.tensor_copy(qsc[:], ps_s[:, 0:512]), "VE",
                    tag=tg + "_bcev_prev")
            ctx.wait_cnt(vector, "VE")
            ctx.inc(vector.reciprocal(invbc[:], qsc[:]), "VE")
            ctx.wait_cnt(vector, "VE")
            ctx.inc(vector.tensor_scalar(sbc[:], invbc[:], 127.0, None, ALU.mult), "VE")
            ctx.wait_cnt(vector, "VE")
            ctx.inc(vector.tensor_scalar(invbc[:], qsc[:], lam / 127.0, None, ALU.mult),
                    "VE", tag=tg + "_sc")

        @S_("vector")
        def _(vector):
            ctx.wait(vector, tg + "_sc")
            for ch in range(nch):
                xap = Xbuf[:, ch * 512:(ch + 1) * 512]
                ctx.wait_cnt(vector, "VE")
                ctx.inc(vector.tensor_tensor(qsc[:], xap, sbc[:], ALU.mult), "VE")
                ctx.wait_cnt(vector, "VE")
                ctx.inc(vector.tensor_scalar(qsc[:], qsc[:], RND, RND, ALU.add,
                                             ALU.subtract), "VE")
                ctx.wait_cnt(vector, "VE")
                ctx.inc(vector.tensor_scalar(qsc[:], qsc[:], -128.0, 127.0, ALU.max,
                                             ALU.min), "VE")
                ctx.wait_cnt(vector, "VE")
                ctx.inc(vector.tensor_tensor(xap, qsc[:], invbc[:], ALU.mult), "VE")
            ctx.tag[tg + "_done"] = ("VE", ctx.cnt["VE"])

    def fwht(Xsrc, Xdst, nch, tg):
        nst = int(math.log2(nch))
        ctx.tag[tg + "_fwev_-2"] = ctx.tag.get("d_psfw_last", ("VE", 0))
        ctx.tag[tg + "_fwev_-1"] = ctx.tag.get("d_psfw_last", ("VE", 0))
        for ch in range(nch):
            bank = ch % 2

            @S_("tensor")
            def _(tensor, ch=ch, bank=bank):
                if ch == 0:
                    ctx.wait(tensor, tg + "_src_ready")
                ctx.wait(tensor, tg + f"_fwev_{ch-2}")
                ctx.inc(tensor.matmul(ps_fw[:, bank * 512:(bank + 1) * 512], H128[:],
                                      Xsrc[:, ch * 512:(ch + 1) * 512],
                                      start=True, stop=True), "PE", tag=tg + f"_fwmm_{ch}")

            @S_("vector")
            def _(vector, ch=ch, bank=bank):
                ctx.wait(vector, tg + f"_fwmm_{ch}")
                ctx.inc(vector.tensor_copy(Xdst[:, ch * 512:(ch + 1) * 512],
                                           ps_fw[:, bank * 512:(bank + 1) * 512]), "VE",
                        tag=tg + f"_fwev_{ch}")
        ctx.tag[tg + "_stA"] = ctx.tag[tg + f"_fwev_{nch-1}"]
        ctx.tag["d_psfw_last"] = ctx.tag[tg + "_stA"]

        bufs = [Xdst, Xsrc]
        for st in range(nst):
            d = 1 << st
            src_b = bufs[st % 2]
            dst_b = bufs[(st + 1) % 2]
            gcount = nch // (2 * d)

            @S_("vector")
            def _(vector, d=d, src_b=src_b, dst_b=dst_b, gcount=gcount, st=st):
                ctx.wait(vector, tg + ("_stA" if st == 0 else f"_st{st-1}"))
                sv = src_b[:].rearrange("p (g two dd e) -> p g two dd e", two=2, dd=d,
                                        e=512)[:, 0:gcount]
                dv = dst_b[:].rearrange("p (g two dd e) -> p g two dd e", two=2, dd=d,
                                        e=512)[:, 0:gcount]
                ctx.inc(vector.tensor_tensor(dv[:, :, 0], sv[:, :, 0], sv[:, :, 1],
                                             ALU.add), "VE")
                ctx.inc(vector.tensor_tensor(dv[:, :, 1], sv[:, :, 0], sv[:, :, 1],
                                             ALU.subtract), "VE", tag=tg + f"_st{st}")

        assert nst % 2 == 0, "result must land back in Xdst"
        ctx.tag[tg + "_done"] = ctx.tag[tg + f"_st{nst-1}"]

    for hf in range(2):
        base = hf * 4

        for tt in range(4):
            i = base + tt

            @S_("sync")
            def _(sync, i=i):
                ctx.wait_cnt(sync, "ST")
                ctx.wait(sync, f"d_xn_{i-1}")
                sync.dma_start(out=xt[:], in_=src[i * 128:(i + 1) * 128, :]).then_inc(ctx.sem["LD"], 16)
                ctx.bump("LD", 16, tag=f"d_ld_{i}")
                if xin_ap is not None:
                    sync.wait_ge(ctx.sem["LD"], ctx.cnt["LD"])
                    sync.dma_start(out=xsp[i * 128:(i + 1) * 128, :], in_=xt[:]).then_inc(ctx.sem["ST"], 16)
                    ctx.bump("ST", 16, tag=f"d_sp_{i}")

            @S_("vector")
            def _(vector, i=i):
                ctx.wait(vector, f"d_ld_{i}")

            @S_("scalar")
            def _(scalar, i=i):
                ctx.wait(scalar, f"d_ld_{i}")

            _ln_sections(ctx, g, xt[:], nm, rs, t0, sqscr, DM, f"d_ln_{i}")

            @S_("vector")
            def _(vector, i=i):
                ctx.wait(vector, f"d_ln_{i}_rs")
                ctx.wait(vector, f"d_tr_{i-1}")
                ctx.inc(vector.tensor_scalar(xnbf[:], xt[:], nm[:], rs[:], ALU.add,
                                             ALU.mult), "VE", tag=f"d_xn_{i}")

            @S_("tensor")
            def _(tensor, i=i, tt=tt, hf=hf):
                ctx.wait(tensor, f"d_xn_{i}")
                ctx.wait(tensor, f"d_trev_{i-1}")
                if tt == 0:
                    ctx.wait(tensor, f"d_xqfree_{hf}")
                for fh in range(2):
                    for f8 in range(8):
                        f = fh * 8 + f8
                        mm = tensor.transpose(
                            ps_t[:, fh * 1024 + f8 * 128:fh * 1024 + (f8 + 1) * 128],
                            xnbf[:, f * 128:(f + 1) * 128], ident[:])
                    ctx.inc(mm, "PE", tag=f"d_trp_{i}_{fh}")
                ctx.tag[f"d_tr_{i}"] = ctx.tag[f"d_trp_{i}_1"]

            @S_("vector")
            def _(vector, i=i, tt=tt):
                for fh in range(2):
                    ctx.wait(vector, f"d_trp_{i}_{fh}")
                    dst = xq[:].rearrange("p (f t) -> p f t", t=512)[:, fh * 8:(fh + 1) * 8,
                                                                    tt * 128:(tt + 1) * 128]
                    srcv = ps_t[:, fh * 1024:(fh + 1) * 1024].rearrange("p (f t) -> p f t",
                                                                       t=128)
                    ctx.inc(vector.tensor_copy(dst, srcv), "VE", tag=f"d_trev_{i}")
        ctx.tag[f"d_xq_ready_{hf}"] = ctx.tag[f"d_trev_{base+3}"]

        x2A = hB[:, 0:8192]
        ctx.tag[f"dfw2_{hf}_src_ready"] = ctx.tag[f"d_xq_ready_{hf}"]
        fwht(xq, x2A, 16, f"dfw2_{hf}")

        ctx.tag[f"dq2_{hf}_mxev_prev"] = ctx.tag.get("d_mxev_last", ("VE", 0))
        ctx.tag[f"dq2_{hf}_rowev_prev"] = ctx.tag.get("d_rowev_last", ("VE", 0))
        ctx.tag[f"dq2_{hf}_bcev_prev"] = ctx.tag.get("d_bcev_last", ("VE", 0))
        quantize(x2A, 16, LAM11, f"dq2_{hf}")
        ctx.tag["d_mxev_last"] = ctx.tag[f"dq2_{hf}_mxev_prev"]
        ctx.tag["d_rowev_last"] = ctx.tag[f"dq2_{hf}_rowev_prev"]
        ctx.tag["d_bcev_last"] = ctx.tag[f"dq2_{hf}_bcev_prev"]

        @S_("vector")
        def _(vector, hf=hf, x2A=x2A):
            ctx.wait(vector, f"dq2_{hf}_done")
            ctx.wait(vector, f"d_xq_ready_{hf}")
            ctx.inc(vector.tensor_copy(xq[:], x2A), "VE", tag=f"d_xqs_{hf}")

        ctx.tag["d_parcons_WL0"] = ctx.tag[f"d_xqs_{hf}"]
        ctx.tag["d_parcons_WL1"] = ctx.tag[f"d_xqs_{hf}"]
        for oc in range(64):
            for gu in range(2):
                wi = gu
                wbuf = wq[(oc * 2 + gu) % 2]
                par = "WL" + str((oc * 2 + gu) % 2)
                wt = W["gt"] if gu == 0 else W["up"]

                wfbuf = wqf[(oc * 2 + gu) % 2]

                @S_("gpsimd")
                def _(gp, wfbuf=wfbuf, oc=oc, gu=gu, hf=hf, par=par, wt=wt):
                    ctx.wait(gp, f"d_parcons_{par}")
                    srcw = wt[:, oc * 128:(oc + 1) * 128].rearrange("(c p) o -> p c o",
                                                                    p=128)
                    gp.dma_start(out=wfbuf.rearrange("p (c o) -> p c o", o=128),
                                 in_=srcw).then_inc(ctx.sem[par], 16)
                    ctx.bump(par, 16, tag=f"d_wld_{hf}_{oc}_{gu}")

                @S_("vector")
                def _(vector, hf=hf, oc=oc, gu=gu):
                    ctx.wait(vector, f"d_wld_{hf}_{oc}_{gu}")
                    ctx.wait(vector, f"d_tern_pe_{par}")

                ternarize(wfbuf, wbuf[:], 2048, gu, f"d_tern_{hf}_{oc}_{gu}")

                @S_("tensor")
                def _(tensor, wbuf=wbuf, oc=oc, gu=gu, hf=hf, par=par):
                    ctx.wait(tensor, f"d_tern_{hf}_{oc}_{gu}")
                    if oc == 0 and gu == 0:
                        ctx.wait(tensor, f"d_xqs_{hf}")
                    ctx.wait(tensor, "d_silu_used" if gu == 0 else "d_h_written")
                    for fc in range(16):
                        mm = tensor.matmul(ps_gu[:, gu * 512:gu * 512 + 512],
                                           wbuf[:, fc * 128:(fc + 1) * 128],
                                           xq[:, fc * 512:(fc + 1) * 512],
                                           start=(fc == 0), stop=(fc == 15))
                    n = ctx.inc(mm, "PE", tag=f"d_gu_{gu}")
                    ctx.tag[f"d_parcons_{par}"] = ("PE", n)
                    ctx.tag[f"d_tern_pe_{par}"] = ("PE", n)

                if gu == 0:
                    @S_("scalar")
                    def _(scalar):
                        ctx.wait(scalar, "d_gu_0")
                        ctx.wait(scalar, "d_silu_read")
                        ctx.inc(scalar.activation(gsil[:], ps_gu[:, 0:512], AF.Silu,
                                                  scale=swcol(0)), "AC",
                                tag="d_silu_used")
                else:
                    @S_("vector")
                    def _(vector, oc=oc):
                        ctx.wait(vector, "d_gu_1")
                        ctx.wait(vector, "d_silu_used")
                        ctx.wait(vector, f"d_hfree_{hf}")
                        n = ctx.inc(vector.scalar_tensor_tensor(
                            hA[:, oc * 512:(oc + 1) * 512], ps_gu[:, 512:1024],
                            swcol(1), gsil[:], ALU.mult, ALU.mult), "VE",
                            tag="d_h_written")
                        ctx.tag["d_silu_read"] = ("VE", n)
        ctx.tag[f"d_xqfree_{hf+1}"] = ctx.tag["d_gu_1"]
        ctx.tag[f"d_h_ready_{hf}"] = ctx.tag["d_h_written"]

        ctx.tag[f"dfw8_{hf}_src_ready"] = ctx.tag[f"d_h_ready_{hf}"]
        fwht(hA, hB, 64, f"dfw8_{hf}")
        ctx.tag[f"dq8_{hf}_mxev_prev"] = ctx.tag["d_mxev_last"]
        ctx.tag[f"dq8_{hf}_rowev_prev"] = ctx.tag["d_rowev_last"]
        ctx.tag[f"dq8_{hf}_bcev_prev"] = ctx.tag["d_bcev_last"]
        quantize(hB, 64, LAM12, f"dq8_{hf}")
        ctx.tag["d_mxev_last"] = ctx.tag[f"dq8_{hf}_mxev_prev"]
        ctx.tag["d_rowev_last"] = ctx.tag[f"dq8_{hf}_rowev_prev"]
        ctx.tag["d_bcev_last"] = ctx.tag[f"dq8_{hf}_bcev_prev"]

        ctx.tag["d_dn_pe"] = ctx.tag[f"dfw8_{hf}_st5"]
        for o2 in range(16):
            @S_("gpsimd")
            def _(gp, o2=o2, hf=hf):
                ctx.wait(gp, "d_dn_pe")
                srcw = W["dn"][:, o2 * 128:(o2 + 1) * 128].rearrange("(c p) o -> p c o",
                                                                     p=128)
                gp.dma_start(out=wdf.rearrange("p (c o) -> p c o", o=128),
                             in_=srcw).then_inc(ctx.sem["WL0"], 16)
                ctx.bump("WL0", 16, tag=f"d_wdld_{hf}_{o2}")

            @S_("vector")
            def _(vector, hf=hf, o2=o2):
                ctx.wait(vector, f"d_wdld_{hf}_{o2}")
                ctx.wait(vector, "d_dn_pe")

            ternarize(wdf, wdb, 8192, 2, f"d_dtern_{hf}_{o2}")

            @S_("tensor")
            def _(tensor, o2=o2, hf=hf):
                ctx.wait(tensor, f"d_dtern_{hf}_{o2}")
                if o2 == 0:
                    ctx.wait(tensor, f"dq8_{hf}_done")
                ctx.wait(tensor, "d_dnev")
                for ic in range(64):
                    mm = tensor.matmul(ps_d[:], wdb[:, ic * 128:(ic + 1) * 128],
                                       hB[:, ic * 512:(ic + 1) * 512],
                                       start=(ic == 0), stop=(ic == 63))
                n = ctx.inc(mm, "PE", tag="d_dnmm")
                ctx.tag["d_dn_pe"] = ("PE", n)

            @S_("vector")
            def _(vector, o2=o2):
                ctx.wait(vector, "d_dnmm")
                ctx.wait(vector, "d_dnfm_free")
                ctx.inc(vector.tensor_scalar(dn_fm[:, o2 * 512:(o2 + 1) * 512], ps_d[:],
                                             swcol(2), None, ALU.mult), "VE",
                        tag="d_dnev")
        ctx.tag["d_parcons_WL0"] = ctx.tag["d_dnmm"]

        for tt in range(4):
            i = base + tt

            @S_("tensor")
            def _(tensor, tt=tt, hf=hf):
                ctx.wait(tensor, "d_dnev")
                ctx.wait(tensor, "d_dtev")
                for fh in range(2):
                    for f8 in range(8):
                        f = fh * 8 + f8
                        mm = tensor.transpose(
                            ps_t[:, fh * 1024 + f8 * 128:fh * 1024 + (f8 + 1) * 128],
                            dn_fm[:, f * 512 + tt * 128:f * 512 + (tt + 1) * 128],
                            ident[:])
                    ctx.inc(mm, "PE", tag=f"d_dtp_{fh}")
                ctx.tag["d_dnfm_free"] = ctx.tag["d_dtp_1"]

            @S_("vector")
            def _(vector, tt=tt):
                for fh in range(2):
                    ctx.wait(vector, f"d_dtp_{fh}")
                    ctx.inc(vector.tensor_copy(dn_tm[:, fh * 1024:(fh + 1) * 1024],
                                               ps_t[:, fh * 1024:(fh + 1) * 1024]),
                            "VE", tag="d_dtev")

            emit_post_ln_add(ctx, g, dn_tm, xt, nm, rs, t0, sqscr, xnbf, i,
                             xout_ap, tag=f"dp_{i}", dn_ready_tag="d_dtev")
        ctx.tag[f"d_hfree_{hf+1}"] = ctx.tag["d_dtp_1"]



def build_program(phases=PHASES):
    nc = bass.Bass("TRN2", target_bir_lowering=False)

    ap = {}

    def inp(name, shape, dt=F32):
        ap[name] = nc.dram_tensor(name, list(shape), dt, kind="ExternalInput").ap()

    inp("xin", (TPC, DM))
    inp("ident", (128, 128), BF16)
    if "A" in phases:
        inp("WqT", (DM, DM)); inp("WkT", (DM, 512)); inp("WvT", (DM, 512)); inp("WoT", (DM, DM))
        inp("SD", (72 * 128, 512), BF16)
    if "B" in phases:
        inp("WguT", (DM, 2 * DI)); inp("Wd0T", (DI, DM))
    if "C" in phases:
        CS = DIN // NCORE
        inp("WinT", (DM, EM)); inp("convw", (CS, 4)); inp("convb", (CS, 1))
        inp("WxpT", (CS, 160)); inp("WdtT", (DTR, CS)); inp("dtb", (CS, 1))
        inp("WopT", (DIN, DM)); inp("Alog", (CS, DST)); inp("Dvec", (CS, 1))
        inp("nsc", (DM, 1))
    if "D" in phases:
        inp("Wg1T", (DM, DI)); inp("Wu1T", (DM, DI)); inp("Wd1T", (DI, DM))
        inp("H128", (128, 128), BF16)

    xout = nc.dram_tensor("xout", [TPC, DM], F32, kind="ExternalOutput").ap()
    g = dict(ap)
    g["xsp"] = nc.dram_tensor("xsp", [TPC, DM], F32).ap()
    if "A" in phases:
        g["kag_in"] = nc.dram_tensor("kag_in", [512, TPC], BF16)
        g["kag_out"] = nc.dram_tensor("kag_out", [8 * 512, TPC], BF16)
        g["vag_in"] = nc.dram_tensor("vag_in", [TPC, 512], BF16)
        g["vag_out"] = nc.dram_tensor("vag_out", [8 * TPC, 512], BF16)
    if "C" in phases:
        kindd = "ExternalOutput" if DEBUG_C else "Internal"
        g["a2a_xi"] = nc.dram_tensor("a2a_xi", [DIN, TPC], BF16)
        g["a2a_xo"] = nc.dram_tensor("a2a_xo", [DIN, TPC], BF16)
        g["ar_xi"] = nc.dram_tensor("ar_xi", [160, S], F32)
        g["ar_xo"] = nc.dram_tensor("ar_xo", [160, S], F32)
        g["a2a_yi"] = nc.dram_tensor("a2a_yi", [DIN, TPC], BF16)
        g["a2a_yo"] = nc.dram_tensor("a2a_yo", [DIN, TPC], BF16)
        g["z_dram"] = nc.dram_tensor("z_dram", [DIN, TPC], BF16, kind=kindd)
        g["xn_dram"] = nc.dram_tensor("xn_dram", [TPC, DM], BF16)
        g["xc_dr"] = nc.dram_tensor("xc_dr", [512, S], BF16, kind=kindd)
        g["bc_dr"] = nc.dram_tensor("bc_dr", [32 * 128, S], BF16)
        if DEBUG_C:
            g["dbg_yi"] = nc.dram_tensor("dbg_yi", [DIN, TPC], BF16, kind="ExternalOutput")
            g["dbg_ar"] = nc.dram_tensor("dbg_ar", [160, S], F32, kind="ExternalOutput")

    with contextlib.ExitStack() as stack:
        E = stack.enter_context
        sem = {k: E(nc.semaphore(k)) for k in
               ("LD", "LDG", "PE", "VE", "AC", "PO", "ST", "CC", "WL0", "WL1",
                "KL0", "KL1", "VL0", "VL1", "SD0", "SD1")}
        block = E(nc.Block())
        ctx = Ctx(nc, block, sem)
        ident_sb = E(nc.sbuf_tensor("ident_sb", [128, 128], BF16))
        g["ident_sb"] = ident_sb
        kconst = E(nc.sbuf_tensor("kconst", [128, 8], F32))
        g["cEPS"] = kconst[:, 0:1]
        g["cONE"] = kconst[:, 1:2]

        @block.sync
        def _(sync):
            sync.dma_start(out=ident_sb[:], in_=ap["ident"]).then_inc(sem["LD"], 16)
            ctx.bump("LD", 16)

        @block.vector
        def _(vector):
            vector.memset(kconst[:, 0:1], EPS)
            ctx.inc(vector.memset(kconst[:, 1:2], 1.0), "VE", tag="konst")

        if "D" in phases:
            H128_sb = E(nc.sbuf_tensor("H128_sb", [128, 128], BF16))
            g["H128_sb"] = H128_sb
            g["sdram"] = nc.dram_tensor("sdram", [1, 16], F32).ap()

            @block.sync
            def _(sync):
                sync.dma_start(out=H128_sb[:], in_=ap["H128"]).then_inc(sem["LD"], 16)
                ctx.bump("LD", 16)

        emitters = {"A": emit_phase_A, "B": emit_phase_B, "C": emit_phase_C,
                    "D": emit_phase_D}
        for pi, ph in enumerate(phases):
            with contextlib.ExitStack() as phase_stack:
                g["phase_stack"] = phase_stack
                xin_ap = ap["xin"] if pi == 0 else None
                xout_ap = xout if pi == len(phases) - 1 else None
                if pi > 0:
                    for eng in ("sync", "vector", "scalar", "tensor", "gpsimd"):
                        @getattr(block, eng)
                        def _(e):
                            for k in ("LD", "LDG", "PE", "VE", "AC", "ST"):
                                e.wait_ge(sem[k], ctx.cnt[k])
                emitters[ph](ctx, g, xin_ap, xout_ap)

        @block.sync
        def _(sync):
            sync.wait_ge(sem["ST"], ctx.cnt["ST"])

    return nc



def _hadamard(n):
    H = np.array([[1.0]])
    while H.shape[0] < n:
        H = np.block([[H, H], [H, -H]])
    return H


def _sd_table(c):
    out = np.zeros((72, 128, 512), np.float32)
    idx = 0
    for i in range(NT):
        q0 = (8 * i + c) * 128
        q = q0 + np.arange(128)[:, None]
        for kc in range(2 * (i + 1)):
            k = kc * 512 + np.arange(512)[None, :]
            d = q - k
            sd = np.sqrt(np.maximum(d, 0).astype(np.float64)).astype(np.float32)
            sd = np.where(d < 0, np.float32(1e30), sd)
            out[idx] = sd
            idx += 1
    return out.reshape(72 * 128, 512).astype(ml_dtypes.bfloat16)


def host_inputs(inputs, phases):
    x = np.ascontiguousarray(np.asarray(inputs["L0_attn_x"], dtype=np.float32).reshape(S, DM))
    xt = x.reshape(64, 128, DM)
    base = {"ident": np.eye(128).astype(ml_dtypes.bfloat16)}
    if "A" in phases:
        base["WqT"] = np.ascontiguousarray(np.asarray(inputs["L0_attn_Wq"]).T)
        base["WkT"] = np.ascontiguousarray(np.asarray(inputs["L0_attn_Wk"]).T)
        base["WvT"] = np.ascontiguousarray(np.asarray(inputs["L0_attn_Wv"]).T)
        base["WoT"] = np.ascontiguousarray(np.asarray(inputs["L0_attn_Wo"]).T)
    if "B" in phases:
        base["WguT"] = np.ascontiguousarray(np.asarray(inputs["L0_ffn_w_gate_up"]).T)
        base["Wd0T"] = np.ascontiguousarray(np.asarray(inputs["L0_ffn_w_down"]).T)
    if "C" in phases:
        base["WinT"] = np.ascontiguousarray(np.asarray(inputs["L1_attn_in_proj_w"]).T)
        base["WopT"] = np.ascontiguousarray(np.asarray(inputs["L1_attn_out_proj_w"]).T)
        base["nsc"] = np.ascontiguousarray(np.asarray(inputs["L1_attn_norm_scale"]).reshape(DM, 1))
    if "D" in phases:
        base["Wg1T"] = np.ascontiguousarray(np.asarray(inputs["L1_ffn_w_gate"]).T)
        base["Wu1T"] = np.ascontiguousarray(np.asarray(inputs["L1_ffn_w_up"]).T)
        base["Wd1T"] = np.ascontiguousarray(np.asarray(inputs["L1_ffn_w_down"]).T)
        base["H128"] = _hadamard(128).astype(ml_dtypes.bfloat16)
    per = []
    for c in range(NCORE):
        m = dict(base)
        m["xin"] = np.ascontiguousarray(xt[c::8].reshape(TPC, DM))
        if "A" in phases:
            m["SD"] = _sd_table(c)
        if "C" in phases:
            CS = DIN // NCORE
            sl = slice(c * CS, (c + 1) * CS)
            m["convw"] = np.ascontiguousarray(np.asarray(inputs["L1_attn_conv_w"]).reshape(DIN, 4)[sl])
            m["convb"] = np.ascontiguousarray(np.asarray(inputs["L1_attn_conv_b"]).reshape(DIN, 1)[sl])
            m["WxpT"] = np.ascontiguousarray(np.asarray(inputs["L1_attn_x_proj_w"]).T[sl])
            m["WdtT"] = np.ascontiguousarray(np.asarray(inputs["L1_attn_dt_proj_w"]).T[:, sl])
            m["dtb"] = np.ascontiguousarray(np.asarray(inputs["L1_attn_dt_proj_b"]).reshape(DIN, 1)[sl])
            m["Alog"] = np.ascontiguousarray(np.asarray(inputs["L1_attn_A_log"])[sl])
            m["Dvec"] = np.ascontiguousarray(np.asarray(inputs["L1_attn_D"]).reshape(DIN, 1)[sl])
        per.append(m)
    return per


def assemble(results, key="xout"):
    full = np.zeros((64, 128, DM), np.float32)
    for c in range(NCORE):
        full[c::8] = np.asarray(results[c][key], dtype=np.float32).reshape(NT, 128, DM)
    return full.reshape(1, S, DM)


def kernel(**inputs):
    if FUSED:
        nc = build_program(PHASES)
        in_maps = host_inputs(inputs, PHASES)
        res = run_bass_kernel_spmd(nc, in_maps, core_ids=list(range(NCORE)))
        return _cast_out(assemble(res.results))
    x = None
    for ph in PHASES:
        nc = build_program((ph,))
        in_maps = host_inputs(inputs, (ph,))
        if x is not None:
            xt = x.reshape(64, 128, DM)
            for c in range(NCORE):
                in_maps[c]["xin"] = np.ascontiguousarray(xt[c::8].reshape(TPC, DM))
        res = run_bass_kernel_spmd(nc, in_maps, core_ids=list(range(NCORE)))
        x = assemble(res.results).reshape(S, DM)
    return _cast_out(x.reshape(1, S, DM))


def _cast_out(x):
    return np.ascontiguousarray(x, dtype=np.float32)


# version 05
# speedup vs baseline: 1.1494x; 1.1337x over previous
import contextlib
import math

import numpy as np
import ml_dtypes
import concourse.bass as bass
import concourse.mybir as mybir
from concourse.bass_utils import run_bass_kernel_spmd
from concourse.aot_env import allow_ici_all_to_all

F32 = mybir.dt.float32
BF16 = mybir.dt.bfloat16
AF = mybir.ActivationFunctionType
ALU = mybir.AluOpType
AX = mybir.AxisListType

NCORE = 8
S = 8192
DM = 2048
TPC = 1024
NT = 8
DI = 8192
EM = 8192
DIN = 4096
DST = 16
DTR = 128
HEADS = 16
GRP = 4
HD = 128
RND = 12582912.0
EPS = 1e-5
LAM11 = 1.0 / math.sqrt(2048.0)
LAM12 = 1.0 / math.sqrt(8192.0)

PHASES = ("A", "B", "C", "D")
FUSED = True
DEBUG_C = False



class Ctx:
    def __init__(self, nc, block, sem):
        self.nc = nc
        self.block = block
        self.sem = sem
        self.cnt = {k: 0 for k in sem}
        self.tag = {}
        self.hwm = {}

    def sec(self, eng):
        return getattr(self.block, eng)

    def inc(self, instr, key, n=1, tag=None):
        instr.then_inc(self.sem[key], n)
        self.cnt[key] += n
        if tag is not None:
            self.tag[tag] = (key, self.cnt[key])
        return self.cnt[key]

    def bump(self, key, n, tag=None):
        self.cnt[key] += n
        if tag is not None:
            self.tag[tag] = (key, self.cnt[key])
        return self.cnt[key]

    def _wait_raw(self, eng, key, v):
        ek = (id(eng.engine) if hasattr(eng, "engine") else id(eng), key)
        if self.hwm.get(ek, -1) >= v:
            return
        self.hwm[ek] = v
        eng.wait_ge(self.sem[key], v)

    def wait(self, eng, tag):
        if tag in self.tag:
            key, v = self.tag[tag]
            self._wait_raw(eng, key, v)

    def wait_cnt(self, eng, key):
        self._wait_raw(eng, key, self.cnt[key])



def _ln_sections(ctx, g, src_ap, nm, rs, t0, sqscr, n_feat, tag, rms_only=False):
    S_ = ctx.sec

    @S_("vector")
    def _(vector):
        if not rms_only:
            ctx.inc(vector.tensor_reduce(nm[:], src_ap, AX.X, ALU.add),
                    "VE", tag=tag + "_sum")

    @S_("scalar")
    def _(scalar):
        ctx.inc(scalar.activation(sqscr[:, 0:n_feat], src_ap, AF.Square,
                                  accum_out=t0[:]), "AC", tag=tag + "_sq")

    @S_("vector")
    def _(vector):
        ctx.wait(vector, tag + "_sq")
        if rms_only:
            ctx.inc(vector.tensor_scalar(t0[:], t0[:], 1.0 / n_feat, None, ALU.mult),
                    "VE", tag=tag + "_var")
        else:
            ctx.wait(vector, tag + "_sum")
            ctx.inc(vector.tensor_scalar(nm[:], nm[:], -1.0 / n_feat, None, ALU.mult), "VE")
            ctx.wait_cnt(vector, "VE")
            ctx.inc(vector.tensor_tensor(rs[:], nm[:], nm[:], ALU.mult), "VE")
            ctx.wait_cnt(vector, "VE")
            ctx.inc(vector.scalar_tensor_tensor(t0[:], t0[:], 1.0 / n_feat, rs[:],
                                                ALU.mult, ALU.subtract),
                    "VE", tag=tag + "_var")

    @S_("scalar")
    def _(scalar):
        ctx.wait(scalar, tag + "_var")
        ctx.inc(scalar.activation(t0[:], t0[:], AF.Ln, bias=g["cEPS"]), "AC")
        ctx.wait_cnt(scalar, "AC")
        ctx.inc(scalar.activation(rs[:], t0[:], AF.Exp, scale=-0.5),
                "AC", tag=tag + "_rs")



def emit_phase_B(ctx, g, xin_ap, xout_ap):
    S_ = ctx.sec
    xsp = g["xsp"]
    WguS, Wd0S = g["WguS"], g["Wd0S"]
    ident = g["ident_sb"]
    src = xin_ap if xin_ap is not None else xsp

    P = g["phase_stack"].enter_context
    nc = ctx.nc
    xt = P(nc.sbuf_tensor("b_xt", [128, DM], F32))
    xnbf = P(nc.sbuf_tensor("b_xnbf", [128, DM], BF16))
    sqscr = P(nc.sbuf_tensor("b_sqscr", [128, DM], F32))
    nm = P(nc.sbuf_tensor("b_nm", [128, 1], F32))
    rs = P(nc.sbuf_tensor("b_rs", [128, 1], F32))
    t0 = P(nc.sbuf_tensor("b_t0", [128, 1], F32))
    xn2T = P(nc.sbuf_tensor("b_xn2T", [128, 16 * TPC], BF16))
    wg = [P(nc.sbuf_tensor(f"b_wg{j}", [128, 16 * 128], BF16)) for j in range(2)]
    wd = [P(nc.sbuf_tensor(f"b_wd{j}", [128, 64 * 128], BF16)) for j in range(2)]
    h_sb = P(nc.sbuf_tensor("b_h", [128, 64 * 512], BF16))
    silu_sb = P(nc.sbuf_tensor("b_silu", [128, 512], BF16))
    dn_fm = P(nc.sbuf_tensor("b_dnfm", [128, 16 * 512], BF16))
    dn_tm = P(nc.sbuf_tensor("b_dntm", [128, DM], BF16))
    ps_t = P(nc.psum_tensor("b_pst", [128, 2048], BF16))
    ps_gu = P(nc.psum_tensor("b_psgu", [128, 1024], F32))
    ps_d = P(nc.psum_tensor("b_psd", [128, 512], F32))

    for i in range(NT):
        @S_("sync")
        def _(sync, i=i):
            ctx.wait(sync, f"b_xn_{i-1}")
            ctx.wait(sync, f"b_sp_{i-1}")
            sync.dma_start(out=xt[:], in_=src[i * 128:(i + 1) * 128, :]).then_inc(ctx.sem["LD"], 16)
            ctx.bump("LD", 16, tag=f"b_ld_{i}")
            if xin_ap is not None:
                sync.wait_ge(ctx.sem["LD"], ctx.cnt["LD"])
                sync.dma_start(out=xsp[i * 128:(i + 1) * 128, :], in_=xt[:]).then_inc(ctx.sem["ST"], 16)
                ctx.bump("ST", 16, tag=f"b_sp_{i}")

        @S_("vector")
        def _(vector, i=i):
            ctx.wait(vector, f"b_ld_{i}")

        @S_("scalar")
        def _(scalar, i=i):
            ctx.wait(scalar, f"b_ld_{i}")

        _ln_sections(ctx, g, xt[:], nm, rs, t0, sqscr, DM, f"b_ln_{i}")

        @S_("vector")
        def _(vector, i=i):
            ctx.wait(vector, f"b_ln_{i}_rs")
            ctx.wait(vector, f"b_tr_{i-1}")
            ctx.inc(vector.tensor_scalar(xnbf[:], xt[:], nm[:], rs[:], ALU.add, ALU.mult),
                    "VE", tag=f"b_xn_{i}")

        @S_("tensor")
        def _(tensor, i=i):
            ctx.wait(tensor, f"b_xn_{i}")
            ctx.wait(tensor, f"b_trev_{i-1}")
            for fh in range(2):
                for f8 in range(8):
                    f = fh * 8 + f8
                    mm = tensor.transpose(ps_t[:, fh * 1024 + f8 * 128:fh * 1024 + (f8 + 1) * 128],
                                          xnbf[:, f * 128:(f + 1) * 128], ident[:])
                ctx.inc(mm, "PE", tag=f"b_trp_{i}_{fh}")
            ctx.tag[f"b_tr_{i}"] = ctx.tag[f"b_trp_{i}_1"]

        @S_("vector")
        def _(vector, i=i):
            for fh in range(2):
                ctx.wait(vector, f"b_trp_{i}_{fh}")
                dst = xn2T[:].rearrange("p (f t) -> p f t", t=TPC)[:, fh * 8:(fh + 1) * 8,
                                                                  i * 128:(i + 1) * 128]
                srcv = ps_t[:, fh * 1024:(fh + 1) * 1024].rearrange("p (f t) -> p f t", t=128)
                ctx.inc(vector.tensor_copy(dst, srcv), "VE", tag=f"b_trev_{i}")

    for hf in range(2):
        tok0 = hf * 512

        for oc in range(64):
            for gu in range(2):
                wbuf = wg[(oc * 2 + gu) % 2]
                col0 = gu * DI + oc * 128

                par = "WL" + str((oc * 2 + gu) % 2)

                @S_("gpsimd")
                def _(gp, wbuf=wbuf, oc=oc, gu=gu, hf=hf, par=par):
                    ctx.wait(gp, f"b_parcons_{par}")
                    gp.dma_start(out=wbuf[:],
                                 in_=WguS[gu * 64 + oc]).then_inc(ctx.sem[par], 16)
                    ctx.bump(par, 16, tag=f"b_wld_{hf}_{oc}_{gu}")

                @S_("tensor")
                def _(tensor, wbuf=wbuf, oc=oc, gu=gu, hf=hf, tok0=tok0, par=par):
                    if hf == 0 and oc == 0 and gu == 0:
                        ctx.wait(tensor, "b_trev_7")
                    ctx.wait(tensor, f"b_wld_{hf}_{oc}_{gu}")
                    ctx.wait(tensor, "b_silu_used" if gu == 0 else "b_h_written")
                    for fc in range(16):
                        mm = tensor.matmul(ps_gu[:, gu * 512:gu * 512 + 512],
                                           wbuf[:, fc * 128:(fc + 1) * 128],
                                           xn2T[:, fc * TPC + tok0:fc * TPC + tok0 + 512],
                                           start=(fc == 0), stop=(fc == 15))
                    ctx.inc(mm, "PE", tag=f"b_wmm_{id(wbuf)}")
                    ctx.tag[f"b_gu_{gu}"] = ctx.tag[f"b_wmm_{id(wbuf)}"]
                    ctx.tag[f"b_parcons_{par}"] = ctx.tag[f"b_wmm_{id(wbuf)}"]

                if gu == 0:
                    @S_("scalar")
                    def _(scalar):
                        ctx.wait(scalar, "b_gu_0")
                        ctx.wait(scalar, "b_silu_read")
                        ctx.inc(scalar.activation(silu_sb[:], ps_gu[:, 0:512], AF.Silu),
                                "AC", tag="b_silu_used")
                else:
                    @S_("vector")
                    def _(vector, oc=oc, hf=hf):
                        ctx.wait(vector, "b_gu_1")
                        ctx.wait(vector, "b_silu_used")
                        ctx.wait(vector, f"b_hWAR_{hf}")
                        n = ctx.inc(vector.tensor_tensor(h_sb[:, oc * 512:(oc + 1) * 512],
                                                         ps_gu[:, 512:1024], silu_sb[:],
                                                         ALU.mult),
                                    "VE", tag="b_h_written")
                        ctx.tag["b_silu_read"] = ("VE", n)

        ctx.tag[f"b_hdone_{hf}"] = ctx.tag["b_h_written"]

        for o2 in range(16):
            wbuf = wd[o2 % 2]

            par = "WL" + str(o2 % 2)

            @S_("gpsimd")
            def _(gp, wbuf=wbuf, o2=o2, hf=hf, par=par):
                ctx.wait(gp, f"b_parcons_{par}")
                gp.dma_start(out=wbuf[:],
                             in_=Wd0S[o2]).then_inc(ctx.sem[par], 16)
                ctx.bump(par, 16, tag=f"b_wdld_{hf}_{o2}")

            @S_("tensor")
            def _(tensor, wbuf=wbuf, o2=o2, hf=hf, par=par):
                ctx.wait(tensor, f"b_wdld_{hf}_{o2}")
                ctx.wait(tensor, f"b_hdone_{hf}")
                ctx.wait(tensor, "b_dnev")
                for ic in range(64):
                    mm = tensor.matmul(ps_d[:], wbuf[:, ic * 128:(ic + 1) * 128],
                                       h_sb[:, ic * 512:(ic + 1) * 512],
                                       start=(ic == 0), stop=(ic == 63))
                ctx.inc(mm, "PE", tag=f"b_wmm_{id(wbuf)}")
                ctx.tag["b_dnmm"] = ctx.tag[f"b_wmm_{id(wbuf)}"]
                ctx.tag[f"b_parcons_{par}"] = ctx.tag[f"b_wmm_{id(wbuf)}"]

            @S_("vector")
            def _(vector, o2=o2, hf=hf):
                ctx.wait(vector, "b_dnmm")
                ctx.wait(vector, "b_dnfm_free")
                ctx.inc(vector.tensor_copy(dn_fm[:, o2 * 512:(o2 + 1) * 512], ps_d[:]),
                        "VE", tag="b_dnev")
        ctx.tag[f"b_hWAR_{1 - hf}"] = ctx.tag["b_dnmm"]

        for tt in range(4):
            i = hf * 4 + tt

            @S_("tensor")
            def _(tensor, tt=tt, hf=hf):
                ctx.wait(tensor, "b_dnev")
                ctx.wait(tensor, f"b_dtev")
                for fh in range(2):
                    for f8 in range(8):
                        f = fh * 8 + f8
                        mm = tensor.transpose(ps_t[:, fh * 1024 + f8 * 128:fh * 1024 + (f8 + 1) * 128],
                                              dn_fm[:, f * 512 + tt * 128:f * 512 + (tt + 1) * 128],
                                              ident[:])
                    ctx.inc(mm, "PE", tag=f"b_dtp_{fh}")
                ctx.tag["b_dnfm_free"] = ctx.tag["b_dtp_1"]

            @S_("vector")
            def _(vector, tt=tt):
                for fh in range(2):
                    ctx.wait(vector, f"b_dtp_{fh}")
                    ctx.inc(vector.tensor_copy(dn_tm[:, fh * 1024:(fh + 1) * 1024],
                                               ps_t[:, fh * 1024:(fh + 1) * 1024]), "VE", tag="b_dtev")

            emit_post_ln_add(ctx, g, dn_tm, xt, nm, rs, t0, sqscr, xnbf, i,
                             xout_ap, tag=f"bp_{i}", dn_ready_tag="b_dtev")


def emit_post_ln_add(ctx, g, dn_tm, xt, nm, rs, t0, sqscr, obf, i, xout_ap, tag,
                     dn_ready_tag):
    S_ = ctx.sec
    xsp = g["xsp"]

    @S_("sync")
    def _(sync, i=i):
        ctx.wait_cnt(sync, "ST")
        ctx.wait(sync, dn_ready_tag)
        sync.dma_start(out=xt[:], in_=xsp[i * 128:(i + 1) * 128, :]).then_inc(ctx.sem["LD"], 16)
        ctx.bump("LD", 16, tag=f"{tag}_ld")

    @S_("vector")
    def _(vector):
        ctx.wait(vector, dn_ready_tag)

    @S_("scalar")
    def _(scalar):
        ctx.wait(scalar, dn_ready_tag)

    _ln_sections(ctx, g, dn_tm[:], nm, rs, t0, sqscr, DM, tag + "_ln")

    @S_("vector")
    def _(vector):
        ctx.wait(vector, tag + "_ln_rs")
        ctx.wait(vector, f"{tag}_ld")
        ctx.inc(vector.tensor_scalar(obf[:], dn_tm[:], nm[:], rs[:], ALU.add, ALU.mult), "VE")
        ctx.wait_cnt(vector, "VE")
        ctx.inc(vector.tensor_tensor(xt[:], xt[:], obf[:], ALU.add),
                "VE", tag=f"{tag}_new")

    @S_("sync")
    def _(sync, i=i):
        ctx.wait(sync, f"{tag}_new")
        sync.dma_start(out=xsp[i * 128:(i + 1) * 128, :], in_=xt[:]).then_inc(ctx.sem["ST"], 16)
        n = ctx.bump("ST", 16)
        if xout_ap is not None:
            sync.dma_start(out=xout_ap[i * 128:(i + 1) * 128, :], in_=xt[:]).then_inc(ctx.sem["ST"], 16)
            n = ctx.bump("ST", 16)
        ctx.tag[f"{tag}_st"] = ("ST", n)
        ctx.tag[_next_xused(tag)] = ("ST", n)


def _next_xused(tag):
    head, idx = tag.rsplit("_", 1)
    return f"{head}_{int(idx) + 1}_xused_prev"



def emit_phase_A(ctx, g, xin_ap, xout_ap):
    S_ = ctx.sec
    nc = ctx.nc
    xsp = g["xsp"]
    ident = g["ident_sb"]
    src = xin_ap if xin_ap is not None else xsp
    WqT, WkT, WvT, WoT, SD = g["WqT"], g["WkT"], g["WvT"], g["WoT"], g["SD"]
    kag_in, kag_out = g["kag_in"], g["kag_out"]
    vag_in, vag_out = g["vag_in"], g["vag_out"]

    P = g["phase_stack"].enter_context
    xt = P(nc.sbuf_tensor("a_xt", [128, DM], F32))
    xnbf = P(nc.sbuf_tensor("a_xnbf", [128, DM], BF16))
    sqscr = P(nc.sbuf_tensor("a_sqscr", [128, DM], F32))
    nm = P(nc.sbuf_tensor("a_nm", [128, 1], F32))
    rs = P(nc.sbuf_tensor("a_rs", [128, 1], F32))
    t0 = P(nc.sbuf_tensor("a_t0", [128, 1], F32))
    abig = P(nc.sbuf_tensor("a_big", [128, 4 * 8192], BF16))
    xn1T = abig[:, 0:16384]
    wkv = abig[:, 16384:32768]
    scores = abig[:]
    qsb = P(nc.sbuf_tensor("a_q", [128, 16 * TPC], BF16))
    wqb = [P(nc.sbuf_tensor(f"a_wq{j}", [128, 16 * 128], BF16)) for j in range(2)]
    wob = [P(nc.sbuf_tensor(f"a_wo{j}", [128, 2048], BF16)) for j in range(2)]
    kvown = P(nc.sbuf_tensor("a_kvown", [128, 1024], BF16))
    ksb = [P(nc.sbuf_tensor(f"a_k{j}", [128, 512], BF16)) for j in range(2)]
    vsb = [P(nc.sbuf_tensor(f"a_v{j}", [128, 512], BF16)) for j in range(2)]
    sdb = [P(nc.sbuf_tensor(f"a_sd{j}", [128, 512], BF16)) for j in range(2)]
    pt_sb = [P(nc.sbuf_tensor(f"a_pt{j}", [128, 512], BF16)) for j in range(2)]
    gmx = P(nc.sbuf_tensor("a_gmx", [128, 8], F32))
    sumc = P(nc.sbuf_tensor("a_sumc", [128, 8], F32))
    bcol = P(nc.sbuf_tensor("a_bcol", [128, 8], F32))
    rcolbf = P(nc.sbuf_tensor("a_rcolbf", [128, 16], BF16))
    rowbf = P(nc.sbuf_tensor("a_rowbf", [1, 2048], BF16))
    nbc = P(nc.sbuf_tensor("a_nbc", [128, 2048], BF16))
    o_sb = P(nc.sbuf_tensor("a_o", [128, 2048], BF16))
    dn_tm = P(nc.sbuf_tensor("a_dntm", [128, DM], BF16))
    onesb_a = P(nc.sbuf_tensor("a_onesb", [1, 128], BF16))
    g["onesb_a"] = onesb_a
    ps_kv = P(nc.psum_tensor("a_pskv", [128, 1024], F32))
    ps_pt = P(nc.psum_tensor("a_pspt", [128, 512], BF16))
    ps_o = P(nc.psum_tensor("a_pso", [128, 2048], F32))
    ps_s = ps_kv
    ps_t = ps_o[:].bitcast(BF16)

    def slope(h):
        return 2.0 ** (-0.5 * (h + 1))

    @S_("vector")
    def _(vector):
        ctx.inc(vector.memset(onesb_a[:], 1.0), "VE", tag="a_ones")

    def sd_idx(i, kc):
        return sum(2 * (j + 1) for j in range(i)) + kc

    def owner(a):
        return a % 8, a // 8

    for i in range(NT):
        @S_("sync")
        def _(sync, i=i):
            ctx.wait(sync, f"a_xn_{i-1}")
            ctx.wait(sync, f"a_sp_{i-1}")
            sync.dma_start(out=xt[:], in_=src[i * 128:(i + 1) * 128, :]).then_inc(ctx.sem["LD"], 16)
            ctx.bump("LD", 16, tag=f"a_ld_{i}")
            if xin_ap is not None:
                sync.wait_ge(ctx.sem["LD"], ctx.cnt["LD"])
                sync.dma_start(out=xsp[i * 128:(i + 1) * 128, :], in_=xt[:]).then_inc(ctx.sem["ST"], 16)
                ctx.bump("ST", 16, tag=f"a_sp_{i}")

        @S_("vector")
        def _(vector, i=i):
            ctx.wait(vector, f"a_ld_{i}")

        @S_("scalar")
        def _(scalar, i=i):
            ctx.wait(scalar, f"a_ld_{i}")

        _ln_sections(ctx, g, xt[:], nm, rs, t0, sqscr, DM, f"a_ln_{i}")

        @S_("vector")
        def _(vector, i=i):
            ctx.wait(vector, f"a_ln_{i}_rs")
            ctx.wait(vector, f"a_tr_{i-1}")
            ctx.inc(vector.tensor_scalar(xnbf[:], xt[:], nm[:], rs[:], ALU.add, ALU.mult),
                    "VE", tag=f"a_xn_{i}")

        @S_("tensor")
        def _(tensor, i=i):
            ctx.wait(tensor, f"a_xn_{i}")
            ctx.wait(tensor, f"a_trev_{i-1}")
            for fh in range(2):
                for f8 in range(8):
                    f = fh * 8 + f8
                    mm = tensor.transpose(
                        ps_t[:, fh * 1024 + f8 * 128:fh * 1024 + (f8 + 1) * 128],
                        xnbf[:, f * 128:(f + 1) * 128], ident[:])
                ctx.inc(mm, "PE", tag=f"a_trp_{i}_{fh}")
            ctx.tag[f"a_tr_{i}"] = ctx.tag[f"a_trp_{i}_1"]

        @S_("vector")
        def _(vector, i=i):
            for fh in range(2):
                ctx.wait(vector, f"a_trp_{i}_{fh}")
                dst = xn1T.rearrange("p (f t) -> p f t", t=TPC)[:, fh * 8:(fh + 1) * 8,
                                                                  i * 128:(i + 1) * 128]
                srcv = ps_t[:, fh * 1024:(fh + 1) * 1024].rearrange("p (f t) -> p f t", t=128)
                ctx.inc(vector.tensor_copy(dst, srcv), "VE", tag=f"a_trev_{i}")
    ctx.tag["a_xnT_ready"] = ctx.tag["a_trev_7"]

    @S_("gpsimd")
    def _(gp):
        sk = WkT.rearrange("(hc p) k -> p hc k", p=128)
        gp.dma_start(out=wkv.rearrange("p (hc e) -> p hc e", e=1024)[:, :, 0:512],
                     in_=sk).then_inc(ctx.sem["LDG"], 16)
        ctx.bump("LDG", 16)
        sv = WvT.rearrange("(hc p) k -> p hc k", p=128)
        gp.dma_start(out=wkv.rearrange("p (hc e) -> p hc e", e=1024)[:, :, 512:1024],
                     in_=sv).then_inc(ctx.sem["LDG"], 16)
        ctx.bump("LDG", 16, tag="a_wkv_ld")

    for gg in range(GRP):
        @S_("tensor")
        def _(tensor, gg=gg):
            if gg == 0:
                ctx.wait(tensor, "a_xnT_ready")
                ctx.wait(tensor, "a_wkv_ld")
            ctx.wait(tensor, "a_kv_ev")
            for hc in range(16):
                for tb in range(2):
                    mm = tensor.matmul(ps_kv[:, tb * 512:(tb + 1) * 512],
                                       wkv[:, hc * 1024 + gg * 128:hc * 1024 + gg * 128 + 128],
                                       xn1T[:, hc * TPC + tb * 512:hc * TPC + (tb + 1) * 512],
                                       start=(hc == 0), stop=(hc == 15))
            ctx.inc(mm, "PE", tag="a_kv_mm")

        @S_("vector")
        def _(vector, gg=gg):
            ctx.wait(vector, "a_kv_mm")
            ctx.wait(vector, "a_kv_st")
            ctx.inc(vector.tensor_copy(kvown[:], ps_kv[:]), "VE", tag="a_kv_ev")

        @S_("sync")
        def _(sync, gg=gg):
            ctx.wait(sync, "a_kv_ev")
            sync.dma_start(out=kag_in.ap()[gg * 128:(gg + 1) * 128, :], in_=kvown[:]).then_inc(ctx.sem["ST"], 16)
            ctx.bump("ST", 16, tag="a_kv_st")

    for tt in range(NT):
        @S_("tensor")
        def _(tensor, tt=tt):
            ctx.wait(tensor, "a_kv_ev")
            for hc in range(16):
                mm = tensor.matmul(ps_kv[:, 0:512],
                                   xn1T[:, hc * TPC + tt * 128:hc * TPC + (tt + 1) * 128],
                                   wkv.rearrange("p (hc e) -> p hc e", e=1024)[:, hc, 512:1024],
                                   start=(hc == 0), stop=(hc == 15))
            ctx.inc(mm, "PE", tag="a_kv_mm")

        @S_("vector")
        def _(vector, tt=tt):
            ctx.wait(vector, "a_kv_mm")
            ctx.wait(vector, "a_kv_st")
            ctx.inc(vector.tensor_copy(kvown[:, 0:512], ps_kv[:, 0:512]), "VE", tag="a_kv_ev")

        @S_("sync")
        def _(sync, tt=tt):
            ctx.wait(sync, "a_kv_ev")
            sync.dma_start(out=vag_in.ap()[tt * 128:(tt + 1) * 128, :], in_=kvown[:, 0:512]).then_inc(ctx.sem["ST"], 16)
            ctx.bump("ST", 16, tag="a_kv_st")

    @S_("gpsimd")
    def _(gp):
        ctx.wait(gp, "a_kv_st")
        gp.collective_compute("AllGather", ALU.bypass, replica_groups=[list(range(8))],
                              ins=[kag_in.ap().opt()], outs=[kag_out.ap().opt()]).then_inc(ctx.sem["CC"])
        ctx.bump("CC", 1)
        gp.wait_ge(ctx.sem["CC"], ctx.cnt["CC"])
        gp.collective_compute("AllGather", ALU.bypass, replica_groups=[list(range(8))],
                              ins=[vag_in.ap().opt()], outs=[vag_out.ap().opt()]).then_inc(ctx.sem["CC"])
        ctx.bump("CC", 1, tag="a_gather")
        gp.wait_ge(ctx.sem["CC"], ctx.cnt["CC"])

    for h in range(HEADS):
        par = "WL" + str(h % 2)
        wbuf = wqb[h % 2]

        @S_("gpsimd")
        def _(gp, h=h, par=par, wbuf=wbuf):
            ctx.wait(gp, f"a_qcons_{par}")
            srcw = WqT[:, h * 128:(h + 1) * 128].rearrange("(c p) o -> p c o", p=128)
            gp.dma_start(out=wbuf[:].rearrange("p (c o) -> p c o", o=128),
                         in_=srcw).then_inc(ctx.sem[par], 16)
            ctx.bump(par, 16, tag=f"a_qld_{h}")

        @S_("tensor")
        def _(tensor, h=h, par=par, wbuf=wbuf):
            ctx.wait(tensor, f"a_qld_{h}")
            ctx.wait(tensor, "a_q_ev")
            if h == 0:
                ctx.wait(tensor, "a_kv_ev")
            for hc in range(16):
                for tb in range(2):
                    mm = tensor.matmul(ps_kv[:, tb * 512:(tb + 1) * 512],
                                       wbuf[:, hc * 128:(hc + 1) * 128],
                                       xn1T[:, hc * TPC + tb * 512:hc * TPC + (tb + 1) * 512],
                                       start=(hc == 0), stop=(hc == 15))
            n = ctx.inc(mm, "PE", tag="a_q_mm")
            ctx.tag[f"a_qcons_{par}"] = ("PE", n)

        @S_("vector")
        def _(vector, h=h):
            ctx.wait(vector, "a_q_mm")
            ctx.inc(vector.tensor_scalar(qsb[:, h * TPC:(h + 1) * TPC], ps_kv[:],
                                         1.0 / (slope(h) * math.sqrt(HD)), None,
                                         ALU.mult), "VE", tag="a_q_ev")
    ctx.tag["a_q_ready"] = ctx.tag["a_q_ev"]

    for i in range(NT):
        nkc = 2 * (i + 1)
        for gg in range(GRP):
            for kc in range(nkc):
                kp = kc % 2

                @S_("sync")
                def _(sync, i=i, gg=gg, kc=kc, kp=kp):
                    if i == 0 and gg == 0 and kc == 0:
                        ctx.wait(sync, "a_gather")
                    ctx.wait(sync, f"a_kuse_{kp}")
                    for sub in range(4):
                        a = kc * 4 + sub
                        rnk, slot = owner(a)
                        sync.dma_start(
                            out=ksb[kp][:, sub * 128:(sub + 1) * 128],
                            in_=kag_out.ap()[rnk * 512 + gg * 128:rnk * 512 + (gg + 1) * 128,
                                             slot * 128:(slot + 1) * 128]).then_inc(ctx.sem["KL" + str(kp)], 16)
                        ctx.bump("KL" + str(kp), 16)
                    ctx.tag[f"a_kld_{kp}"] = ("KL" + str(kp), ctx.cnt["KL" + str(kp)])

                @S_("scalar")
                def _(scalar, i=i, kc=kc, kp=kp):
                    ctx.wait(scalar, f"a_sduse_{kp}")
                    scalar.dma_start(out=sdb[kp][:],
                                     in_=SD[sd_idx(i, kc) * 128:(sd_idx(i, kc) + 1) * 128, :]).then_inc(ctx.sem["SD" + str(kp)], 16)
                    ctx.bump("SD" + str(kp), 16, tag=f"a_sdld_{kp}")

                for r in range(4):
                    h = gg * 4 + r
                    sp = (kc * 4 + r) % 2

                    @S_("tensor")
                    def _(tensor, i=i, h=h, kc=kc, kp=kp, sp=sp):
                        if kc == 0 and r == 0:
                            ctx.wait(tensor, "a_q_ready")
                        ctx.wait(tensor, f"a_kld_{kp}")
                        ctx.wait(tensor, f"a_sev_{sp}")
                        ctx.inc(tensor.matmul(ps_s[:, sp * 512:(sp + 1) * 512],
                                              qsb[:, h * TPC + i * 128:h * TPC + (i + 1) * 128],
                                              ksb[kp][:], start=True, stop=True),
                                "PE", tag=f"a_smm_{sp}")
                        if r == 3:
                            ctx.tag[f"a_kuse_{kp}"] = ("PE", ctx.cnt["PE"])

                    @S_("vector")
                    def _(vector, r=r, kc=kc, kp=kp, sp=sp):
                        ctx.wait(vector, f"a_smm_{sp}")
                        ctx.wait(vector, f"a_sdld_{kp}")
                        ctx.wait(vector, "a_sc_free")
                        n = ctx.inc(vector.tensor_tensor(
                            scores[:, r * 8192 + kc * 512:r * 8192 + (kc + 1) * 512],
                            ps_s[:, sp * 512:(sp + 1) * 512], sdb[kp][:], ALU.subtract),
                            "VE", tag=f"a_sev_{sp}")
                        if r == 3:
                            ctx.tag[f"a_sduse_{kp}"] = ("VE", n)
                ctx.tag[f"a_scores_{gg}_{kc}"] = ctx.tag[f"a_sev_{(kc * 4 + 3) % 2}"]
            ctx.tag["a_passA"] = ("VE", ctx.cnt["VE"])

            @S_("vector")
            def _(vector, i=i, gg=gg, nkc=nkc):
                ctx.wait(vector, "a_passA")
                for r in range(4):
                    ctx.wait_cnt(vector, "VE")
                    ctx.inc(vector.reduce_max(gmx[:, r:r + 1],
                                              scores[:, r * 8192:r * 8192 + nkc * 512],
                                              AX.X), "VE", tag=f"a_mx_{r}")

            @S_("vector")
            def _(vector, gg=gg):
                for r in range(4):
                    ctx.wait(vector, f"a_mx_{r}")
                    ctx.wait_cnt(vector, "VE")
                    ctx.inc(vector.tensor_scalar(bcol[:, r:r + 1], gmx[:, r:r + 1],
                                                 -slope(gg * 4 + r), None, ALU.mult),
                            "VE", tag=f"a_bc_{r}")

            for r in range(4):
                @S_("scalar")
                def _(scalar, r=r, gg=gg, nkc=nkc):
                    ctx.wait(scalar, f"a_bc_{r}")
                    ctx.wait(scalar, "a_rc_done")
                    ctx.inc(scalar.activation(scores[:, r * 8192:r * 8192 + nkc * 512],
                                              scores[:, r * 8192:r * 8192 + nkc * 512],
                                              AF.Exp, bias=bcol[:, r:r + 1],
                                              scale=slope(gg * 4 + r),
                                              accum_out=sumc[:, r:r + 1]), "AC",
                            tag=f"a_exp_{r}")

            for kc in range(nkc):
                kp = kc % 2

                @S_("sync")
                def _(sync, gg=gg, kc=kc, kp=kp):
                    ctx.wait(sync, f"a_vuse_{kp}")
                    for sub in range(4):
                        a = kc * 4 + sub
                        rnk, slot = owner(a)
                        sync.dma_start(
                            out=vsb[kp][:, sub * 128:(sub + 1) * 128],
                            in_=vag_out.ap()[rnk * 1024 + slot * 128:rnk * 1024 + (slot + 1) * 128,
                                             gg * 128:(gg + 1) * 128]).then_inc(ctx.sem["VL" + str(kp)], 16)
                        ctx.bump("VL" + str(kp), 16)
                    ctx.tag[f"a_vld_{kp}"] = ("VL" + str(kp), ctx.cnt["VL" + str(kp)])

                for sub in range(4):
                    pp = (kc * 4 + sub) % 2

                    @S_("tensor")
                    def _(tensor, gg=gg, kc=kc, sub=sub, pp=pp, i=i):
                        for r in range(4):
                            if r == 0:
                                ctx.wait(tensor, "a_exp_3")
                                ctx.wait(tensor, f"a_ptev_{pp}")
                            mm = tensor.transpose(
                                ps_pt[:, r * 128:(r + 1) * 128],
                                scores[:, r * 8192 + kc * 512 + sub * 128:r * 8192 + kc * 512 + (sub + 1) * 128],
                                ident[:])
                        ctx.inc(mm, "PE", tag=f"a_pt_{pp}")

                    @S_("vector")
                    def _(vector, pp=pp):
                        ctx.wait(vector, f"a_pt_{pp}")
                        ctx.wait(vector, f"a_ptuse_{pp}")
                        ctx.inc(vector.tensor_copy(pt_sb[pp][:], ps_pt[:]), "VE",
                                tag=f"a_ptev_{pp}")

                    @S_("tensor")
                    def _(tensor, gg=gg, kc=kc, sub=sub, pp=pp, kp=kp):
                        ctx.wait(tensor, f"a_ptev_{pp}")
                        ctx.wait(tensor, f"a_vld_{kp}")
                        if kc == 0 and sub == 0:
                            ctx.wait(tensor, "a_opev")
                        n = ctx.inc(tensor.matmul(ps_o[:, gg * 512:(gg + 1) * 512],
                                                  vsb[kp][:, sub * 128:(sub + 1) * 128],
                                                  pt_sb[pp][:],
                                                  start=(kc == 0 and sub == 0),
                                                  stop=(kc == nkc - 1 and sub == 3)),
                                    "PE", tag=f"a_pv_{pp}")
                        ctx.tag[f"a_ptuse_{pp}"] = ("PE", n)
                        if sub == 3:
                            ctx.tag[f"a_vuse_{kp}"] = ("PE", n)
            ctx.tag["a_sc_free"] = ("PE", ctx.cnt["PE"])

            @S_("vector")
            def _(vector, gg=gg):
                ctx.wait(vector, "a_exp_3")
                for r in range(4):
                    ctx.wait_cnt(vector, "VE")
                    ctx.inc(vector.reciprocal(bcol[:, 4 + r:5 + r], sumc[:, r:r + 1]), "VE")
                    ctx.wait_cnt(vector, "VE")
                    ctx.inc(vector.tensor_copy(rcolbf[:, gg * 4 + r:gg * 4 + r + 1],
                                               bcol[:, 4 + r:5 + r]),
                            "VE", tag="a_rc_done")

        for rnd in range(2):
            @S_("tensor")
            def _(tensor, rnd=rnd):
                ctx.wait(tensor, "a_rc_done")
                ctx.wait(tensor, "a_rowev")
                for h8 in range(8):
                    h = rnd * 8 + h8
                    mm = tensor.matmul(ps_s[0:1, h8 * 128:(h8 + 1) * 128],
                                       rcolbf[:, h:h + 1], ident[:],
                                       start=True, stop=True)
                ctx.inc(mm, "PE", tag="a_nrow")

            @S_("vector")
            def _(vector, rnd=rnd):
                ctx.wait(vector, "a_nrow")
                ctx.inc(vector.tensor_copy(rowbf[0:1, rnd * 1024:(rnd + 1) * 1024],
                                           ps_s[0:1, 0:1024]), "VE", tag="a_rowev")

        for cch in range(4):
            @S_("tensor")
            def _(tensor, cch=cch):
                ctx.wait(tensor, "a_rowev")
                ctx.wait(tensor, f"a_nbcev_{(cch - 2) % 2}")
                ctx.inc(tensor.matmul(ps_s[:, (cch % 2) * 512:(cch % 2 + 1) * 512],
                                      g["onesb_a"][:], rowbf[0:1, cch * 512:(cch + 1) * 512],
                                      start=True, stop=True), "PE", tag=f"a_nbc_{cch % 2}")

            @S_("vector")
            def _(vector, cch=cch):
                ctx.wait(vector, f"a_nbc_{cch % 2}")
                ctx.inc(vector.tensor_copy(nbc[:, cch * 512:(cch + 1) * 512],
                                           ps_s[:, (cch % 2) * 512:(cch % 2 + 1) * 512]),
                        "VE", tag=f"a_nbcev_{cch % 2}")

        @S_("vector")
        def _(vector, i=i):
            ctx.wait(vector, "a_nbcev_0")
            ctx.wait(vector, "a_nbcev_1")
            for cch in range(4):
                ctx.wait_cnt(vector, "VE")
                ctx.inc(vector.tensor_tensor(o_sb[:, cch * 512:(cch + 1) * 512],
                                             ps_o[:, cch * 512:(cch + 1) * 512],
                                             nbc[:, cch * 512:(cch + 1) * 512],
                                             ALU.mult), "VE", tag="a_o_ev")

        for h in range(HEADS):
            par = "WL" + str(h % 2)
            wbuf = wob[h % 2]

            @S_("gpsimd")
            def _(gp, h=h, par=par, wbuf=wbuf):
                ctx.wait(gp, f"a_ocons_{par}")
                gp.dma_start(out=wbuf[:], in_=WoT[h * 128:(h + 1) * 128, :]).then_inc(ctx.sem[par], 16)
                ctx.bump(par, 16, tag=f"a_old_{h}")

            @S_("tensor")
            def _(tensor, h=h, par=par, wbuf=wbuf):
                ctx.wait(tensor, f"a_old_{h}")
                if h == 0:
                    ctx.wait(tensor, "a_o_ev")
                for oc in range(4):
                    mm = tensor.matmul(ps_o[:, oc * 512:(oc + 1) * 512],
                                       o_sb[:, h * 128:(h + 1) * 128],
                                       wbuf[:, oc * 512:(oc + 1) * 512],
                                       start=(h == 0), stop=(h == 15))
                n = ctx.inc(mm, "PE", tag="a_opmm")
                ctx.tag[f"a_ocons_{par}"] = ("PE", n)

        @S_("vector")
        def _(vector, i=i):
            ctx.wait(vector, "a_opmm")
            for cch in range(4):
                ctx.wait_cnt(vector, "VE")
                ctx.inc(vector.tensor_copy(dn_tm[:, cch * 512:(cch + 1) * 512],
                                           ps_o[:, cch * 512:(cch + 1) * 512]), "VE",
                        tag="a_opev")

        emit_post_ln_add(ctx, g, dn_tm, xt, nm, rs, t0, sqscr, xnbf, i,
                         xout_ap, tag=f"ap_{i}", dn_ready_tag="a_opev")


def emit_phase_C(ctx, g, xin_ap, xout_ap):
    S_ = ctx.sec
    nc = ctx.nc
    xsp = g["xsp"]
    ident = g["ident_sb"]
    src = xin_ap if xin_ap is not None else xsp
    WinT, WopT = g["WinT"], g["WopT"]
    a2a_xi, a2a_xo = g["a2a_xi"], g["a2a_xo"]
    a2a_yi, a2a_yo = g["a2a_yi"], g["a2a_yo"]
    ar_xi, ar_xo = g["ar_xi"], g["ar_xo"]
    z_dram, xn_dram = g["z_dram"], g["xn_dram"]
    xc_dr, bc_dr = g["xc_dr"], g["bc_dr"]
    H = S // 2

    P = g["phase_stack"].enter_context
    xt = P(nc.sbuf_tensor("c_xt", [128, DM], F32))
    xnbf = P(nc.sbuf_tensor("c_xnbf", [128, DM], BF16))
    sqscr = P(nc.sbuf_tensor("c_sqscr", [128, DM], F32))
    nm = P(nc.sbuf_tensor("c_nm", [128, 1], F32))
    rs = P(nc.sbuf_tensor("c_rs", [128, 1], F32))
    t0 = P(nc.sbuf_tensor("c_t0", [128, 1], F32))
    rs2 = P(nc.sbuf_tensor("c_rs2", [128, 1], F32))
    wi_b = [P(nc.sbuf_tensor(f"c_wi{j}", [128, 16 * 128], BF16)) for j in range(2)]
    wo_b = [P(nc.sbuf_tensor(f"c_wo{j}", [128, 16 * 128], BF16)) for j in range(2)]
    evb = P(nc.sbuf_tensor("c_evb", [128, 1024], BF16))
    nsc_sb = P(nc.sbuf_tensor("c_nsc", [128, 16], F32))
    carry = P(nc.sbuf_tensor("c_carry", [128, 16], F32))
    convw = P(nc.sbuf_tensor("c_convw", [128, 16], F32))
    convb = P(nc.sbuf_tensor("c_convb", [128, 4], F32))
    alogc = P(nc.sbuf_tensor("c_alog", [128, 64], F32))
    acol = P(nc.sbuf_tensor("c_acol", [128, 64], F32))
    dcol = P(nc.sbuf_tensor("c_dcol", [128, 4], F32))
    dtbc = P(nc.sbuf_tensor("c_dtbc", [128, 4], F32))
    wxp = P(nc.sbuf_tensor("c_wxp", [128, 4 * 160], BF16))
    wdt = P(nc.sbuf_tensor("c_wdt", [128, 512], BF16))
    xd1 = P(nc.sbuf_tensor("c_xd1", [128, 2048], F32))
    dn_tm = P(nc.sbuf_tensor("c_dntm", [128, DM], BF16))
    onesb = P(nc.sbuf_tensor("c_onesb", [1, 128], BF16))
    AR = P(nc.sbuf_tensor("c_arena", [128, 73760], BF16))
    ps_a = P(nc.psum_tensor("c_psa", [128, 1024], F32))
    ps_b = P(nc.psum_tensor("c_psb", [128, 1024], F32))
    ps_x1 = P(nc.psum_tensor("c_psx1", [32, 512], F32))
    ps_t = ps_a[:].bitcast(BF16)

    xmT = AR[:, 0:16384]
    xin_sb = AR[:, 16384:24579]
    xc_sl = AR[:, 24580:32772]
    accf = AR[:, 32772:40964].bitcast(F32)
    xpr0 = AR[:, 40964:57348].bitcast(F32)
    X_sl = AR[:, 0:8192]
    dl_sl = AR[:, 8192:16384]
    bcb = [AR[:, 40964 + j * 8192:40964 + (j + 1) * 8192] for j in range(2)]
    yacf = AR[:, 57348:65540].bitcast(F32)
    b_sl = AR[:, 65540:69636]
    h_sl = AR[:, 69636:73732]
    spr = [AR[:, 57348 + j * 1024:57348 + (j + 1) * 1024].bitcast(F32) for j in range(4)]
    ym = AR[:, 0:16384]
    opacc = AR[:, 16384:32768]
    zy_sl = AR[:, 65540:67588]

    @S_("vector")
    def _(vector):
        vector.memset(xin_sb[:, 0:3], 0.0)
        ctx.inc(vector.memset(onesb[:], 1.0), "VE", tag="c_pad")

    @S_("sync")
    def _(sync):
        sync.dma_start(allow_slow_non_contiguous=True, out=nsc_sb[:].rearrange("p (f one) -> p f one", one=1), in_=g["nsc"].rearrange("(f p) one -> p f one", p=128)).then_inc(ctx.sem["LD"], 16)
        sync.dma_start(allow_slow_non_contiguous=True, out=convw[:].rearrange("p (ct k) -> p ct k", k=4), in_=g["convw"].rearrange("(ct p) k -> p ct k", p=128)).then_inc(ctx.sem["LD"], 16)
        sync.dma_start(allow_slow_non_contiguous=True, out=convb[:].rearrange("p (ct one) -> p ct one", one=1), in_=g["convb"].rearrange("(ct p) one -> p ct one", p=128)).then_inc(ctx.sem["LD"], 16)
        sync.dma_start(allow_slow_non_contiguous=True, out=dcol[:].rearrange("p (ct one) -> p ct one", one=1), in_=g["Dvec"].rearrange("(ct p) one -> p ct one", p=128)).then_inc(ctx.sem["LD"], 16)
        sync.dma_start(allow_slow_non_contiguous=True, out=dtbc[:].rearrange("p (ct one) -> p ct one", one=1), in_=g["dtb"].rearrange("(ct p) one -> p ct one", p=128)).then_inc(ctx.sem["LD"], 16)
        sync.dma_start(allow_slow_non_contiguous=True, out=alogc[:].rearrange("p (ct s) -> p ct s", s=16), in_=g["Alog"].rearrange("(ct p) s -> p ct s", p=128)).then_inc(ctx.sem["LD"], 16)
        ctx.bump("LD", 16 * 6, tag="c_cols_ld")

    @S_("scalar")
    def _(scalar):
        ctx.wait(scalar, "c_cols_ld")
        ctx.inc(scalar.activation(acol[:], alogc[:], AF.Exp), "AC", tag="c_alog_e")

    @S_("vector")
    def _(vector):
        ctx.wait(vector, "c_alog_e")
        ctx.wait_cnt(vector, "VE")
        ctx.inc(vector.tensor_scalar(acol[:], acol[:], -1.0, None, ALU.mult),
                "VE", tag="c_acol")

    @S_("gpsimd")
    def _(gp):
        gp.dma_start(allow_slow_non_contiguous=True, out=wxp[:].rearrange("p (ct r) -> p ct r", r=160), in_=g["WxpT"].rearrange("(ct p) r -> p ct r", p=128)).then_inc(ctx.sem["LDG"], 16)
        ctx.bump("LDG", 16)
        gp.dma_start(out=wdt[:], in_=g["WdtT"]).then_inc(ctx.sem["LDG"], 16)
        ctx.bump("LDG", 16, tag="c_w_ld")

    for i in range(NT):
        @S_("sync")
        def _(sync, i=i):
            ctx.wait(sync, f"c_xm_{i-1}")
            ctx.wait(sync, f"c_sp_{i-1}")
            sync.dma_start(out=xt[:], in_=src[i * 128:(i + 1) * 128, :]).then_inc(ctx.sem["LD"], 16)
            ctx.bump("LD", 16, tag=f"c_ld_{i}")
            if xin_ap is not None:
                sync.wait_ge(ctx.sem["LD"], ctx.cnt["LD"])
                sync.dma_start(out=xsp[i * 128:(i + 1) * 128, :], in_=xt[:]).then_inc(ctx.sem["ST"], 16)
                ctx.bump("ST", 16, tag=f"c_sp_{i}")

        @S_("vector")
        def _(vector, i=i):
            ctx.wait(vector, f"c_ld_{i}")

        @S_("scalar")
        def _(scalar, i=i):
            ctx.wait(scalar, f"c_ld_{i}")

        _ln_sections(ctx, g, xt[:], nm, rs, t0, sqscr, DM, f"c_ln_{i}")

        @S_("vector")
        def _(vector, i=i):
            ctx.wait(vector, f"c_ln_{i}_rs")
            ctx.wait(vector, f"c_hst_{i-1}")
            ctx.inc(vector.tensor_scalar(xnbf[:], xt[:], nm[:], rs[:], ALU.add, ALU.mult),
                    "VE", tag=f"c_h_{i}")

        @S_("sync")
        def _(sync, i=i):
            ctx.wait(sync, f"c_h_{i}")
            sync.dma_start(out=xn_dram.ap()[i * 128:(i + 1) * 128, :], in_=xnbf[:]).then_inc(ctx.sem["ST"], 16)
            ctx.bump("ST", 16, tag=f"c_hst_{i}")

        _ln_sections(ctx, g, xnbf[:], nm, rs2, t0, sqscr, DM, f"c_rms_{i}", rms_only=True)

        @S_("vector")
        def _(vector, i=i):
            ctx.wait(vector, f"c_rms_{i}_rs")
            ctx.wait(vector, f"c_hst_{i}")
            ctx.wait(vector, f"c_tr_{i-1}")
            ctx.inc(vector.tensor_scalar(xnbf[:], xnbf[:], rs2[:], None, ALU.mult),
                    "VE", tag=f"c_xm_{i}")

        @S_("tensor")
        def _(tensor, i=i):
            ctx.wait(tensor, f"c_xm_{i}")
            ctx.wait(tensor, f"c_trev_{i-1}")
            for fh in range(2):
                for f8 in range(8):
                    f = fh * 8 + f8
                    mm = tensor.transpose(
                        ps_t[:, fh * 1024 + f8 * 128:fh * 1024 + (f8 + 1) * 128],
                        xnbf[:, f * 128:(f + 1) * 128], ident[:])
                ctx.inc(mm, "PE", tag=f"c_trp_{i}_{fh}")
            ctx.tag[f"c_tr_{i}"] = ctx.tag[f"c_trp_{i}_1"]

        @S_("vector")
        def _(vector, i=i):
            if i == 0:
                ctx.wait(vector, "c_cols_ld")
            for fh in range(2):
                ctx.wait(vector, f"c_trp_{i}_{fh}")
                for f8 in range(8):
                    f = fh * 8 + f8
                    ctx.inc(vector.tensor_scalar(
                        xmT[:, f * TPC + i * 128:f * TPC + (i + 1) * 128],
                        ps_t[:, fh * 1024 + f8 * 128:fh * 1024 + (f8 + 1) * 128],
                        nsc_sb[:, f:f + 1], None, ALU.mult), "VE", tag=f"c_trev_{i}")
    ctx.tag["c_xmT_ready"] = ctx.tag["c_trev_7"]

    for ec in range(64):
        par = "WL" + str(ec % 2)
        wbuf = wi_b[ec % 2]

        @S_("gpsimd")
        def _(gp, ec=ec, par=par, wbuf=wbuf):
            ctx.wait(gp, f"c_icons_{par}")
            srcw = WinT[:, ec * 128:(ec + 1) * 128].rearrange("(c p) o -> p c o", p=128)
            gp.dma_start(out=wbuf[:].rearrange("p (c o) -> p c o", o=128),
                         in_=srcw).then_inc(ctx.sem[par], 16)
            ctx.bump(par, 16, tag=f"c_ild_{ec}")

        @S_("tensor")
        def _(tensor, ec=ec, par=par, wbuf=wbuf):
            if ec == 0:
                ctx.wait(tensor, "c_xmT_ready")
            ctx.wait(tensor, f"c_ild_{ec}")
            ctx.wait(tensor, "c_iev")
            for hc in range(16):
                for tb in range(2):
                    mm = tensor.matmul(ps_b[:, tb * 512:(tb + 1) * 512],
                                       wbuf[:, hc * 128:(hc + 1) * 128],
                                       xmT[:, hc * TPC + tb * 512:hc * TPC + (tb + 1) * 512],
                                       start=(hc == 0), stop=(hc == 15))
            n = ctx.inc(mm, "PE", tag="c_imm")
            ctx.tag[f"c_icons_{par}"] = ("PE", n)

        @S_("vector")
        def _(vector, ec=ec):
            ctx.wait(vector, "c_imm")
            ctx.wait(vector, "c_ist")
            ctx.inc(vector.tensor_copy(evb[:], ps_b[:]), "VE", tag="c_iev")

        @S_("sync")
        def _(sync, ec=ec):
            ctx.wait(sync, "c_iev")
            if ec < 32:
                dst = a2a_xi.ap()[ec * 128:(ec + 1) * 128, :]
            else:
                dst = z_dram.ap()[(ec - 32) * 128:(ec - 31) * 128, :]
            sync.dma_start(out=dst, in_=evb[:]).then_inc(ctx.sem["ST"], 16)
            ctx.bump("ST", 16, tag="c_ist")

    @S_("gpsimd")
    def _(gp):
        ctx.wait(gp, "c_ist")
        with allow_ici_all_to_all():
            gp.collective_compute("AllToAll", ALU.bypass, replica_groups=[list(range(8))],
                                  ins=[a2a_xi.ap().opt()], outs=[a2a_xo.ap().opt()]).then_inc(ctx.sem["CC"])
        ctx.bump("CC", 1, tag="c_a2a_x")
        gp.wait_ge(ctx.sem["CC"], ctx.cnt["CC"])

    for ct in range(4):
        @S_("sync")
        def _(sync, ct=ct):
            if ct == 0:
                ctx.wait(sync, "c_a2a_x")
            ctx.wait(sync, f"c_xpdone_{ct-1}")
            flat = a2a_xo.ap().rearrange("r c -> (r c)").rearrange(
                "(s chp j) -> s chp j", s=8, chp=512)
            vv = flat[:, ct * 128:(ct + 1) * 128, :].rearrange(
                "s p (i tau) -> p i s tau", tau=128)
            dst = xin_sb[:, 3:3 + S].rearrange("p (i s tau) -> p i s tau", i=8, s=8)
            sync.dma_start(out=dst, in_=vv).then_inc(ctx.sem["LD"], 16)
            ctx.bump("LD", 16, tag=f"c_xinld_{ct}")

        for half in range(2):
            o0 = half * H

            @S_("vector")
            def _(vector, ct=ct, half=half, o0=o0):
                ctx.wait(vector, f"c_xinld_{ct}")
                ctx.wait(vector, "c_pad")
                ctx.wait(vector, f"c_silu_{ct}_{half-1}" if half else f"c_silu_{ct-1}_1")
                ctx.wait_cnt(vector, "VE")
                ctx.inc(vector.tensor_scalar(accf[:], xin_sb[:, o0:o0 + H],
                                             convw[:, ct * 4:ct * 4 + 1], None, ALU.mult), "VE")
                for k in range(1, 4):
                    ctx.wait_cnt(vector, "VE")
                    ctx.inc(vector.scalar_tensor_tensor(accf[:], xin_sb[:, o0 + k:o0 + k + H],
                                                        convw[:, ct * 4 + k:ct * 4 + k + 1],
                                                        accf[:], ALU.mult, ALU.add), "VE")
                ctx.tag[f"c_conv_{ct}_{half}"] = ("VE", ctx.cnt["VE"])

            @S_("scalar")
            def _(scalar, ct=ct, half=half, o0=o0):
                ctx.wait(scalar, f"c_conv_{ct}_{half}")
                ctx.inc(scalar.activation(xc_sl[:, o0:o0 + H], accf[:], AF.Silu,
                                          bias=convb[:, ct:ct + 1]), "AC",
                        tag=f"c_silu_{ct}_{half}")

        @S_("sync")
        def _(sync, ct=ct):
            ctx.wait(sync, f"c_silu_{ct}_1")
            sync.dma_start(out=xc_dr.ap()[ct * 128:(ct + 1) * 128, :], in_=xc_sl[:]).then_inc(ctx.sem["ST"], 16)
            ctx.bump("ST", 16, tag=f"c_xcst_{ct}")

        for tc in range(16):
            @S_("tensor")
            def _(tensor, ct=ct, tc=tc):
                ctx.wait(tensor, f"c_silu_{ct}_1")
                if ct == 0 and tc == 0:
                    ctx.wait(tensor, "c_w_ld")
                ctx.wait(tensor, f"c_xpev_0")
                ctx.wait(tensor, f"c_xpev_1")
                ctx.inc(tensor.matmul(ps_a[:, (tc % 2) * 512:(tc % 2 + 1) * 512],
                                      wxp[:, ct * 160:ct * 160 + 128],
                                      xc_sl[:, tc * 512:(tc + 1) * 512],
                                      start=True, stop=True), "PE")
                ctx.inc(tensor.matmul(ps_x1[:, 0:512],
                                      wxp[:, ct * 160 + 128:ct * 160 + 160],
                                      xc_sl[:, tc * 512:(tc + 1) * 512],
                                      start=True, stop=True), "PE", tag=f"c_xpmm_{tc % 2}")

            @S_("vector")
            def _(vector, ct=ct, tc=tc):
                ctx.wait(vector, f"c_xpmm_{tc % 2}")
                ctx.wait_cnt(vector, "VE")
                pa = ps_a[:, (tc % 2) * 512:(tc % 2 + 1) * 512]
                x1dst = xd1[(tc % 4) * 32:(tc % 4 + 1) * 32, (tc // 4) * 512:(tc // 4 + 1) * 512]
                if ct == 0:
                    vector.tensor_copy(xpr0[:, tc * 512:(tc + 1) * 512], pa)
                    ctx.inc(vector.tensor_copy(x1dst, ps_x1[:, 0:512]), "VE",
                            tag=f"c_xpev_{tc % 2}")
                else:
                    vector.tensor_tensor(xpr0[:, tc * 512:(tc + 1) * 512],
                                         xpr0[:, tc * 512:(tc + 1) * 512], pa, ALU.add)
                    ctx.inc(vector.tensor_tensor(x1dst, x1dst, ps_x1[:, 0:512], ALU.add),
                            "VE", tag=f"c_xpev_{tc % 2}")
        ctx.tag[f"c_xpdone_{ct}"] = ctx.tag[f"c_xpev_1"]

    @S_("sync")
    def _(sync):
        ctx.wait(sync, "c_xpdone_3")
        sync.dma_start(out=ar_xi.ap()[0:128, :], in_=xpr0[:]).then_inc(ctx.sem["ST"], 16)
        ctx.bump("ST", 16)
        for q in range(4):
            xsrc = xd1[q * 32:(q + 1) * 32, :].rearrange("r (blk c) -> r blk c", blk=4)
            xdst = ar_xi.ap()[128:160, :].rearrange("r (blk qq c) -> r blk qq c",
                                                    blk=4, qq=4)[:, :, q, :]
            sync.dma_start(out=xdst, in_=xsrc).then_inc(ctx.sem["ST"], 16)
            ctx.bump("ST", 16)
        ctx.tag["c_xp_st"] = ("ST", ctx.cnt["ST"])

    @S_("gpsimd")
    def _(gp):
        ctx.wait(gp, "c_xp_st")
        gp.collective_compute("AllReduce", ALU.add, replica_groups=[list(range(8))],
                              ins=[ar_xi.ap().opt()], outs=[ar_xo.ap().opt()]).then_inc(ctx.sem["CC"])
        ctx.bump("CC", 1, tag="c_ar")
        gp.wait_ge(ctx.sem["CC"], ctx.cnt["CC"])

    @S_("gpsimd")
    def _(gp):
        ctx.wait(gp, "c_ar")
        for s in range(2 * DST):
            row = ar_xo.ap()[128 + s:129 + s, :]
            gp.dma_start(out=bc_dr.ap()[s * 128:(s + 1) * 128, :],
                         in_=row.broadcast_to([128, S])).then_inc(ctx.sem["LDG"], 16)
            ctx.bump("LDG", 16)
        ctx.tag["c_bc_built"] = ("LDG", ctx.cnt["LDG"])

    for ct in range(4):
        @S_("gpsimd")
        def _(gp, ct=ct):
            ctx.wait(gp, "c_ar")
            ctx.wait(gp, f"c_Xfree_{ct-1}")
            gp.dma_start(out=X_sl, in_=ar_xo.ap()[0:128, :]).then_inc(ctx.sem["LDG"], 16)
            ctx.bump("LDG", 16, tag=f"c_dtlld_{ct}")

        @S_("sync")
        def _(sync, ct=ct):
            ctx.wait(sync, f"c_xcfree_{ct-1}")
            sync.dma_start(out=xc_sl, in_=xc_dr.ap()[ct * 128:(ct + 1) * 128, :]).then_inc(ctx.sem["LD"], 16)
            ctx.bump("LD", 16, tag=f"c_xcld_{ct}")

        for tc in range(16):
            pp = tc % 2

            @S_("tensor")
            def _(tensor, ct=ct, tc=tc, pp=pp):
                if tc == 0:
                    ctx.wait(tensor, f"c_dtlld_{ct}")
                    if ct == 0:
                        ctx.wait(tensor, "c_iev")
                ctx.wait(tensor, f"c_dtu_{pp}")
                ctx.inc(tensor.matmul(ps_b[:, pp * 512:(pp + 1) * 512],
                                      wdt[:, ct * 128:(ct + 1) * 128],
                                      X_sl[:, tc * 512:(tc + 1) * 512],
                                      start=True, stop=True), "PE", tag=f"c_dtmm_{pp}")

            @S_("scalar")
            def _(scalar, ct=ct, tc=tc, pp=pp):
                ctx.wait(scalar, f"c_dtmm_{pp}")
                ctx.wait(scalar, f"c_dtudone_{pp}")
                ctx.inc(scalar.activation(accf[:, pp * 512:(pp + 1) * 512],
                                          ps_b[:, pp * 512:(pp + 1) * 512],
                                          AF.Identity, bias=dtbc[:, ct:ct + 1]), "AC",
                        tag=f"c_dtu_{pp}")

            @S_("vector")
            def _(vector, tc=tc, pp=pp):
                ctx.wait(vector, f"c_dtu_{pp}")
                ctx.wait_cnt(vector, "VE")
                ctx.inc(vector.tensor_scalar(spr[pp][:], accf[:, pp * 512:(pp + 1) * 512],
                                             0.0, None, ALU.max), "VE")
                ctx.wait_cnt(vector, "VE")
                ctx.inc(vector.tensor_scalar(spr[2 + pp][:], accf[:, pp * 512:(pp + 1) * 512],
                                             -1.0, None, ALU.mult), "VE")
                ctx.wait_cnt(vector, "VE")
                ctx.inc(vector.tensor_tensor(spr[2 + pp][:], spr[2 + pp][:],
                                             accf[:, pp * 512:(pp + 1) * 512], ALU.min),
                        "VE", tag=f"c_dtm_{pp}")

            @S_("scalar")
            def _(scalar, tc=tc, pp=pp):
                ctx.wait(scalar, f"c_dtm_{pp}")
                ctx.inc(scalar.activation(spr[2 + pp][:], spr[2 + pp][:], AF.Exp), "AC")
                ctx.wait_cnt(scalar, "AC")
                ctx.inc(scalar.activation(spr[2 + pp][:], spr[2 + pp][:], AF.Ln,
                                          bias=g["cONE"]), "AC", tag=f"c_dtln_{pp}")

            @S_("vector")
            def _(vector, ct=ct, tc=tc, pp=pp):
                ctx.wait(vector, f"c_dtln_{pp}")
                ctx.wait_cnt(vector, "VE")
                n = ctx.inc(vector.tensor_tensor(dl_sl[:, tc * 512:(tc + 1) * 512],
                                                 spr[pp][:], spr[2 + pp][:], ALU.add),
                            "VE", tag=f"c_dtev_{pp}")
                ctx.tag[f"c_dtudone_{pp}"] = ("VE", n)
        ctx.tag["c_delta"] = ctx.tag[f"c_dtev_1"]

        @S_("vector")
        def _(vector, ct=ct):
            ctx.wait(vector, "c_delta")
            ctx.wait(vector, f"c_xcld_{ct}")
            ctx.wait(vector, f"c_dtmm_0")
            ctx.wait(vector, f"c_dtmm_1")
            ctx.wait_cnt(vector, "VE")
            ctx.inc(vector.tensor_tensor(X_sl, dl_sl, xc_sl, ALU.mult), "VE", tag="c_du")

        for half in range(2):
            o0 = half * H

            @S_("vector")
            def _(vector, half=half):
                ctx.wait(vector, "c_yst")
                ctx.wait_cnt(vector, "VE")
                ctx.inc(vector.memset(yacf[:], 0.0), "VE", tag="c_yz")

            for s in range(DST):
                bp = s % 2

                @S_("gpsimd")
                def _(gp, ct=ct, half=half, s=s, bp=bp, o0=o0):
                    ctx.wait(gp, "c_bc_built")
                    ctx.wait(gp, f"c_bcuse_{bp}")
                    gp.dma_start(out=bcb[bp][:, 0:H],
                                 in_=bc_dr.ap()[s * 128:(s + 1) * 128, o0:o0 + H]).then_inc(ctx.sem["SD" + str(bp)], 16)
                    ctx.bump("SD" + str(bp), 16)
                    gp.dma_start(out=bcb[bp][:, H:2 * H],
                                 in_=bc_dr.ap()[(16 + s) * 128:(17 + s) * 128, o0:o0 + H]).then_inc(ctx.sem["SD" + str(bp)], 16)
                    ctx.bump("SD" + str(bp), 16, tag=f"c_bcld_{bp}")

                @S_("vector")
                def _(vector, ct=ct, s=s, o0=o0):
                    ctx.wait(vector, "c_acol")
                    ctx.wait(vector, "c_du")
                    ctx.wait(vector, "c_yz")
                    ctx.wait(vector, "c_dAe")
                    ctx.wait_cnt(vector, "VE")
                    ctx.inc(vector.tensor_scalar(accf[:], dl_sl[:, o0:o0 + H],
                                                 acol[:, ct * 16 + s:ct * 16 + s + 1],
                                                 None, ALU.mult), "VE", tag="c_dAa")

                @S_("scalar")
                def _(scalar, s=s):
                    ctx.wait(scalar, "c_dAa")
                    ctx.wait(scalar, "c_scan")
                    ctx.inc(scalar.activation(accf[:], accf[:], AF.Exp), "AC", tag="c_dAe")

                @S_("vector")
                def _(vector, ct=ct, half=half, s=s, bp=bp, o0=o0):
                    ctx.wait(vector, f"c_bcld_{bp}")
                    ctx.wait_cnt(vector, "VE")
                    ctx.inc(vector.tensor_tensor(b_sl, X_sl[:, o0:o0 + H],
                                                 bcb[bp][:, 0:H], ALU.mult), "VE")
                    ctx.wait(vector, "c_dAe")
                    ctx.wait_cnt(vector, "VE")
                    init = 0.0 if half == 0 else carry[:, s:s + 1]
                    ctx.inc(vector.tensor_tensor_scan(h_sl, accf[:].bitcast(F32)
                                                      if False else accf[:], b_sl, init,
                                                      ALU.mult, ALU.add), "VE", tag="c_scan")
                    ctx.wait_cnt(vector, "VE")
                    ctx.inc(vector.tensor_copy(carry[:, s:s + 1], h_sl[:, H - 1:H]), "VE")
                    ctx.wait_cnt(vector, "VE")
                    ctx.inc(vector.tensor_tensor(b_sl, h_sl, bcb[bp][:, H:2 * H],
                                                 ALU.mult), "VE")
                    ctx.wait_cnt(vector, "VE")
                    n = ctx.inc(vector.tensor_tensor(yacf[:], yacf[:], b_sl, ALU.add),
                                "VE", tag="c_yadd")
                    ctx.tag[f"c_bcuse_{bp}"] = ("VE", n)

            @S_("vector")
            def _(vector, ct=ct, half=half, o0=o0):
                ctx.wait(vector, "c_yadd")
                ctx.wait_cnt(vector, "VE")
                ctx.inc(vector.scalar_tensor_tensor(b_sl, xc_sl[:, o0:o0 + H],
                                                    dcol[:, ct:ct + 1], yacf[:],
                                                    ALU.mult, ALU.add), "VE",
                        tag=f"c_yo_{ct}_{half}")

            @S_("sync")
            def _(sync, ct=ct, half=half):
                ctx.wait(sync, f"c_yo_{ct}_{half}")
                for d in range(8):
                    srcv = b_sl.rearrange("p (i dd tau) -> p dd i tau", dd=8, tau=128)[:, d]
                    dstv = a2a_yi.ap()[d * 512 + ct * 128:d * 512 + (ct + 1) * 128,
                                       half * 512:(half + 1) * 512].rearrange(
                        "p (i tau) -> p i tau", tau=128)
                    sync.dma_start(out=dstv, in_=srcv).then_inc(ctx.sem["ST"], 16)
                    ctx.bump("ST", 16)
                ctx.tag["c_yst"] = ("ST", ctx.cnt["ST"])
        ctx.tag[f"c_Xfree_{ct}"] = ctx.tag["c_scan"]
        ctx.tag[f"c_xcfree_{ct}"] = ctx.tag[f"c_yo_{ct}_1"]

    @S_("gpsimd")
    def _(gp):
        ctx.wait(gp, "c_yst")
        with allow_ici_all_to_all():
            gp.collective_compute("AllToAll", ALU.bypass, replica_groups=[list(range(8))],
                                  ins=[a2a_yi.ap().opt()], outs=[a2a_yo.ap().opt()]).then_inc(ctx.sem["CC"])
        ctx.bump("CC", 1, tag="c_a2a_y")
        gp.wait_ge(ctx.sem["CC"], ctx.cnt["CC"])

    if "dbg_yi" in g:
        @S_("gpsimd")
        def _(gp):
            ctx.wait(gp, "c_a2a_y")
            gp.dma_start(out=g["dbg_yi"].ap(), in_=a2a_yi.ap()).then_inc(ctx.sem["LDG"], 16)
            ctx.bump("LDG", 16)
            gp.dma_start(out=g["dbg_ar"].ap(), in_=ar_xo.ap()).then_inc(ctx.sem["LDG"], 16)
            ctx.bump("LDG", 16)
            gp.wait_ge(ctx.sem["LDG"], ctx.cnt["LDG"])

    for zh in range(2):
        for zc in range(16):
            ch = zh * 16 + zc

            @S_("sync")
            def _(sync, ch=ch, zc=zc, zh=zh):
                if zh == 0 and zc == 0:
                    ctx.wait(sync, "c_a2a_y")
                if zc == 0:
                    ctx.wait(sync, f"c_ymused_{zh-1}")
                ctx.wait(sync, "c_zs")
                sync.dma_start(out=zy_sl[:, 0:1024], in_=z_dram.ap()[ch * 128:(ch + 1) * 128, :]).then_inc(ctx.sem["LD"], 16)
                ctx.bump("LD", 16)
                sync.dma_start(out=zy_sl[:, 1024:2048], in_=a2a_yo.ap()[ch * 128:(ch + 1) * 128, :]).then_inc(ctx.sem["LD"], 16)
                ctx.bump("LD", 16, tag="c_zyld")

            @S_("scalar")
            def _(scalar, zc=zc):
                ctx.wait(scalar, "c_zyld")
                ctx.wait(scalar, "c_zs_used")
                ctx.inc(scalar.activation(evb[:, 0:1024], zy_sl[:, 0:1024], AF.Silu),
                        "AC", tag="c_zs")

            @S_("vector")
            def _(vector, zc=zc, zh=zh):
                ctx.wait(vector, "c_zs")
                ctx.wait_cnt(vector, "VE")
                n = ctx.inc(vector.tensor_tensor(ym[:, zc * 1024:(zc + 1) * 1024],
                                                 zy_sl[:, 1024:2048], evb[:, 0:1024],
                                                 ALU.mult), "VE", tag="c_ym")
                ctx.tag["c_zs_used"] = ("VE", n)
        ctx.tag[f"c_ym_{zh}"] = ctx.tag["c_ym"]

        for oc in range(16):
            par = "WL" + str(oc % 2)
            wbuf = wo_b[oc % 2]

            @S_("gpsimd")
            def _(gp, oc=oc, zh=zh, par=par, wbuf=wbuf):
                ctx.wait(gp, f"c_ocons_{par}")
                srcw = WopT[zh * 2048:(zh + 1) * 2048, oc * 128:(oc + 1) * 128]
                gp.dma_start(out=wbuf[:].rearrange("p (c o) -> p c o", o=128),
                             in_=srcw.rearrange("(c p) o -> p c o", p=128)).then_inc(ctx.sem[par], 16)
                ctx.bump(par, 16, tag=f"c_old_{zh}_{oc}")

            @S_("tensor")
            def _(tensor, oc=oc, zh=zh, par=par, wbuf=wbuf):
                ctx.wait(tensor, f"c_old_{zh}_{oc}")
                ctx.wait(tensor, f"c_ym_{zh}")
                ctx.wait(tensor, f"c_opev_0")
                ctx.wait(tensor, f"c_opev_1")
                for cc in range(16):
                    for tb in range(2):
                        mm = tensor.matmul(ps_b[:, tb * 512:(tb + 1) * 512],
                                           wbuf[:, cc * 128:(cc + 1) * 128],
                                           ym[:, cc * 1024 + tb * 512:cc * 1024 + (tb + 1) * 512],
                                           start=(cc == 0), stop=(cc == 15))
                n = ctx.inc(mm, "PE", tag=f"c_opmm_{oc % 2}")
                ctx.tag[f"c_ocons_{par}"] = ("PE", n)
                if oc == 15:
                    ctx.tag[f"c_ymused_{zh}"] = ("PE", n)

            @S_("vector")
            def _(vector, oc=oc, zh=zh):
                ctx.wait(vector, f"c_opmm_{oc % 2}")
                ctx.wait_cnt(vector, "VE")
                if zh == 0:
                    ctx.inc(vector.tensor_copy(opacc[:, oc * 1024:(oc + 1) * 1024],
                                               ps_b[:]), "VE", tag=f"c_opev_{oc % 2}")
                else:
                    ctx.inc(vector.tensor_tensor(opacc[:, oc * 1024:(oc + 1) * 1024],
                                                 opacc[:, oc * 1024:(oc + 1) * 1024],
                                                 ps_b[:], ALU.add), "VE",
                            tag=f"c_opev_{oc % 2}")
    ctx.tag["c_opdone"] = ("VE", ctx.cnt["VE"])

    for i in range(NT):
        @S_("tensor")
        def _(tensor, i=i):
            ctx.wait(tensor, "c_opdone")
            ctx.wait(tensor, "c_ottev")
            for fh in range(2):
                for f8 in range(8):
                    f = fh * 8 + f8
                    mm = tensor.transpose(
                        ps_t[:, fh * 1024 + f8 * 128:fh * 1024 + (f8 + 1) * 128],
                        opacc[:, f * 1024 + i * 128:f * 1024 + (i + 1) * 128],
                        ident[:])
                ctx.inc(mm, "PE", tag=f"c_ot_{fh}")

        @S_("sync")
        def _(sync, i=i):
            ctx.wait(sync, "c_hld_used")
            sync.dma_start(out=xnbf[:], in_=xn_dram.ap()[i * 128:(i + 1) * 128, :]).then_inc(ctx.sem["LD"], 16)
            ctx.bump("LD", 16, tag="c_hld")

        @S_("vector")
        def _(vector, i=i):
            ctx.wait(vector, "c_hld")
            for fh in range(2):
                ctx.wait(vector, f"c_ot_{fh}")
                ctx.wait_cnt(vector, "VE")
                ctx.inc(vector.tensor_tensor(dn_tm[:, fh * 1024:(fh + 1) * 1024],
                                             ps_t[:, fh * 1024:(fh + 1) * 1024],
                                             xnbf[:, fh * 1024:(fh + 1) * 1024],
                                             ALU.add), "VE", tag="c_ottev")
            ctx.tag["c_hld_used"] = ("VE", ctx.cnt["VE"])

        emit_post_ln_add(ctx, g, dn_tm, xt, nm, rs, t0, sqscr, xnbf, i,
                         xout_ap, tag=f"cp_{i}", dn_ready_tag="c_ottev")


def emit_phase_D(ctx, g, xin_ap, xout_ap):
    S_ = ctx.sec
    nc = ctx.nc
    xsp = g["xsp"]
    ident = g["ident_sb"]
    H128 = g["H128_sb"]
    sdram = g["sdram"]
    src = xin_ap if xin_ap is not None else xsp

    P = g["phase_stack"].enter_context
    xt = P(nc.sbuf_tensor("d_xt", [128, DM], F32))
    xnbf = P(nc.sbuf_tensor("d_xnbf", [128, DM], BF16))
    sqscr = P(nc.sbuf_tensor("d_sqscr", [128, DM], F32))
    nm = P(nc.sbuf_tensor("d_nm", [128, 1], F32))
    rs = P(nc.sbuf_tensor("d_rs", [128, 1], F32))
    t0 = P(nc.sbuf_tensor("d_t0", [128, 1], F32))
    xq = P(nc.sbuf_tensor("d_xq", [128, 16 * 512], BF16))
    hA = P(nc.sbuf_tensor("d_hA", [128, 64 * 512], BF16))
    hB = P(nc.sbuf_tensor("d_hB", [128, 64 * 512], BF16))
    wq = [P(nc.sbuf_tensor(f"d_wq{j}", [128, 16 * 128], BF16)) for j in range(4)]
    wscr = P(nc.sbuf_tensor("d_wscr", [128, 1024], F32))
    gsil = P(nc.sbuf_tensor("d_gsil", [128, 512], BF16))
    dn_tm = P(nc.sbuf_tensor("d_dntm", [128, DM], BF16))
    qsc = P(nc.sbuf_tensor("d_qsc", [128, 512], F32))
    sbc = P(nc.sbuf_tensor("d_sbc", [128, 512], BF16))
    invbc = P(nc.sbuf_tensor("d_invbc", [128, 512], F32))
    tmax = P(nc.sbuf_tensor("d_tmax", [128, 512], BF16))
    tmin = P(nc.sbuf_tensor("d_tmin", [128, 512], BF16))
    colsc = P(nc.sbuf_tensor("d_colsc", [128, 8], F32))
    colbf = P(nc.sbuf_tensor("d_colbf", [128, 8], BF16))
    rowsc = P(nc.sbuf_tensor("d_rowsc", [1, 512], F32))
    rowbf = P(nc.sbuf_tensor("d_rowbf", [1, 512], BF16))
    swcols = P(nc.sbuf_tensor("d_swcols", [128, 8], F32))
    sacc = P(nc.sbuf_tensor("d_sacc", [128, 2], F32))
    onesb = P(nc.sbuf_tensor("d_onesb", [1, 128], BF16))
    onesf = P(nc.sbuf_tensor("d_onesf", [128, 1], F32))
    ps_t = P(nc.psum_tensor("d_pst", [128, 2048], BF16))
    ps_gu = P(nc.psum_tensor("d_psgu", [128, 1024], F32))
    ps_d = P(nc.psum_tensor("d_psd", [128, 512], F32))
    ps_fw = P(nc.psum_tensor("d_psfw", [128, 1024], F32))
    ps_s = P(nc.psum_tensor("d_pss", [128, 512], F32))

    wdb = hA[:, 8192:16384]
    wdb2 = hA[:, 16384:24576]
    dn_fm = hA[:, 0:8192]
    W = {"gt": g["Wg1S"], "up": g["Wu1S"], "dn": g["Wd1S"]}

    @S_("vector")
    def _(vector):
        vector.memset(onesb[:], 1.0)
        ctx.inc(vector.memset(onesf[:], 1.0), "VE", tag="d_ones")

    for wi, wk in enumerate(("gt", "up", "dn")):
        nel = DM * DI

        @S_("vector")
        def _(vector, wi=wi):
            ctx.inc(vector.memset(sacc[:, 0:1], 0.0), "VE", tag=f"d_sz_{wi}")

        for ci in range(16):
            par = "WL" + str(ci % 2)

            @S_("gpsimd")
            def _(gp, wk=wk, ci=ci, wi=wi, par=par):
                ctx.wait(gp, f"d_swuse_{wi}_{ci-2}")
                flat = W[wk].rearrange("a b c -> (a b c)").rearrange(
                    "(n p e) -> n p e", p=128, e=8192)
                dst = hB[:, (ci % 2) * 16384:(ci % 2 + 1) * 16384].bitcast(F32)
                gp.dma_start(out=dst, in_=flat[ci]).then_inc(ctx.sem[par], 16)
                ctx.bump(par, 16, tag=f"d_swld_{wi}_{ci}")

            if ci % 3 == 0:
                @S_("scalar")
                def _(scalar, ci=ci, wi=wi):
                    ctx.wait(scalar, f"d_swld_{wi}_{ci}")
                    ctx.wait(scalar, f"d_swsum_{wi}_{ci-1}")
                    if ci == 0:
                        ctx.wait(scalar, f"d_swsum_{wi-1}_15")
                    srcf = hB[:, (ci % 2) * 16384:(ci % 2 + 1) * 16384].bitcast(F32)
                    ctx.inc(scalar.activation(srcf, srcf, AF.Abs, accum_out=sacc[:, 1:2]),
                            "AC", tag=f"d_swuse_{wi}_{ci}")
            else:
                @S_("vector")
                def _(vector, ci=ci, wi=wi):
                    ctx.wait(vector, f"d_swld_{wi}_{ci}")
                    ctx.wait(vector, f"d_sz_{wi}")
                    ctx.wait(vector, f"d_swsum_{wi}_{ci-1}")
                    ctx.wait_cnt(vector, "VE")
                    ctx.inc(vector.tensor_reduce(sacc[:, 1:2],
                                                 hB[:, (ci % 2) * 16384:(ci % 2 + 1) * 16384].bitcast(F32),
                                                 AX.X, ALU.add, apply_absolute_value=True),
                            "VE", tag=f"d_swuse_{wi}_{ci}")

            @S_("vector")
            def _(vector, ci=ci, wi=wi):
                ctx.wait(vector, f"d_swuse_{wi}_{ci}")
                ctx.wait_cnt(vector, "VE")
                ctx.inc(vector.tensor_tensor(sacc[:, 0:1], sacc[:, 0:1], sacc[:, 1:2],
                                             ALU.add), "VE", tag=f"d_swsum_{wi}_{ci}")

        @S_("tensor")
        def _(tensor, wi=wi):
            ctx.wait(tensor, f"d_swsum_{wi}_15")
            ctx.wait(tensor, f"d_swbc_done_{wi-1}")
            ctx.wait(tensor, "d_ones")
            ctx.inc(tensor.matmul(ps_s[0:1, 0:1], sacc[:, 0:1], onesf[:],
                                  start=True, stop=True), "PE", tag=f"d_swred_{wi}")

        @S_("vector")
        def _(vector, wi=wi, nel=nel):
            ctx.wait(vector, f"d_swred_{wi}")
            ctx.inc(vector.tensor_scalar(rowsc[0:1, 0:1], ps_s[0:1, 0:1], 1.0 / nel, EPS,
                                         ALU.mult, ALU.max), "VE")
            ctx.wait_cnt(vector, "VE")
            ctx.inc(vector.reciprocal(rowsc[0:1, 1:2], rowsc[0:1, 0:1]), "VE",
                    tag=f"d_swval_{wi}")

        @S_("sync")
        def _(sync, wi=wi):
            ctx.wait(sync, f"d_swval_{wi}")
            sync.dma_start(out=sdram[0:1, 2 * wi:2 * wi + 2], in_=rowsc[0:1, 0:2]).then_inc(ctx.sem["ST"], 16)
            ctx.bump("ST", 16)
            sync.wait_ge(ctx.sem["ST"], ctx.cnt["ST"])
            bsrc = sdram[0:1, 2 * wi:2 * wi + 2].broadcast_to([128, 2])
            sync.dma_start(out=swcols[:, 2 * wi:2 * wi + 2], in_=bsrc).then_inc(ctx.sem["LD"], 16)
            ctx.bump("LD", 16, tag=f"d_swbc_done_{wi}")

    @S_("vector")
    def _(vector):
        for wi in range(3):
            ctx.wait(vector, f"d_swbc_done_{wi}")

    def swcol(wi):
        return swcols[:, 2 * wi:2 * wi + 1]

    def swinv(wi):
        return swcols[:, 2 * wi + 1:2 * wi + 2]

    WC = {"gt": g["wg1c"], "up": g["wu1c"], "dn": g["wd1c"]}
    for wi, wk in enumerate(("gt", "up", "dn")):
        for ci in range(16):
            par = "WL" + str(ci % 2)

            gi = wi * 16 + ci

            @S_("gpsimd")
            def _(gp, wk=wk, ci=ci, wi=wi, par=par, gi=gi):
                if gi >= 2:
                    ctx.wait(gp, f"d_tpuse_{gi-2}")
                else:
                    ctx.wait(gp, "d_swbc_done_2")
                flat = W[wk].rearrange("a b c -> (a b c)").rearrange(
                    "(n p e) -> n p e", p=128, e=8192)
                dst = hB[:, (ci % 2) * 16384:(ci % 2 + 1) * 16384].bitcast(F32)
                gp.dma_start(out=dst, in_=flat[ci]).then_inc(ctx.sem[par], 16)
                ctx.bump(par, 16, tag=f"d_tpld_{wi}_{ci}")

            @S_("vector")
            def _(e, ci=ci, wi=wi, tkey="VE"):
                ctx.wait(e, f"d_tpld_{wi}_{ci}")
                srcf = hB[:, (ci % 2) * 16384:(ci % 2 + 1) * 16384].bitcast(F32)
                dstb = hA[:, (ci % 2) * 8192:(ci % 2 + 1) * 8192]
                ctx.wait_cnt(e, tkey)
                ctx.inc(e.tensor_scalar(srcf, srcf, swinv(wi), RND,
                                        ALU.mult, ALU.add), tkey)
                ctx.wait_cnt(e, tkey)
                ctx.inc(e.tensor_scalar(srcf, srcf, RND, -1.0,
                                        ALU.subtract, ALU.max), tkey)
                ctx.wait_cnt(e, tkey)
                ctx.inc(e.tensor_scalar(dstb, srcf, 1.0, None, ALU.min), tkey,
                        tag=f"d_tpt_{wi}_{ci}")

            @S_("sync")
            def _(sync, wk=wk, ci=ci, wi=wi, gi=gi):
                ctx.wait(sync, f"d_tpt_{wi}_{ci}")
                cflat = WC[wk].ap().rearrange("a b c -> (a b c)").rearrange(
                    "(n p e) -> n p e", p=128, e=8192)
                sync.dma_start(out=cflat[ci],
                               in_=hA[:, (ci % 2) * 8192:(ci % 2 + 1) * 8192]).then_inc(ctx.sem["ST"], 16)
                n = ctx.bump("ST", 16, tag=f"d_tpst_{wi}_{ci}")
                ctx.tag[f"d_tpuse_{gi}"] = ("ST", n)
        ctx.tag[f"d_tern_done_{wi}"] = ("ST", ctx.cnt["ST"])

    def ternarize(srcf_ap, dst_ap, ncols, wi, tg):
        np_ = (ncols + 1023) // 1024
        @S_("vector")
        def _(vector):
            for pp in range(np_):
                c0, c1 = pp * 1024, min(ncols, (pp + 1) * 1024)
                ctx.wait_cnt(vector, "VE")
                ctx.inc(vector.tensor_scalar(wscr[:, 0:c1 - c0], srcf_ap[:, c0:c1],
                                             swinv(wi), RND, ALU.mult, ALU.add), "VE")
                ctx.wait_cnt(vector, "VE")
                ctx.inc(vector.tensor_scalar(wscr[:, 0:c1 - c0], wscr[:, 0:c1 - c0],
                                             RND, -1.0, ALU.subtract, ALU.max), "VE")
                ctx.wait_cnt(vector, "VE")
                ctx.inc(vector.tensor_scalar(dst_ap[:, c0:c1], wscr[:, 0:c1 - c0],
                                             1.0, None, ALU.min), "VE", tag=tg)

    def quantize(Xbuf, nch, lam, tg):
        @S_("vector")
        def _(vector):
            ctx.wait_cnt(vector, "VE")
            vector.tensor_copy(tmax[:], Xbuf[:, 0:512])
            ctx.inc(vector.tensor_copy(tmin[:], Xbuf[:, 0:512]), "VE")
            for ch in range(1, nch):
                ctx.wait_cnt(vector, "VE")
                vector.tensor_tensor(tmax[:], tmax[:], Xbuf[:, ch * 512:(ch + 1) * 512],
                                     ALU.max)
                ctx.inc(vector.tensor_tensor(tmin[:], tmin[:], Xbuf[:, ch * 512:(ch + 1) * 512],
                                             ALU.min), "VE")
            ctx.tag[tg + "_tmax"] = ("VE", ctx.cnt["VE"])

        @S_("tensor")
        def _(tensor):
            ctx.wait(tensor, tg + "_tmax")
            ctx.wait(tensor, tg + "_mxev_prev")
            for st in range(4):
                tensor.transpose(ps_t[:, st * 256:st * 256 + 128],
                                 tmax[:, st * 128:(st + 1) * 128], ident[:])
                ctx.inc(tensor.transpose(ps_t[:, st * 256 + 128:st * 256 + 256],
                                         tmin[:, st * 128:(st + 1) * 128], ident[:]),
                        "PE", tag=tg + f"_mxt_{st}")

        @S_("vector")
        def _(vector):
            ctx.wait(vector, tg + "_mxt_3")
            for st in range(4):
                ctx.inc(vector.tensor_reduce(colsc[:, st:st + 1],
                                             ps_t[:, st * 256:(st + 1) * 256],
                                             AX.X, ALU.max, apply_absolute_value=True),
                        "VE")
            ctx.tag[tg + "_mxev_prev"] = ("VE", ctx.cnt["VE"])
            for st in range(4):
                ctx.wait_cnt(vector, "VE")
                ctx.inc(vector.tensor_scalar(colbf[:, st:st + 1], colsc[:, st:st + 1],
                                             EPS / lam, None, ALU.max), "VE",
                        tag=tg + f"_mxbf_{st}")

        @S_("tensor")
        def _(tensor):
            ctx.wait(tensor, tg + "_mxbf_3")
            ctx.wait(tensor, tg + "_rowev_prev")
            for st in range(4):
                ctx.inc(tensor.matmul(ps_s[0:1, st * 128:(st + 1) * 128],
                                      colbf[:, st:st + 1],
                                      ident[:], start=True, stop=True), "PE",
                        tag=tg + f"_mxrow_{st}")

        @S_("vector")
        def _(vector):
            ctx.wait(vector, tg + "_mxrow_3")
            ctx.inc(vector.tensor_copy(rowbf[0:1, 0:512], ps_s[0:1, 0:512]), "VE",
                    tag=tg + "_rowev_prev")

        @S_("tensor")
        def _(tensor):
            ctx.wait(tensor, tg + "_rowev_prev")
            ctx.wait(tensor, tg + "_bcev_prev")
            ctx.inc(tensor.matmul(ps_s[:, 0:512], onesb[:], rowbf[0:1, 0:512],
                                  start=True, stop=True), "PE", tag=tg + "_bc")

        @S_("vector")
        def _(vector, lam=lam):
            ctx.wait(vector, tg + "_bc")
            ctx.inc(vector.tensor_copy(qsc[:], ps_s[:, 0:512]), "VE",
                    tag=tg + "_bcev_prev")
            ctx.wait_cnt(vector, "VE")
            ctx.inc(vector.reciprocal(invbc[:], qsc[:]), "VE")
            ctx.wait_cnt(vector, "VE")
            ctx.inc(vector.tensor_scalar(sbc[:], invbc[:], 127.0, None, ALU.mult), "VE")
            ctx.wait_cnt(vector, "VE")
            ctx.inc(vector.tensor_scalar(invbc[:], qsc[:], lam / 127.0, None, ALU.mult),
                    "VE", tag=tg + "_sc")

        @S_("vector")
        def _(vector):
            ctx.wait(vector, tg + "_sc")
            for ch in range(nch):
                xap = Xbuf[:, ch * 512:(ch + 1) * 512]
                ctx.wait_cnt(vector, "VE")
                ctx.inc(vector.tensor_tensor(qsc[:], xap, sbc[:], ALU.mult), "VE")
                ctx.wait_cnt(vector, "VE")
                ctx.inc(vector.tensor_scalar(qsc[:], qsc[:], RND, RND, ALU.add,
                                             ALU.subtract), "VE")
                ctx.wait_cnt(vector, "VE")
                ctx.inc(vector.tensor_tensor(xap, qsc[:], invbc[:], ALU.mult), "VE")
            ctx.tag[tg + "_done"] = ("VE", ctx.cnt["VE"])

    def fwht(Xsrc, Xdst, nch, tg):
        nst = int(math.log2(nch))
        ctx.tag[tg + "_fwev_-2"] = ctx.tag.get("d_psfw_last", ("VE", 0))
        ctx.tag[tg + "_fwev_-1"] = ctx.tag.get("d_psfw_last", ("VE", 0))
        for ch in range(nch):
            bank = ch % 2

            @S_("tensor")
            def _(tensor, ch=ch, bank=bank):
                if ch == 0:
                    ctx.wait(tensor, tg + "_src_ready")
                ctx.wait(tensor, tg + f"_fwev_{ch-2}")
                ctx.inc(tensor.matmul(ps_fw[:, bank * 512:(bank + 1) * 512], H128[:],
                                      Xsrc[:, ch * 512:(ch + 1) * 512],
                                      start=True, stop=True), "PE", tag=tg + f"_fwmm_{ch}")

            @S_("vector")
            def _(vector, ch=ch, bank=bank):
                ctx.wait(vector, tg + f"_fwmm_{ch}")
                ctx.inc(vector.tensor_copy(Xdst[:, ch * 512:(ch + 1) * 512],
                                           ps_fw[:, bank * 512:(bank + 1) * 512]), "VE",
                        tag=tg + f"_fwev_{ch}")
        ctx.tag[tg + "_stA"] = ctx.tag[tg + f"_fwev_{nch-1}"]
        ctx.tag["d_psfw_last"] = ctx.tag[tg + "_stA"]

        bufs = [Xdst, Xsrc]
        for st in range(nst):
            d = 1 << st
            src_b = bufs[st % 2]
            dst_b = bufs[(st + 1) % 2]
            gcount = nch // (2 * d)

            @S_("vector")
            def _(vector, d=d, src_b=src_b, dst_b=dst_b, gcount=gcount, st=st):
                ctx.wait(vector, tg + ("_stA" if st == 0 else f"_st{st-1}"))
                sv = src_b[:].rearrange("p (g two dd e) -> p g two dd e", two=2, dd=d,
                                        e=512)[:, 0:gcount]
                dv = dst_b[:].rearrange("p (g two dd e) -> p g two dd e", two=2, dd=d,
                                        e=512)[:, 0:gcount]
                ctx.inc(vector.tensor_tensor(dv[:, :, 0], sv[:, :, 0], sv[:, :, 1],
                                             ALU.add), "VE")
                ctx.inc(vector.tensor_tensor(dv[:, :, 1], sv[:, :, 0], sv[:, :, 1],
                                             ALU.subtract), "VE", tag=tg + f"_st{st}")

        assert nst % 2 == 0, "result must land back in Xdst"
        ctx.tag[tg + "_done"] = ctx.tag[tg + f"_st{nst-1}"]

    for hf in range(2):
        base = hf * 4

        for tt in range(4):
            i = base + tt

            @S_("sync")
            def _(sync, i=i):
                ctx.wait_cnt(sync, "ST")
                ctx.wait(sync, f"d_xn_{i-1}")
                sync.dma_start(out=xt[:], in_=src[i * 128:(i + 1) * 128, :]).then_inc(ctx.sem["LD"], 16)
                ctx.bump("LD", 16, tag=f"d_ld_{i}")
                if xin_ap is not None:
                    sync.wait_ge(ctx.sem["LD"], ctx.cnt["LD"])
                    sync.dma_start(out=xsp[i * 128:(i + 1) * 128, :], in_=xt[:]).then_inc(ctx.sem["ST"], 16)
                    ctx.bump("ST", 16, tag=f"d_sp_{i}")

            @S_("vector")
            def _(vector, i=i):
                ctx.wait(vector, f"d_ld_{i}")

            @S_("scalar")
            def _(scalar, i=i):
                ctx.wait(scalar, f"d_ld_{i}")

            _ln_sections(ctx, g, xt[:], nm, rs, t0, sqscr, DM, f"d_ln_{i}")

            @S_("vector")
            def _(vector, i=i):
                ctx.wait(vector, f"d_ln_{i}_rs")
                ctx.wait(vector, f"d_tr_{i-1}")
                ctx.inc(vector.tensor_scalar(xnbf[:], xt[:], nm[:], rs[:], ALU.add,
                                             ALU.mult), "VE", tag=f"d_xn_{i}")

            @S_("tensor")
            def _(tensor, i=i, tt=tt, hf=hf):
                ctx.wait(tensor, f"d_xn_{i}")
                ctx.wait(tensor, f"d_trev_{i-1}")
                if tt == 0:
                    ctx.wait(tensor, f"d_xqfree_{hf}")
                for fh in range(2):
                    for f8 in range(8):
                        f = fh * 8 + f8
                        mm = tensor.transpose(
                            ps_t[:, fh * 1024 + f8 * 128:fh * 1024 + (f8 + 1) * 128],
                            xnbf[:, f * 128:(f + 1) * 128], ident[:])
                    ctx.inc(mm, "PE", tag=f"d_trp_{i}_{fh}")
                ctx.tag[f"d_tr_{i}"] = ctx.tag[f"d_trp_{i}_1"]

            @S_("vector")
            def _(vector, i=i, tt=tt):
                for fh in range(2):
                    ctx.wait(vector, f"d_trp_{i}_{fh}")
                    dst = xq[:].rearrange("p (f t) -> p f t", t=512)[:, fh * 8:(fh + 1) * 8,
                                                                    tt * 128:(tt + 1) * 128]
                    srcv = ps_t[:, fh * 1024:(fh + 1) * 1024].rearrange("p (f t) -> p f t",
                                                                       t=128)
                    ctx.inc(vector.tensor_copy(dst, srcv), "VE", tag=f"d_trev_{i}")
        ctx.tag[f"d_xq_ready_{hf}"] = ctx.tag[f"d_trev_{base+3}"]

        x2A = hB[:, 0:8192]
        ctx.tag[f"dfw2_{hf}_src_ready"] = ctx.tag[f"d_xq_ready_{hf}"]
        fwht(xq, x2A, 16, f"dfw2_{hf}")

        ctx.tag[f"dq2_{hf}_mxev_prev"] = ctx.tag.get("d_mxev_last", ("VE", 0))
        ctx.tag[f"dq2_{hf}_rowev_prev"] = ctx.tag.get("d_rowev_last", ("VE", 0))
        ctx.tag[f"dq2_{hf}_bcev_prev"] = ctx.tag.get("d_bcev_last", ("VE", 0))
        quantize(x2A, 16, LAM11, f"dq2_{hf}")
        ctx.tag["d_mxev_last"] = ctx.tag[f"dq2_{hf}_mxev_prev"]
        ctx.tag["d_rowev_last"] = ctx.tag[f"dq2_{hf}_rowev_prev"]
        ctx.tag["d_bcev_last"] = ctx.tag[f"dq2_{hf}_bcev_prev"]

        @S_("vector")
        def _(vector, hf=hf, x2A=x2A):
            ctx.wait(vector, f"dq2_{hf}_done")
            ctx.wait(vector, f"d_xq_ready_{hf}")
            ctx.inc(vector.tensor_copy(xq[:], x2A), "VE", tag=f"d_xqs_{hf}")

        for oc in range(64):
            for gu in range(2):
                wbuf = wq[gu * 2 + oc % 2]
                par = "WL" + str(gu)
                wc = WC["gt"] if gu == 0 else WC["up"]
                qeng = "sync" if gu == 0 else "gpsimd"

                @S_(qeng)
                def _(q, wc=wc, wbuf=wbuf, oc=oc, gu=gu, hf=hf, par=par):
                    ctx.wait(q, f"d_wqw_{gu}_{oc % 2}")
                    if hf == 0 and oc == 0:
                        ctx.wait(q, f"d_tern_done_{gu}")
                    q.dma_start(out=wbuf[:],
                                in_=wc.ap()[oc]).then_inc(ctx.sem[par], 16)
                    ctx.bump(par, 16, tag=f"d_tern_{hf}_{oc}_{gu}")

                @S_("tensor")
                def _(tensor, wbuf=wbuf, oc=oc, gu=gu, hf=hf, par=par):
                    ctx.wait(tensor, f"d_tern_{hf}_{oc}_{gu}")
                    if oc == 0 and gu == 0:
                        ctx.wait(tensor, f"d_xqs_{hf}")
                    ctx.wait(tensor, "d_silu_used" if gu == 0 else "d_h_written")
                    for fc in range(16):
                        mm = tensor.matmul(ps_gu[:, gu * 512:gu * 512 + 512],
                                           wbuf[:, fc * 128:(fc + 1) * 128],
                                           xq[:, fc * 512:(fc + 1) * 512],
                                           start=(fc == 0), stop=(fc == 15))
                    n = ctx.inc(mm, "PE", tag=f"d_gu_{gu}")
                    ctx.tag[f"d_wqw_{gu}_{oc % 2}"] = ("PE", n)

                if gu == 0:
                    @S_("scalar")
                    def _(scalar):
                        ctx.wait(scalar, "d_gu_0")
                        ctx.wait(scalar, "d_silu_read")
                        ctx.inc(scalar.activation(gsil[:], ps_gu[:, 0:512], AF.Silu,
                                                  scale=swcol(0)), "AC",
                                tag="d_silu_used")
                else:
                    @S_("vector")
                    def _(vector, oc=oc, hf=hf):
                        ctx.wait(vector, "d_gu_1")
                        ctx.wait(vector, "d_silu_used")
                        ctx.wait(vector, f"d_hfree_{hf}")
                        if hf == 0 and oc == 0:
                            ctx.wait(vector, "d_tern_done_2")
                        n = ctx.inc(vector.scalar_tensor_tensor(
                            hA[:, oc * 512:(oc + 1) * 512], ps_gu[:, 512:1024],
                            swcol(1), gsil[:], ALU.mult, ALU.mult), "VE",
                            tag="d_h_written")
                        ctx.tag["d_silu_read"] = ("VE", n)
        ctx.tag[f"d_xqfree_{hf+1}"] = ctx.tag["d_gu_1"]
        ctx.tag[f"d_h_ready_{hf}"] = ctx.tag["d_h_written"]

        ctx.tag[f"dfw8_{hf}_src_ready"] = ctx.tag[f"d_h_ready_{hf}"]
        fwht(hA, hB, 64, f"dfw8_{hf}")
        ctx.tag[f"dq8_{hf}_mxev_prev"] = ctx.tag["d_mxev_last"]
        ctx.tag[f"dq8_{hf}_rowev_prev"] = ctx.tag["d_rowev_last"]
        ctx.tag[f"dq8_{hf}_bcev_prev"] = ctx.tag["d_bcev_last"]
        quantize(hB, 64, LAM12, f"dq8_{hf}")
        ctx.tag["d_mxev_last"] = ctx.tag[f"dq8_{hf}_mxev_prev"]
        ctx.tag["d_rowev_last"] = ctx.tag[f"dq8_{hf}_rowev_prev"]
        ctx.tag["d_bcev_last"] = ctx.tag[f"dq8_{hf}_bcev_prev"]

        ctx.tag["d_dnw_0"] = ctx.tag[f"dfw8_{hf}_st5"]
        ctx.tag["d_dnw_1"] = ctx.tag[f"dfw8_{hf}_st5"]
        for o2 in range(16):
            wdbuf = wdb if o2 % 2 == 0 else wdb2
            par = "WL" + str(o2 % 2)
            qeng = "sync" if o2 % 2 == 0 else "gpsimd"

            @S_(qeng)
            def _(q, o2=o2, hf=hf, wdbuf=wdbuf, par=par):
                ctx.wait(q, f"d_dnw_{o2 % 2}")
                if hf == 0 and o2 == 0:
                    ctx.wait(q, "d_tern_done_2")
                q.dma_start(out=wdbuf[:, 0:8192],
                            in_=WC["dn"].ap()[o2]).then_inc(ctx.sem[par], 16)
                ctx.bump(par, 16, tag=f"d_dtern_{hf}_{o2}")

            @S_("tensor")
            def _(tensor, o2=o2, hf=hf, wdbuf=wdbuf):
                ctx.wait(tensor, f"d_dtern_{hf}_{o2}")
                if o2 == 0:
                    ctx.wait(tensor, f"dq8_{hf}_done")
                ctx.wait(tensor, "d_dnev")
                for ic in range(64):
                    mm = tensor.matmul(ps_d[:], wdbuf[:, ic * 128:(ic + 1) * 128],
                                       hB[:, ic * 512:(ic + 1) * 512],
                                       start=(ic == 0), stop=(ic == 63))
                n = ctx.inc(mm, "PE", tag="d_dnmm")
                ctx.tag[f"d_dnw_{o2 % 2}"] = ("PE", n)

            @S_("vector")
            def _(vector, o2=o2):
                ctx.wait(vector, "d_dnmm")
                ctx.wait(vector, "d_dnfm_free")
                ctx.inc(vector.tensor_scalar(dn_fm[:, o2 * 512:(o2 + 1) * 512], ps_d[:],
                                             swcol(2), None, ALU.mult), "VE",
                        tag="d_dnev")


        for tt in range(4):
            i = base + tt

            @S_("tensor")
            def _(tensor, tt=tt, hf=hf):
                ctx.wait(tensor, "d_dnev")
                ctx.wait(tensor, "d_dtev")
                for fh in range(2):
                    for f8 in range(8):
                        f = fh * 8 + f8
                        mm = tensor.transpose(
                            ps_t[:, fh * 1024 + f8 * 128:fh * 1024 + (f8 + 1) * 128],
                            dn_fm[:, f * 512 + tt * 128:f * 512 + (tt + 1) * 128],
                            ident[:])
                    ctx.inc(mm, "PE", tag=f"d_dtp_{fh}")
                ctx.tag["d_dnfm_free"] = ctx.tag["d_dtp_1"]

            @S_("vector")
            def _(vector, tt=tt):
                for fh in range(2):
                    ctx.wait(vector, f"d_dtp_{fh}")
                    ctx.inc(vector.tensor_copy(dn_tm[:, fh * 1024:(fh + 1) * 1024],
                                               ps_t[:, fh * 1024:(fh + 1) * 1024]),
                            "VE", tag="d_dtev")

            emit_post_ln_add(ctx, g, dn_tm, xt, nm, rs, t0, sqscr, xnbf, i,
                             xout_ap, tag=f"dp_{i}", dn_ready_tag="d_dtev")
        ctx.tag[f"d_hfree_{hf+1}"] = ctx.tag["d_dtp_1"]



def build_program(phases=PHASES):
    nc = bass.Bass("TRN2", target_bir_lowering=False)

    ap = {}

    def inp(name, shape, dt=F32):
        ap[name] = nc.dram_tensor(name, list(shape), dt, kind="ExternalInput").ap()

    inp("xin", (TPC, DM))
    inp("ident", (128, 128), BF16)
    if "A" in phases:
        inp("WqT", (DM, DM)); inp("WkT", (DM, 512)); inp("WvT", (DM, 512)); inp("WoT", (DM, DM))
        inp("SD", (72 * 128, 512), BF16)
    if "B" in phases:
        inp("WguS", (128, 128, 2048)); inp("Wd0S", (16, 128, 8192))
    if "C" in phases:
        CS = DIN // NCORE
        inp("WinT", (DM, EM)); inp("convw", (CS, 4)); inp("convb", (CS, 1))
        inp("WxpT", (CS, 160)); inp("WdtT", (DTR, CS)); inp("dtb", (CS, 1))
        inp("WopT", (DIN, DM)); inp("Alog", (CS, DST)); inp("Dvec", (CS, 1))
        inp("nsc", (DM, 1))
    if "D" in phases:
        inp("Wg1S", (64, 128, 2048)); inp("Wu1S", (64, 128, 2048))
        inp("Wd1S", (16, 128, 8192))
        inp("H128", (128, 128), BF16)

    xout = nc.dram_tensor("xout", [TPC, DM], F32, kind="ExternalOutput").ap()
    g = dict(ap)
    g["xsp"] = nc.dram_tensor("xsp", [TPC, DM], F32).ap()
    if "A" in phases:
        g["kag_in"] = nc.dram_tensor("kag_in", [512, TPC], BF16)
        g["kag_out"] = nc.dram_tensor("kag_out", [8 * 512, TPC], BF16)
        g["vag_in"] = nc.dram_tensor("vag_in", [TPC, 512], BF16)
        g["vag_out"] = nc.dram_tensor("vag_out", [8 * TPC, 512], BF16)
    if "C" in phases:
        kindd = "ExternalOutput" if DEBUG_C else "Internal"
        g["a2a_xi"] = nc.dram_tensor("a2a_xi", [DIN, TPC], BF16)
        g["a2a_xo"] = nc.dram_tensor("a2a_xo", [DIN, TPC], BF16)
        g["ar_xi"] = nc.dram_tensor("ar_xi", [160, S], F32)
        g["ar_xo"] = nc.dram_tensor("ar_xo", [160, S], F32)
        g["a2a_yi"] = nc.dram_tensor("a2a_yi", [DIN, TPC], BF16)
        g["a2a_yo"] = nc.dram_tensor("a2a_yo", [DIN, TPC], BF16)
        g["z_dram"] = nc.dram_tensor("z_dram", [DIN, TPC], BF16, kind=kindd)
        g["xn_dram"] = nc.dram_tensor("xn_dram", [TPC, DM], BF16)
        g["xc_dr"] = nc.dram_tensor("xc_dr", [512, S], BF16, kind=kindd)
        g["bc_dr"] = nc.dram_tensor("bc_dr", [32 * 128, S], BF16)
        if DEBUG_C:
            g["dbg_yi"] = nc.dram_tensor("dbg_yi", [DIN, TPC], BF16, kind="ExternalOutput")
            g["dbg_ar"] = nc.dram_tensor("dbg_ar", [160, S], F32, kind="ExternalOutput")

    with contextlib.ExitStack() as stack:
        E = stack.enter_context
        sem = {k: E(nc.semaphore(k)) for k in
               ("LD", "LDG", "PE", "VE", "AC", "PO", "ST", "CC", "WL0", "WL1",
                "KL0", "KL1", "VL0", "VL1", "SD0", "SD1")}
        block = E(nc.Block())
        ctx = Ctx(nc, block, sem)
        ident_sb = E(nc.sbuf_tensor("ident_sb", [128, 128], BF16))
        g["ident_sb"] = ident_sb
        kconst = E(nc.sbuf_tensor("kconst", [128, 8], F32))
        g["cEPS"] = kconst[:, 0:1]
        g["cONE"] = kconst[:, 1:2]

        @block.sync
        def _(sync):
            sync.dma_start(out=ident_sb[:], in_=ap["ident"]).then_inc(sem["LD"], 16)
            ctx.bump("LD", 16)

        @block.vector
        def _(vector):
            vector.memset(kconst[:, 0:1], EPS)
            ctx.inc(vector.memset(kconst[:, 1:2], 1.0), "VE", tag="konst")

        if "D" in phases:
            H128_sb = E(nc.sbuf_tensor("H128_sb", [128, 128], BF16))
            g["H128_sb"] = H128_sb
            g["sdram"] = nc.dram_tensor("sdram", [1, 16], F32).ap()
            g["wg1c"] = nc.dram_tensor("wg1c", [64, 128, 2048], BF16)
            g["wu1c"] = nc.dram_tensor("wu1c", [64, 128, 2048], BF16)
            g["wd1c"] = nc.dram_tensor("wd1c", [16, 128, 8192], BF16)

            @block.sync
            def _(sync):
                sync.dma_start(out=H128_sb[:], in_=ap["H128"]).then_inc(sem["LD"], 16)
                ctx.bump("LD", 16)

        emitters = {"A": emit_phase_A, "B": emit_phase_B, "C": emit_phase_C,
                    "D": emit_phase_D}
        for pi, ph in enumerate(phases):
            with contextlib.ExitStack() as phase_stack:
                g["phase_stack"] = phase_stack
                xin_ap = ap["xin"] if pi == 0 else None
                xout_ap = xout if pi == len(phases) - 1 else None
                if pi > 0:
                    for eng in ("sync", "vector", "scalar", "tensor", "gpsimd"):
                        @getattr(block, eng)
                        def _(e):
                            for k in ("LD", "LDG", "PE", "VE", "AC", "ST"):
                                e.wait_ge(sem[k], ctx.cnt[k])
                emitters[ph](ctx, g, xin_ap, xout_ap)

        @block.sync
        def _(sync):
            sync.wait_ge(sem["ST"], ctx.cnt["ST"])

    return nc



def _hadamard(n):
    H = np.array([[1.0]])
    while H.shape[0] < n:
        H = np.block([[H, H], [H, -H]])
    return H


def _sd_table(c):
    out = np.zeros((72, 128, 512), np.float32)
    idx = 0
    for i in range(NT):
        q0 = (8 * i + c) * 128
        q = q0 + np.arange(128)[:, None]
        for kc in range(2 * (i + 1)):
            k = kc * 512 + np.arange(512)[None, :]
            d = q - k
            sd = np.sqrt(np.maximum(d, 0).astype(np.float64)).astype(np.float32)
            sd = np.where(d < 0, np.float32(1e30), sd)
            out[idx] = sd
            idx += 1
    return out.reshape(72 * 128, 512).astype(ml_dtypes.bfloat16)


def host_inputs(inputs, phases):
    x = np.ascontiguousarray(np.asarray(inputs["L0_attn_x"], dtype=np.float32).reshape(S, DM))
    xt = x.reshape(64, 128, DM)
    base = {"ident": np.eye(128).astype(ml_dtypes.bfloat16)}
    if "A" in phases:
        base["WqT"] = np.ascontiguousarray(np.asarray(inputs["L0_attn_Wq"]).T)
        base["WkT"] = np.ascontiguousarray(np.asarray(inputs["L0_attn_Wk"]).T)
        base["WvT"] = np.ascontiguousarray(np.asarray(inputs["L0_attn_Wv"]).T)
        base["WoT"] = np.ascontiguousarray(np.asarray(inputs["L0_attn_Wo"]).T)
    if "B" in phases:
        wgu = np.asarray(inputs["L0_ffn_w_gate_up"]).T.astype(np.float32)
        base["WguS"] = np.ascontiguousarray(
            wgu.reshape(16, 128, 2, 64, 128).transpose(2, 3, 1, 0, 4).reshape(128, 128, 2048))
        wd0 = np.asarray(inputs["L0_ffn_w_down"]).T.astype(np.float32)
        base["Wd0S"] = np.ascontiguousarray(
            wd0.reshape(64, 128, 16, 128).transpose(2, 1, 0, 3).reshape(16, 128, 8192))
    if "C" in phases:
        base["WinT"] = np.ascontiguousarray(np.asarray(inputs["L1_attn_in_proj_w"]).T)
        base["WopT"] = np.ascontiguousarray(np.asarray(inputs["L1_attn_out_proj_w"]).T)
        base["nsc"] = np.ascontiguousarray(np.asarray(inputs["L1_attn_norm_scale"]).reshape(DM, 1))
    if "D" in phases:
        def _shufA(w):
            return np.ascontiguousarray(np.asarray(w).T.astype(np.float32).reshape(
                16, 128, 64, 128).transpose(2, 1, 0, 3).reshape(64, 128, 2048))

        base["Wg1S"] = _shufA(inputs["L1_ffn_w_gate"])
        base["Wu1S"] = _shufA(inputs["L1_ffn_w_up"])
        wd1 = np.asarray(inputs["L1_ffn_w_down"]).T.astype(np.float32)
        base["Wd1S"] = np.ascontiguousarray(
            wd1.reshape(64, 128, 16, 128).transpose(2, 1, 0, 3).reshape(16, 128, 8192))
        base["H128"] = _hadamard(128).astype(ml_dtypes.bfloat16)
    per = []
    for c in range(NCORE):
        m = dict(base)
        m["xin"] = np.ascontiguousarray(xt[c::8].reshape(TPC, DM))
        if "A" in phases:
            m["SD"] = _sd_table(c)
        if "C" in phases:
            CS = DIN // NCORE
            sl = slice(c * CS, (c + 1) * CS)
            m["convw"] = np.ascontiguousarray(np.asarray(inputs["L1_attn_conv_w"]).reshape(DIN, 4)[sl])
            m["convb"] = np.ascontiguousarray(np.asarray(inputs["L1_attn_conv_b"]).reshape(DIN, 1)[sl])
            m["WxpT"] = np.ascontiguousarray(np.asarray(inputs["L1_attn_x_proj_w"]).T[sl])
            m["WdtT"] = np.ascontiguousarray(np.asarray(inputs["L1_attn_dt_proj_w"]).T[:, sl])
            m["dtb"] = np.ascontiguousarray(np.asarray(inputs["L1_attn_dt_proj_b"]).reshape(DIN, 1)[sl])
            m["Alog"] = np.ascontiguousarray(np.asarray(inputs["L1_attn_A_log"])[sl])
            m["Dvec"] = np.ascontiguousarray(np.asarray(inputs["L1_attn_D"]).reshape(DIN, 1)[sl])
        per.append(m)
    return per


def assemble(results, key="xout"):
    full = np.zeros((64, 128, DM), np.float32)
    for c in range(NCORE):
        full[c::8] = np.asarray(results[c][key], dtype=np.float32).reshape(NT, 128, DM)
    return full.reshape(1, S, DM)


def kernel(**inputs):
    if FUSED:
        nc = build_program(PHASES)
        in_maps = host_inputs(inputs, PHASES)
        res = run_bass_kernel_spmd(nc, in_maps, core_ids=list(range(NCORE)))
        return _cast_out(assemble(res.results))
    x = None
    for ph in PHASES:
        nc = build_program((ph,))
        in_maps = host_inputs(inputs, (ph,))
        if x is not None:
            xt = x.reshape(64, 128, DM)
            for c in range(NCORE):
                in_maps[c]["xin"] = np.ascontiguousarray(xt[c::8].reshape(TPC, DM))
        res = run_bass_kernel_spmd(nc, in_maps, core_ids=list(range(NCORE)))
        x = assemble(res.results).reshape(S, DM)
    return _cast_out(x.reshape(1, S, DM))


def _cast_out(x):
    return np.ascontiguousarray(x, dtype=np.float32)
